# version 12
# baseline (speedup 1.0000x reference)
import numpy as np
import ml_dtypes
import concourse.bass as bass
import concourse.mybir as mybir
from concourse.bass_utils import run_bass_kernel_spmd

F32 = mybir.dt.float32
BF16 = mybir.dt.bfloat16
AF = mybir.ActivationFunctionType
ALU = mybir.AluOpType
AX = mybir.AxisListType

D = 1024
T = 4096
HID = 4096
NDT = D // 128
NJT = HID // 128
NBE = 2
NEG = -1.0e30
TC = 512
NCH = T // TC
NQT = T // 128


class Em:
    ENGS = ("t", "v", "s", "g", "y")

    def __init__(self, nc):
        self.nc = nc
        self.fns = {k: [] for k in self.ENGS}
        self.sem = {}
        self.cnt = {}
        self.hwm = {e: {} for e in self.ENGS}
        for name in ("pe", "dv", "ac", "go"):
            self.sem[name] = nc.alloc_semaphore(name="sem_" + name)
            self.cnt[name] = 0

    def _get_sem(self, name):
        if name not in self.sem:
            self.sem[name] = self.nc.alloc_semaphore(name="sem_" + name)
            self.cnt[name] = 0
        return self.sem[name]

    def op(self, eng, fn, inc=None, waits=(), ndma=1):
        hw = self.hwm[eng]
        wl = []
        for s, v in waits:
            if v and v > 0 and hw.get(s, 0) < v:
                wl.append((self._get_sem(s), v))
                hw[s] = v
        amt = 16 if (inc and inc.startswith("d_")) else 1
        sem = self._get_sem(inc) if inc else None

        def run(h, fn=fn, wl=wl, sem=sem, amt=amt):
            for s, v in wl:
                h.wait_ge(s, v)
            ins = fn(h)
            if sem is not None:
                if isinstance(ins, (list, tuple)):
                    for i_ in ins:
                        i_.then_inc(sem, amt)
                else:
                    ins.then_inc(sem, amt)

        self.fns[eng].append(run)
        if inc:
            self.cnt[inc] += amt * ndma
            return self.cnt[inc]
        return None

    def all_counts(self):
        return [(k, v) for k, v in self.cnt.items() if k != "go" and v > 0]

    def finish(self):
        with self.nc.Block() as block:
            @block.tensor
            def _(h):
                for fn in self.fns["t"]:
                    fn(h)

            @block.vector
            def _(h):
                for fn in self.fns["v"]:
                    fn(h)

            @block.scalar
            def _(h):
                for fn in self.fns["s"]:
                    fn(h)

            @block.gpsimd
            def _(h):
                for fn in self.fns["g"]:
                    fn(h)

            @block.sync
            def _(h):
                for fn in self.fns["y"]:
                    fn(h)


def apx(handle, offset, dims):
    return bass.AP(handle, offset, [list(d) for d in dims])


def bcast_mid(ps_ap, n):
    return bass.AP(ps_ap.tensor, ps_ap.offset,
                   [list(ps_ap.ap[0]), [0, n], list(ps_ap.ap[1])])


def bcast_inner(ap, n):
    return bass.AP(ap.tensor, ap.offset,
                   [list(ap.ap[0]), list(ap.ap[1]), [0, n]])


def build_nc(nbe=NBE, dbg_scratch=False, nphase=8 * NBE):
    nc = bass.Bass("TRN2", target_bir_lowering=False)

    xt = nc.dram_tensor("xt", [nbe, D, T], F32, kind="ExternalInput").ap()
    outT = nc.dram_tensor("outT", [nbe, D, T], F32, kind="ExternalOutput").ap()
    wqT = nc.dram_tensor("wqT", [128, NDT, D], F32, kind="ExternalInput").ap()
    wkT = nc.dram_tensor("wkT", [128, NDT, D], F32, kind="ExternalInput").ap()
    wvT = nc.dram_tensor("wvT", [128, NDT, D], F32, kind="ExternalInput").ap()
    waT = nc.dram_tensor("waT", [128, NDT, 3 * D], F32, kind="ExternalInput").ap()
    wpT = nc.dram_tensor("wpT", [128, NDT, D], F32, kind="ExternalInput").ap()
    w1p = nc.dram_tensor("w1p", [128, NDT, NDT, 512], F32, kind="ExternalInput").ap()
    w2p = nc.dram_tensor("w2p", [128, NDT, NJT, 128], F32, kind="ExternalInput").ap()
    bqp = nc.dram_tensor("bqp", [128, NDT], F32, kind="ExternalInput").ap()
    bkp = nc.dram_tensor("bkp", [128, NDT], F32, kind="ExternalInput").ap()
    b1p = nc.dram_tensor("b1p", [128, NJT], F32, kind="ExternalInput").ap()
    bv_r = nc.dram_tensor("bv_r", [1, D], F32, kind="ExternalInput").ap()
    b2_r = nc.dram_tensor("b2_r", [1, D], F32, kind="ExternalInput").ap()
    sc_d = nc.dram_tensor("scal", [1, 1], F32, kind="ExternalInput").ap()

    skind = "ExternalOutput" if dbg_scratch else "Internal"
    nsc = nbe if dbg_scratch else 1
    q_scr = nc.dram_tensor("q_scr", [D, T], BF16, kind="Internal").ap()
    v_scr = nc.dram_tensor("v_scr", [T, D], BF16, kind="Internal").ap()
    w1b_scr = nc.dram_tensor("w1b_scr", [128, NDT, NDT, 512], BF16, kind="Internal").ap()
    w2b_scr = nc.dram_tensor("w2b_scr", [128, NDT, NJT, 128], BF16, kind="Internal").ap()
    w2_eff = nc.dram_tensor("w2_eff", [128, NDT, NJT, 128], BF16, kind="Internal").ap()
    xr1 = nc.dram_tensor("xr1", [nsc, D, T], F32, kind=skind).ap()
    xr2 = nc.dram_tensor("xr2", [nsc, D, T], F32, kind=skind).ap()
    xr3 = nc.dram_tensor("xr3", [nsc, D, T], F32, kind=skind).ap()

    eye_d = nc.inline_tensor(np.eye(128, dtype=ml_dtypes.bfloat16), name="eye_bf").ap()
    eye32_d = nc.inline_tensor(np.eye(128, dtype=np.float32), name="eye_f32").ap()
    negeye_d = nc.inline_tensor((NEG * np.eye(128)).astype(ml_dtypes.bfloat16),
                                name="negeye_bf").ap()
    h0 = np.array([2 * (r // 32) + (r // 16) % 2 for r in range(128)])
    hsA_np = np.zeros((128, 16), dtype=ml_dtypes.bfloat16)
    hsB_np = np.zeros((128, 16), dtype=ml_dtypes.bfloat16)
    selA_np = np.zeros((16, 128), dtype=ml_dtypes.bfloat16)
    selB_np = np.zeros((16, 128), dtype=ml_dtypes.bfloat16)
    for r in range(128):
        hsA_np[r, h0[r]] = 1.0
        hsB_np[r, 8 + h0[r]] = 1.0
        selA_np[h0[r], r] = 1.0
        selB_np[8 + h0[r], r] = 1.0
    m01_np = np.zeros((128, 16), dtype=ml_dtypes.bfloat16)
    for r in range(128):
        for t in range(16):
            if r % 16 <= t:
                m01_np[r, t] = 1.0
    hsA_d = nc.inline_tensor(hsA_np, name="hsA").ap()
    hsB_d = nc.inline_tensor(hsB_np, name="hsB").ap()
    selA_d = nc.inline_tensor(selA_np, name="selA").ap()
    selB_d = nc.inline_tensor(selB_np, name="selB").ap()
    m01_d = nc.inline_tensor(m01_np, name="m01").ap()

    em = Em(nc)
    guards = []

    def sb(name, shape, dt):
        g = nc.sbuf_tensor(name, shape, dt)
        t = g.__enter__()
        guards.append(g)
        return t

    ones_col_bf = sb("ones_col_bf", [128, 1], BF16)
    ones_row_bf = sb("ones_row_bf", [1, TC], BF16)
    ones_row_f32 = sb("ones_row_f32", [1, 128], F32)
    eye_sb = sb("eye_sb", [128, 128], BF16)
    eye32_sb = sb("eye32_sb", [128, 128], F32)
    negeye_sb = sb("negeye_sb", [128, 128], BF16)
    hsA_sb = sb("hsA_sb", [128, 16], BF16)
    hsB_sb = sb("hsB_sb", [128, 16], BF16)
    selA_sb = sb("selA_sb", [16, 128], BF16)
    selB_sb = sb("selB_sb", [16, 128], BF16)
    m01_sb = sb("m01_sb", [128, 16], BF16)
    bq_sb = sb("bq_sb", [128, NDT], F32)
    bk_sb = sb("bk_sb", [128, NDT], F32)
    b1_sb = sb("b1_sb", [128, NJT], F32)
    b2f = sb("b2f", [1, D], BF16)
    sc4 = sb("sc4", [4, 1], F32)
    scv128 = sb("scv128", [128, 1], F32)
    eps_sb = sb("eps_sb", [1, 1], F32)

    def clr(h):
        for s_ in em.sem.values():
            h.sem_clear(s_)
        return h.memset(ones_col_bf[:, :], 1.0)
    em.op("g", clr, inc="go")
    for e in ("t", "v", "s", "y"):
        def wgo(h):
            h.wait_ge(em.sem["go"], 1)
        em.fns[e].append(wgo)

    e_ms = em.op("v", lambda h: (h.memset(ones_row_bf[:, :], 1.0),
                                 h.memset(ones_row_f32[:, :], 1.0),
                                 h.memset(eps_sb[:, :], 1e-5))[-1], inc="dv")

    def loads(h):
        return [
            h.dma_start(eye_sb[:, :], eye_d[:, :]),
            h.dma_start(eye32_sb[:, :], eye32_d[:, :]),
            h.dma_start(negeye_sb[:, :], negeye_d[:, :]),
            h.dma_start(hsA_sb[:, :], hsA_d[:, :]),
            h.dma_start(hsB_sb[:, :], hsB_d[:, :]),
            h.dma_start(selA_sb[:, :], selA_d[:, :]),
            h.dma_start(selB_sb[:, :], selB_d[:, :]),
            h.dma_start(m01_sb[:, :], m01_d[:, :]),
            h.dma_start(bq_sb[:, :], bqp[:, :]),
            h.dma_start(bk_sb[:, :], bkp[:, :]),
            h.dma_start(b1_sb[:, :], b1p[:, :]),
            h.dma_start(b2f[0:1, :], b2_r[:, :]),
            h.dma_start(sc4[:, :], bass.AP(sc_d.tensor, 0, [[0, 4], [1, 1]])),
            h.dma_start(scv128[:, :], bass.AP(sc_d.tensor, 0, [[0, 128], [1, 1]])),
        ]
    e_consts = em.op("g", loads, inc="d_c0", ndma=14)
    wcast = {"done": 0}

    def emit_wcast_slab(j):
        wcast["done"] = em.op("g", lambda h, j=j: [
            h.dma_start(w1b_scr[:, j, :, :], w1p[:, j, :, :]),
            h.dma_start(w2b_scr[:, j, :, :], w2p[:, j, :, :])], inc="d_c1", ndma=2)
    e_w2b_final = 16 * 16

    def emit_ln(xb, xsq, xn_out, ps_s1, ps_s2, ps_m, ps_r, rows_f32, rows_bf,
                waits_x, waits_reuse, xtile=None):
        inv = 1.0 / D
        if xtile is not None:
            e_cast = em.op("v", lambda h: h.tensor_copy(xb[:, :, :], xtile),
                           inc="dv", waits=list(waits_x) + list(waits_reuse))
            xb_w = [("dv", e_cast)]
        else:
            xb_w = list(waits_x)
        e_sq = em.op("s", lambda h: h.square(xsq[:, :, :], xb[:, :, :]),
                     inc="ac", waits=xb_w + list(waits_reuse))

        def stats(h):
            ins = None
            for dt in range(NDT):
                h.matmul(ps_s1, ones_col_bf[:, 0:1], xb[:, dt, :],
                         start=(dt == 0), stop=(dt == NDT - 1))
            for dt in range(NDT):
                ins = h.matmul(ps_s2, ones_col_bf[:, 0:1], xsq[:, dt, :],
                               start=(dt == 0), stop=(dt == NDT - 1))
            return ins
        e_st = em.op("t", stats, inc="pe",
                     waits=xb_w + [("ac", e_sq)] + list(waits_reuse))
        e_r1 = em.op("v", lambda h: h.tensor_scalar_mul(rows_bf[0:1, 0, :], ps_s1, inv),
                     inc="dv", waits=[("pe", e_st)] + list(waits_reuse))
        e_r2 = em.op("v", lambda h: h.tensor_mul(
            rows_f32[0:1, 0, :], rows_bf[0:1, 0, :], rows_bf[0:1, 0, :]), inc="dv",
            waits=[("dv", e_r1)])
        e_r3 = em.op("v", lambda h: h.scalar_tensor_tensor(
            rows_f32[0:1, 1, :], ps_s2, inv, rows_f32[0:1, 0, :], ALU.mult, ALU.subtract),
            inc="dv", waits=[("dv", e_r2)])
        e_r4 = em.op("s", lambda h: h.activation(
            rows_f32[0:1, 1, :], rows_f32[0:1, 1, :], AF.Sqrt, bias=eps_sb[0:1, 0:1],
            scale=1.0), inc="ac", waits=[("dv", e_r3), ("dv", e_ms)])
        e_r5a = em.op("v", lambda h: h.reciprocal(rows_f32[0:1, 0, :], rows_f32[0:1, 1, :]),
                      inc="dv", waits=[("ac", e_r4)])
        e_r5 = em.op("v", lambda h: h.tensor_copy(rows_bf[0:1, 1, :], rows_f32[0:1, 0, :]),
                     inc="dv", waits=[("dv", e_r5a)])
        if ps_r is not None:
            def bcast(h):
                h.matmul(ps_m, ones_row_bf[0:1, 0:128], rows_bf[0:1, 0, :],
                         start=True, stop=True)
                return h.matmul(ps_r, ones_row_bf[0:1, 0:128], rows_bf[0:1, 1, :],
                                start=True, stop=True)
            e_bc = em.op("t", bcast, inc="pe", waits=[("dv", e_r5)] + list(waits_reuse))
            e_n1 = em.op("v", lambda h: h.tensor_tensor(
                xn_out[:, :, :], xb[:, :, :], bcast_mid(ps_m, NDT), ALU.subtract),
                inc="dv", waits=[("pe", e_bc)] + list(waits_reuse))
            e_n2 = em.op("v", lambda h: h.tensor_tensor(
                xn_out[:, :, :], xn_out[:, :, :], bcast_mid(ps_r, NDT), ALU.mult),
                inc="dv", waits=[("dv", e_n1)])
        else:
            e_b1_ = em.op("t", lambda h: h.matmul(
                ps_m, ones_row_bf[0:1, 0:128], rows_bf[0:1, 0, :], start=True, stop=True),
                inc="pe", waits=[("dv", e_r5)] + list(waits_reuse))
            e_n1 = em.op("v", lambda h: h.tensor_tensor(
                xn_out[:, :, :], xb[:, :, :], bcast_mid(ps_m, NDT), ALU.subtract),
                inc="dv", waits=[("pe", e_b1_)] + list(waits_reuse))
            e_b2_ = em.op("t", lambda h: h.matmul(
                ps_m, ones_row_bf[0:1, 0:128], rows_bf[0:1, 1, :], start=True, stop=True),
                inc="pe", waits=[("dv", e_n1)])
            e_n2 = em.op("v", lambda h: h.tensor_tensor(
                xn_out[:, :, :], xn_out[:, :, :], bcast_mid(ps_m, NDT), ALU.mult),
                inc="dv", waits=[("pe", e_b2_)])
        return {"xn": e_n2, "stats": e_st, "rows": e_r5}

    def phase_A(be, entry):
        gA, gA1 = [], []

        def mk(name, shape, dt, glist):
            g = nc.sbuf_tensor(f"{name}_{be}", shape, dt)
            t_ = g.__enter__()
            glist.append(g)
            return t_

        kT_sb = mk("kT", [128, NDT, T], BF16, gA)

        wq_sb = mk("wq", [128, NDT, D], BF16, gA1)
        wk_sb = mk("wk", [128, NDT, D], BF16, gA1)
        wv_sb = mk("wv", [128, NDT, D], BF16, gA1)
        bvbc = mk("bvbc", [128, D], F32, gA1)
        bv_row = mk("bv_row", [1, D], F32, gA1)
        xb_c = mk("xb", [128, 2, NDT, TC], BF16, gA1)
        xsq_c = mk("xsq", [128, NDT, TC], BF16, gA1)
        xn_c = mk("xn", [128, 2, NDT, TC], BF16, gA1)
        qo_c = mk("qo", [128, 2, NDT, TC], BF16, gA1)
        vo_c = mk("vo", [128, 2, 4, D], BF16, gA1)
        rows_f = mk("rowsf", [1, 2, TC], F32, gA1)
        rows_b = mk("rowsb", [1, 2, TC], BF16, gA1)

        psA_g = nc.psum_tensor(f"psA_{be}", [128, 4, 512], F32)
        psSt_g = nc.psum_tensor(f"psSt_{be}", [1, 2, TC], F32)
        psBc_g = nc.psum_tensor(f"psBc_{be}", [128, 2, TC], F32)
        psA, psSt, psBc = psA_g.__enter__(), psSt_g.__enter__(), psBc_g.__enter__()

        e_w = em.op("g", lambda h: [
            h.dma_start(bv_row[0:1, :], bv_r[:, :]),
            h.dma_start(wq_sb[:, :, :], wqT[:, :, :]),
            h.dma_start(wk_sb[:, :, :], wkT[:, :, :]),
            h.dma_start(wv_sb[:, :, :], wvT[:, :, :])],
            inc="d_aw", waits=list(entry) + [("d_c0", e_consts)], ndma=4)

        e_bv = 0
        for cc in range(2):
            e_pe = em.op("t", lambda h, cc=cc: h.matmul(
                psBc[:, 0, :], ones_row_f32[0:1, :], bv_row[0:1, cc * 512:(cc + 1) * 512],
                start=True, stop=True), inc="pe",
                waits=[("d_aw", e_w), ("dv", e_bv), ("dv", e_ms)] + list(entry))
            e_bv = em.op("v", lambda h, cc=cc: h.tensor_copy(
                bvbc[:, cc * 512:(cc + 1) * 512], psBc[:, 0, :]), inc="dv",
                waits=[("pe", e_pe)])

        e_xdma = [0] * NCH
        e_ln = [None] * NCH
        e_qkv_last = [0] * NCH
        e_qdma = [0] * NCH
        e_vdma = [0] * NCH
        bank_hist = []

        for c in range(NCH):
            p = c % 2
            xw = list(entry) if c < 2 else [("pe", e_ln[c - 2]["stats"]),
                                            ("dv", e_ln[c - 2]["xn"])]
            e_xdma[c] = em.op("g", lambda h, c=c, p=p: h.dma_start(
                xb_c[:, p, :, :],
                xt[be].rearrange("(a q) t -> q a t", q=128)[:, :, c * TC:(c + 1) * TC]),
                inc="d_ax%d" % p, waits=xw)
            if be == 0:
                emit_wcast_slab(c)
            ln_reuse = []
            if c == 0:
                ln_reuse.append(("dv", e_bv))
            if c >= 1:
                ln_reuse += [("pe", e_ln[c - 1]["stats"]), ("dv", e_ln[c - 1]["xn"])]
            if c >= 2:
                ln_reuse.append(("pe", e_qkv_last[c - 2]))
            ln = emit_ln(xb_c[:, p], xsq_c, xn_c[:, p],
                         psSt[0:1, 0, :], psSt[0:1, 1, :],
                         psBc[:, 0, :], psBc[:, 1, :], rows_f, rows_b,
                         waits_x=[("d_ax%d" % p, e_xdma[c]), ("dv", e_bv)],
                         waits_reuse=ln_reuse)
            e_ln[c] = ln

            base_w = [("d_aw", e_w), ("dv", ln["xn"])]
            last_pe = 0

            def grp_waits():
                wts = list(base_w)
                gi = len(bank_hist)
                if gi >= 4:
                    wts.append(bank_hist[gi - 4])
                return wts, gi % 4

            for ot in range(NDT):
                wts, bank = grp_waits()
                def qmm(h, ot=ot, bank=bank, p=p):
                    ins = None
                    for dt in range(NDT):
                        ins = h.matmul(psA[:, bank, :], wq_sb[:, dt, ot * 128:(ot + 1) * 128],
                                       xn_c[:, p, dt, :], start=(dt == 0), stop=(dt == NDT - 1))
                    return ins
                e_pe = em.op("t", qmm, inc="pe", waits=wts)
                last_pe = e_pe
                evw = [("pe", e_pe)]
                if c >= 2:
                    evw.append(("d_aq%d" % p, e_qdma[c - 2]))
                e_ev = em.op("s", lambda h, ot=ot, bank=bank, p=p: h.activation(
                    qo_c[:, p, ot, :], psA[:, bank, :], AF.Identity,
                    bias=bq_sb[:, ot:ot + 1], scale=1.0), inc="ac", waits=evw)
                bank_hist.append(("ac", e_ev))
            e_qdma[c] = em.op("y", lambda h, c=c, p=p: h.dma_start(
                q_scr.rearrange("(a q) t -> q a t", q=128)[:, :, c * TC:(c + 1) * TC],
                qo_c[:, p, :, :]), inc="d_aq%d" % p, waits=[bank_hist[-1]])

            for ot in range(NDT):
                wts, bank = grp_waits()
                def kmm(h, ot=ot, bank=bank, p=p):
                    ins = None
                    for dt in range(NDT):
                        ins = h.matmul(psA[:, bank, :], wk_sb[:, dt, ot * 128:(ot + 1) * 128],
                                       xn_c[:, p, dt, :], start=(dt == 0), stop=(dt == NDT - 1))
                    return ins
                e_pe = em.op("t", kmm, inc="pe", waits=wts)
                last_pe = e_pe
                e_ev = em.op("s", lambda h, ot=ot, bank=bank, c=c: h.activation(
                    kT_sb[:, ot, c * TC:(c + 1) * TC], psA[:, bank, :], AF.Identity,
                    bias=bk_sb[:, ot:ot + 1], scale=1.0), inc="ac", waits=[("pe", e_pe)])
                bank_hist.append(("ac", e_ev))

            for mt in range(4):
                for cc in range(2):
                    wts, bank = grp_waits()
                    def vmm(h, mt=mt, cc=cc, bank=bank, p=p):
                        ins = None
                        for dt in range(NDT):
                            ins = h.matmul(psA[:, bank, :],
                                           xn_c[:, p, dt, mt * 128:(mt + 1) * 128],
                                           wv_sb[:, dt, cc * 512:(cc + 1) * 512],
                                           start=(dt == 0), stop=(dt == NDT - 1))
                        return ins
                    e_pe = em.op("t", vmm, inc="pe", waits=wts)
                    last_pe = e_pe
                    evw = [("pe", e_pe), ("dv", e_bv)]
                    if c >= 2:
                        evw.append(("d_av%d" % p, e_vdma[c - 2]))
                    e_ev = em.op("v", lambda h, mt=mt, cc=cc, bank=bank, p=p: h.tensor_tensor(
                        vo_c[:, p, mt, cc * 512:(cc + 1) * 512], psA[:, bank, :],
                        bvbc[:, cc * 512:(cc + 1) * 512], ALU.add), inc="dv",
                        waits=evw)
                    bank_hist.append(("dv", e_ev))
            e_vdma[c] = em.op("y", lambda h, c=c, p=p: h.dma_start(
                v_scr.rearrange("(a q) o -> q a o", q=128)[:, c * 4:(c + 1) * 4, :],
                vo_c[:, p, :, :]), inc="d_av%d" % p, waits=[bank_hist[-1]])
            e_qkv_last[c] = last_pe

        a1_tail = em.all_counts()
        for g in reversed([psA_g, psSt_g, psBc_g]):
            g.__exit__(None, None, None)
        for g in reversed(gA1):
            g.__exit__(None, None, None)

        gA2 = []
        v_sb = mk("v", [128, T // 128, D], BF16, gA2)
        qt_sb = mk("qt", [128, 2, NDT, 128], BF16, gA2)
        xc_sb = mk("xc", [128, 2, NDT, 128], F32, gA2)
        pT = mk("pT", [128, 2, 512], BF16, gA2)
        ob = mk("ob", [128, 2, D], BF16, gA2)
        x1o = mk("x1o", [128, 2, NDT, 128], F32, gA2)
        rtot = mk("rtot", [128, 2], F32, gA2)

        psS_g = nc.psum_tensor(f"psS_{be}", [128, 2, 512], F32)
        psO_g = nc.psum_tensor(f"psO_{be}", [128, 1024], F32)
        psT_g = nc.psum_tensor(f"psT_{be}", [128, 2, 1024], BF16)
        psD_g = nc.psum_tensor(f"psD_{be}", [128, 8], F32)
        psS, psO, psT, psD = (psS_g.__enter__(), psO_g.__enter__(),
                              psT_g.__enter__(), psD_g.__enter__())

        NG = 8
        e_vload = [0] * NG
        e_qt = [0] * (NQT + 2)
        e_xc = [0] * (NQT + 2)
        e_fin = [0] * (NQT + 1)
        e_rcp = [0] * (NQT + 1)
        e_add = [0] * NQT
        e_x1dma = [0] * NQT
        e_slast = [0] * NQT
        bankS = [list(a1_tail), list(a1_tail)]
        bankP = [list(a1_tail), list(a1_tail)]
        bankT = [list(a1_tail), list(a1_tail)]
        g_ob = [[], []]

        def dma_qt(i):
            if i >= NQT:
                return
            pi = i % 2
            wq_ = list(a1_tail) if i < 2 else [("pe", e_slast[i - 2])]
            e_qt[i] = em.op("y", lambda h, i=i, pi=pi: h.dma_start(
                qt_sb[:, pi, :, :],
                q_scr.rearrange("(a q) t -> q a t", q=128)[:, :, i * 128:(i + 1) * 128]),
                inc="d_qt%d" % pi, waits=wq_)

        def dma_xc(i):
            if i >= NQT:
                return
            pi = i % 2
            wx_ = list(a1_tail) if i < 2 else [("dv", e_add[i - 2])]
            e_xc[i] = em.op("y", lambda h, i=i, pi=pi: h.dma_start(
                xc_sb[:, pi, :, :],
                xt[be].rearrange("(a q) t -> q a t", q=128)[:, :, i * 128:(i + 1) * 128]),
                inc="d_xc%d" % pi, waits=wx_)

        def emit_sgroup(ti, g):
            p = ti % 2
            b = g % 2
            def smm(h, ti=ti, g=g, b=b, p=p):
                ins = None
                for kk in range(4):
                    kt = 4 * g + kk
                    dodiag = (kt == ti)
                    for ot in range(NDT):
                        ins = h.matmul(
                            psS[:, b, kk * 128:(kk + 1) * 128],
                            kT_sb[:, ot, kt * 128:(kt + 1) * 128],
                            qt_sb[:, p, ot, :],
                            start=(ot == 0), stop=(ot == NDT - 1 and not dodiag))
                    if dodiag:
                        ins = h.matmul(psS[:, b, kk * 128:(kk + 1) * 128],
                                       negeye_sb[:, :], eye_sb[:, :],
                                       start=False, stop=True)
                return ins
            e_s = em.op("t", smm, inc="pe",
                        waits=[("d_qt%d" % p, e_qt[ti])] + list(bankS[b]))
            e_slast[ti] = e_s
            e_ex = em.op("s", lambda h, b=b: h.activation(
                pT[:, b, :], psS[:, b, :], AF.Exp, scale=1.0 / D), inc="ac",
                waits=[("pe", e_s)] + list(bankP[b]))
            bankS[b] = [("ac", e_ex)]
            return e_ex

        def emit_pvgroup(ti, g, e_ex):
            b = g % 2
            def pvg(h, g=g, b=b):
                ins = None
                for kk in range(4):
                    kt = 4 * g + kk
                    lhs = pT[:, b, kk * 128:(kk + 1) * 128]
                    for cc in range(2):
                        h.matmul(psO[:, cc * 512:(cc + 1) * 512], lhs,
                                 v_sb[:, kt, cc * 512:(cc + 1) * 512],
                                 start=(kt == 0), stop=(kt == 31),
                                 skip_group_check=True)
                    ins = h.matmul(psD[:, 0:1], lhs, ones_col_bf[:, 0:1],
                                   start=(kt == 0), stop=(kt == 31),
                                   skip_group_check=True)
                return ins
            pvw = [("ac", e_ex), ("d_vl", e_vload[g])]
            if g == 0:
                pvw += [("dv", e_fin[ti - 1] if ti > 0 else 0),
                        ("dv", e_rcp[ti - 1] if ti > 0 else 0)]
            e_pv = em.op("t", pvg, inc="pe", waits=pvw)
            bankP[b] = [("pe", e_pv)]
            return e_pv

        def emit_otrans(ti):
            p = ti % 2
            last_otr = 0
            for j in range(NDT):
                bp = j % 2
                e_otr = em.op("t", lambda h, j=j, bp=bp, p=p: h.transpose(
                    psT[:, bp, 0:128], ob[:, p, j * 128:(j + 1) * 128], eye_sb[:, :]),
                    inc="pe", waits=[("dv", e_fin[ti])] + list(bankT[bp]))
                last_otr = e_otr
                aw = [("pe", e_otr), ("d_xc%d" % p, e_xc[ti])]
                if ti >= 2:
                    aw.append(("d_x1%d" % p, e_x1dma[ti - 2]))
                last_add = em.op("v", lambda h, j=j, bp=bp, p=p: h.tensor_tensor(
                    x1o[:, p, j, :], psT[:, bp, 0:128], xc_sb[:, p, j, :], ALU.add),
                    inc="dv", waits=aw)
                bankT[bp] = [("dv", last_add)]
            g_ob[p] = [("pe", last_otr)]
            e_add[ti] = last_add
            dst = xr1[be if dbg_scratch else 0]
            e_x1dma[ti] = em.op("y", lambda h, ti=ti, p=p, dst=dst: h.dma_start(
                dst.rearrange("(a q) t -> q a t", q=128)[:, :, ti * 128:(ti + 1) * 128],
                x1o[:, p, :, :]), inc="d_x1%d" % p, waits=[("dv", e_add[ti])])

        dma_qt(0)
        dma_qt(1)
        dma_xc(0)
        dma_xc(1)
        for k in range(NG):
            e_vload[k] = em.op("g", lambda h, k=k: h.dma_start(
                v_sb[:, k * 4:(k + 1) * 4, :],
                v_scr.rearrange("(a q) o -> q a o", q=128)[:, k * 4:(k + 1) * 4, :]),
                inc="d_vl", waits=list(a1_tail))
        for ti in range(NQT):
            p = ti % 2
            ex = [0] * NG
            ex[0] = emit_sgroup(ti, 0)
            ex[1] = emit_sgroup(ti, 1)
            if ti > 0:
                emit_otrans(ti - 1)
                dma_xc(ti + 1)
            e_pvlast = 0
            for g in range(NG):
                e_pvlast = emit_pvgroup(ti, g, ex[g])
                if g + 2 < NG:
                    ex[g + 2] = emit_sgroup(ti, g + 2)
            e_rcp[ti] = em.op("v", lambda h, p=p: h.reciprocal(
                rtot[:, p:p + 1], psD[:, 0:1]), inc="dv",
                waits=[("pe", e_pvlast)])
            e_fin[ti] = em.op("v", lambda h, p=p: h.tensor_scalar_mul(
                ob[:, p, :], psO[:, :], rtot[:, p:p + 1]), inc="dv",
                waits=[("dv", e_rcp[ti])] + list(g_ob[p]))
            dma_qt(ti + 2)
        emit_otrans(NQT - 1)

        for g in reversed(gA2):
            g.__exit__(None, None, None)
        for g in reversed([psS_g, psO_g, psT_g, psD_g]):
            g.__exit__(None, None, None)
        for g in reversed(gA):
            g.__exit__(None, None, None)
        return em.all_counts()

    def phase_F(be, app, xin, xout, entry):
        gF = []

        def mk(name, shape, dt):
            g = nc.sbuf_tensor(f"{name}_{be}_{app}", shape, dt)
            t_ = g.__enter__()
            gF.append(g)
            return t_

        w1r = mk("w1r", [128, NDT, NDT, 512], BF16)
        g_sb = mk("gF", [128, NJT, TC], BF16)
        xtb = mk("xtbF", [128, 2, NDT, TC], F32)
        xn = mk("xnF", [128, NDT, TC], BF16)
        xbs = mk("xbsF", [128, 2, TC], BF16)
        xqs = mk("xqsF", [128, 2, TC], BF16)
        w2blk = mk("w2blk", [128, 2, NJT, 128], BF16)
        w2bld = mk("w2bld", [128, NJT, 128], BF16)
        ltr = mk("ltrF", [128, 2, 2, D], BF16)
        lt_e = mk("lteF", [8, NDT, 128], BF16)
        lt_o = mk("ltoF", [8, NDT, 128], BF16)
        xn_sc = mk("xnscF", [128, NDT, 8], BF16)
        rows_f = mk("rowsfF", [1, 2, TC], F32)
        rows_b = mk("rowsbF", [1, 2, TC], BF16)

        psh_g = nc.psum_tensor(f"psh_{be}_{app}", [128, 2, 512], F32)
        pso_g = nc.psum_tensor(f"pso_{be}_{app}", [128, 2, 512], F32)
        psW_g = nc.psum_tensor(f"psW_{be}_{app}", [128, 2, 512], F32)
        psbc_g = nc.psum_tensor(f"psbc_{be}_{app}", [128, 512], F32)
        psLt_g = nc.psum_tensor(f"psLt_{be}_{app}", [128, 2, 512], BF16)
        psh, pso, psW, psbc, psLt = (psh_g.__enter__(), pso_g.__enter__(),
                                     psW_g.__enter__(), psbc_g.__enter__(),
                                     psLt_g.__enter__())

        def flat2(sl, n):
            return bass.AP(sl.tensor, sl.offset, [list(sl.ap[0]), [1, n]])

        e_xin = [0] * NCH
        e_xout = [0] * NCH

        e_w1b = [0] * NDT
        for jc in range(NDT):
            e_w1b[jc] = em.op("g", lambda h, jc=jc: h.dma_start(
                w1r[:, jc, :, :], w1b_scr[:, jc, :, :]), inc="d_w1b",
                waits=list(entry) + [("d_c1", e_w2b_final)])

        def dma_x(c):
            if c >= NCH:
                return
            p = c % 2
            xw = list(entry) if c < 2 else [("d_fo%d" % p, e_xout[c - 2])]
            e_xin[c] = em.op("y", lambda h, c=c, p=p: h.dma_start(
                xtb[:, p, :, :],
                xin.rearrange("(a q) t -> q a t", q=128)[:, :, c * TC:(c + 1) * TC]),
                inc="d_fx%d" % p, waits=xw)

        st = {"r3": 0, "r5": 0, "bm": 0, "br": 0, "mul": 0, "sub": 0,
              "stats": 0, "mm_prev": [0] * NDT}

        def emit_stats(c, psw_free):
            p = c % 2
            e_mm = [0] * NDT
            for dt in range(NDT):
                s = dt % 2
                cw = [("d_fx%d" % p, e_xin[c])]
                if dt >= 2:
                    cw.append(("pe", e_mm[dt - 2]))
                elif st["mm_prev"][NDT - 2 + dt]:
                    cw.append(("pe", st["mm_prev"][NDT - 2 + dt]))
                e_cast = em.op("v", lambda h, dt=dt, s=s, p=p: h.tensor_copy(
                    xbs[:, s, :], xtb[:, p, dt, :]), inc="dv", waits=cw)
                e_sq = em.op("s", lambda h, s=s: h.square(
                    xqs[:, s, :], xbs[:, s, :]), inc="ac",
                    waits=[("dv", e_cast)] + cw[1:])

                def stmm(h, dt=dt, s=s):
                    h.matmul(psW[0:1, 0, :], ones_col_bf[:, 0:1], xbs[:, s, :],
                             start=(dt == 0), stop=(dt == NDT - 1),
                             skip_group_check=True)
                    return h.matmul(psW[0:1, 1, :], ones_col_bf[:, 0:1],
                                    xqs[:, s, :], start=(dt == 0),
                                    stop=(dt == NDT - 1), skip_group_check=True)
                e_mm[dt] = em.op("t", stmm, inc="pe",
                                 waits=[("dv", e_cast), ("ac", e_sq)] +
                                       (psw_free if dt == 0 else []))
            st["mm_prev"] = e_mm
            st["stats"] = e_mm[NDT - 1]

        def emit_rows(c):
            inv = 1.0 / D
            rw = [("pe", st["stats"])]
            if st["br"]:
                rw.append(("pe", st["br"]))
            e_r1 = em.op("v", lambda h: h.tensor_scalar_mul(
                rows_b[0:1, 0, :], psW[0:1, 0, :], inv), inc="dv", waits=rw)
            e_r2 = em.op("v", lambda h: h.tensor_mul(
                rows_f[0:1, 0, :], rows_b[0:1, 0, :], rows_b[0:1, 0, :]),
                inc="dv", waits=[("dv", e_r1)])
            e_r3 = em.op("v", lambda h: h.scalar_tensor_tensor(
                rows_f[0:1, 1, :], psW[0:1, 1, :], inv, rows_f[0:1, 0, :],
                ALU.mult, ALU.subtract), inc="dv", waits=[("dv", e_r2)])
            e_r4 = em.op("s", lambda h: h.activation(
                rows_f[0:1, 1, :], rows_f[0:1, 1, :], AF.Sqrt,
                bias=eps_sb[0:1, 0:1], scale=1.0), inc="ac",
                waits=[("dv", e_r3), ("dv", e_ms)])
            e_r5a = em.op("v", lambda h: h.reciprocal(
                rows_f[0:1, 0, :], rows_f[0:1, 1, :]), inc="dv",
                waits=[("ac", e_r4)])
            e_r5 = em.op("v", lambda h: h.tensor_copy(
                rows_b[0:1, 1, :], rows_f[0:1, 0, :]), inc="dv",
                waits=[("dv", e_r5a)])
            st["r3"] = e_r3
            st["r5"] = e_r5

        def emit_bcast_m(c):
            bw = [("dv", st["r5"])]
            if st["mul"]:
                bw.append(("dv", st["mul"]))
            e_bm = em.op("t", lambda h: h.matmul(
                psbc[:, :], ones_row_bf[0:1, 0:128], rows_b[0:1, 0, :],
                start=True, stop=True), inc="pe", waits=bw)
            p = c % 2
            e_sub = em.op("v", lambda h, p=p: h.tensor_tensor(
                xn[:, :, :], xtb[:, p, :, :], bcast_mid(psbc[:, :], NDT),
                ALU.subtract), inc="dv", waits=[("pe", e_bm)])
            st["bm"] = e_bm
            st["sub"] = e_sub

        def emit_bcast_r(c):
            e_br = em.op("t", lambda h: h.matmul(
                psbc[:, :], ones_row_bf[0:1, 0:128], rows_b[0:1, 1, :],
                start=True, stop=True), inc="pe", waits=[("dv", st["sub"])])
            e_mul = em.op("v", lambda h: h.tensor_tensor(
                xn[:, :, :], xn[:, :, :], bcast_mid(psbc[:, :], NDT),
                ALU.mult), inc="dv", waits=[("pe", e_br)])
            st["br"] = e_br
            st["mul"] = e_mul

        dma_x(0)
        dma_x(1)
        emit_stats(0, list(entry))
        emit_rows(0)
        emit_bcast_m(0)
        emit_bcast_r(0)
        e_mul0 = st["mul"]

        e_sc = [0] * NDT
        for dt in range(NDT):
            sl = xn[:, dt, 0:TC]
            odd_ap = bass.AP(sl.tensor, sl.offset + 2, [list(sl.ap[0]), [2, 8]])
            e_sc[dt] = em.op("v", lambda h, dt=dt, odd_ap=odd_ap: h.tensor_scalar_mul(
                xn_sc[:, dt, :], odd_ap, scv128[:, 0:1]), inc="dv",
                waits=[("dv", e_mul0), ("d_c0", e_consts)])
        e_ltcp = 0
        e_lttr = 0
        for dt in range(NDT):
            sl = xn[:, dt, 0:TC]
            ev_ap = bass.AP(sl.tensor, sl.offset + 1, [list(sl.ap[0]), [2, 8]])
            for which in (0, 1):
                tw = [("dv", e_mul0 if which == 0 else e_sc[dt])]
                if e_ltcp:
                    tw.append(("dv", e_ltcp))
                tw += list(entry)
                if which == 0:
                    e_lttr = em.op("t", lambda h, ev_ap=ev_ap: h.transpose(
                        psLt[0:8, 0, 0:128], ev_ap, eye_sb[:, :]), inc="pe",
                        waits=tw)
                    dstt = lt_e
                else:
                    e_lttr = em.op("t", lambda h, dt=dt: h.transpose(
                        psLt[0:8, 0, 0:128], xn_sc[:, dt, :], eye_sb[:, :]),
                        inc="pe", waits=tw)
                    dstt = lt_o
                e_ltcp = em.op("v", lambda h, dt=dt, dstt=dstt: h.tensor_copy(
                    dstt[0:8, dt, :], psLt[0:8, 0, 0:128]), inc="dv",
                    waits=[("pe", e_lttr)])

        def scatA(h):
            out = []
            for i in range(4):
                b = 32 * (i >> 1)
                ia = i & 1
                out.append(h.dma_start(ltr[b:b + 1, ia, 0, :], lt_e[i:i + 1, :, :]))
                out.append(h.dma_start(ltr[b:b + 1, ia, 1, :], lt_o[i:i + 1, :, :]))
            return out
        e_sca = em.op("g", scatA, inc="d_lt", waits=[("dv", e_ltcp)], ndma=8)

        bank_h = [list(entry), list(entry)]
        bank_o = [list(entry), list(entry)]
        e_w1add = [0] * NDT
        e_w1mm = [0] * NDT
        prev_add = [0, 0]
        e_glast = 0

        def emit_w1build(jc):
            i = jc >> 1
            b = 32 * (i >> 1)
            ia = i & 1
            e_add = 0
            last = 0
            for dt in range(NDT):
                bp = dt % 2
                def w1mm(h, jc=jc, dt=dt, b=b, ia=ia, bp=bp):
                    return h.matmul(
                        psW[:, bp, :],
                        ltr[b:b + 1, ia, 1, dt * 128:(dt + 1) * 128],
                        ltr[b:b + 1, ia, 0, (jc & 1) * 512:((jc & 1) + 1) * 512],
                        start=True, stop=True)
                ww = [("d_lt", e_sca), ("dv", st["r3"])]
                if prev_add[bp]:
                    ww.append(("dv", prev_add[bp]))
                last = em.op("t", w1mm, inc="pe", waits=ww)
                e_add = em.op("v", lambda h, jc=jc, dt=dt, bp=bp: h.tensor_tensor(
                    w1r[:, jc, dt, :], psW[:, bp, :], w1r[:, jc, dt, :], ALU.add),
                    inc="dv", waits=[("pe", last), ("d_w1b", e_w1b[jc])])
                prev_add[bp] = e_add
            e_w1add[jc] = e_add
            e_w1mm[jc] = last

        def emit_hgroup(c, jt, extra=()):
            bp = jt % 2
            jc = jt >> 2
            jjo = (jt & 3) * 128
            def hmm(h, jc=jc, jjo=jjo, bp=bp):
                ins = None
                for dt in range(NDT):
                    ins = h.matmul(psh[:, bp, :], w1r[:, jc, dt, jjo:jjo + 128],
                                   xn[:, dt, :], start=(dt == 0),
                                   stop=(dt == NDT - 1))
                return ins
            e_h = em.op("t", hmm, inc="pe",
                        waits=[("dv", st["mul"])] + list(bank_h[bp]) + list(extra))
            e_g = em.op("s", lambda h, jt=jt, bp=bp: h.activation(
                g_sb[:, jt, :], psh[:, bp, :], AF.Gelu,
                bias=b1_sb[:, jt:jt + 1], scale=1.0), inc="ac",
                waits=[("pe", e_h)])
            bank_h[bp] = [("ac", e_g)]
            return e_h, e_g

        for jc in range(NDT):
            emit_w1build(jc)
            for jj in range(4):
                jt = jc * 4 + jj
                e_hlast, e_glast = emit_hgroup(0, jt, extra=[("dv", e_w1add[jc])])

        emit_stats(1, [("dv", e_w1add[NDT - 1])])
        emit_rows(1)

        def scatB(h):
            out = []
            for i in range(4):
                b = 32 * (i >> 1)
                ia = i & 1
                se = lt_e[4 + i:5 + i, 0:NDT, 0:128]
                de = ltr[b:b + 1, ia, 0, 0:D]
                for r in range(4):
                    src_e = bass.AP(se.tensor, se.offset + r,
                                    [list(se.ap[0]), [128, 8], [4, 32]])
                    dst_e = bass.AP(de.tensor, de.offset + r * 256,
                                    [list(de.ap[0]), [32, 8], [1, 32]])
                    out.append(h.dma_start(dst_e, src_e,
                                           allow_slow_non_contiguous=True))
                out.append(h.dma_start(ltr[b:b + 1, ia, 1, :], lt_o[4 + i:5 + i, :, :]))
            return out
        e_scb = em.op("g", scatB, inc="d_lt2",
                      waits=[("pe", e_w1mm[NDT - 1])], ndma=20)

        e_slab = [0] * NDT
        e_w2addl = 0
        prev_add2 = [("dv", e_w1add[NDT - 1]), ("dv", e_w1add[NDT - 1])]
        prev_slab_dma = 0
        for ot in range(NDT):
            i = ot >> 1
            b = 32 * (i >> 1)
            ia = i & 1
            bw = [("d_c1", e_w2b_final)]
            if prev_slab_dma:
                bw.append(("d_w2e", prev_slab_dma))
            e_w2bb = em.op("g", lambda h, ot=ot: h.dma_start(
                w2bld[:, :, :], w2b_scr[:, ot, :, :]), inc="d_w2bb", waits=bw)
            for jtq in range(NDT):
                bp = jtq % 2
                def b2mm(h, ot=ot, jtq=jtq, b=b, ia=ia, bp=bp):
                    ins = None
                    for jj in range(4):
                        jt = jtq * 4 + jj
                        k = jt >> 3
                        ins = h.matmul(
                            psW[:, bp, jj * 128:(jj + 1) * 128],
                            ltr[b:b + 1, ia, 1, (jt & 7) * 128:((jt & 7) + 1) * 128],
                            ltr[b:b + 1, ia, 0,
                                k * 256 + (ot & 1) * 128: k * 256 + (ot & 1) * 128 + 128],
                            start=True, stop=True)
                    return ins
                e_m2 = em.op("t", b2mm, inc="pe",
                             waits=[("d_lt2", e_scb), ("dv", st["r3"]), prev_add2[bp]])
                sl = w2bld[:, jtq * 4:(jtq + 1) * 4, :]
                e_a2 = em.op("v", lambda h, sl=sl, bp=bp: h.tensor_tensor(
                    flat2(sl, 512), psW[:, bp, :], flat2(sl, 512), ALU.add),
                    inc="dv", waits=[("pe", e_m2), ("d_w2bb", e_w2bb)])
                prev_add2[bp] = ("dv", e_a2)
                e_w2addl = e_a2
            e_slab[ot] = em.op("g", lambda h, ot=ot: h.dma_start(
                w2_eff[:, ot, :, :], w2bld[:, :, :]), inc="d_w2e",
                waits=[("dv", e_w2addl)])
            prev_slab_dma = e_slab[ot]

        e_w2d = {}
        e_omm_arr = [0] * NDT
        e_omm_prev = [0] * NDT

        def dma_w2(c, ot):
            if ot >= NDT:
                return
            bpw = ot % 2
            ww = [("d_w2e", e_slab[ot])]
            if ot >= 2:
                ww.append(("pe", e_omm_arr[ot - 2]))
            elif c >= 1 and e_omm_prev[NDT - 2 + ot]:
                ww.append(("pe", e_omm_prev[NDT - 2 + ot]))
            e_w2d[(c, ot)] = em.op("g", lambda h, ot=ot, bpw=bpw: h.dma_start(
                w2blk[:, bpw, :, :], w2_eff[:, ot, :, :]), inc="d_fw2%d" % bpw,
                waits=ww)

        for c in range(NCH):
            p = c % 2
            if c > 0:
                for jt in range(NJT):
                    e_hlast, e_glast = emit_hgroup(c, jt)
                if c < NCH - 1:
                    psw_free = [("dv", e_w2addl)] if c == 1 else [("dv", st["r3"])]
                    emit_stats(c + 1, psw_free)
                    emit_rows(c + 1)
            dma_w2(c, 0)
            dma_w2(c, 1)
            e_oev_last = 0
            for ot in range(NDT):
                bp = ot % 2
                def omm(h, c=c, ot=ot, bp=bp):
                    for jt in range(NJT):
                        h.matmul(pso[:, bp, :], w2blk[:, ot % 2, jt, :],
                                 g_sb[:, jt, :], start=(jt == 0), stop=False)
                    return h.matmul(pso[:, bp, :], b2f[0:1, ot * 128:(ot + 1) * 128],
                                    ones_row_bf[0:1, :], start=False, stop=True)
                e_o = em.op("t", omm, inc="pe",
                            waits=[("ac", e_glast), ("d_fw2%d" % (ot % 2), e_w2d[(c, ot)]),
                                   ("d_c0", e_consts)] + list(bank_o[bp]))
                e_omm_arr[ot] = e_o
                dma_w2(c, ot + 2)
                e_oev = em.op("v", lambda h, ot=ot, bp=bp, p=p: h.tensor_tensor(
                    xtb[:, p, ot, :], pso[:, bp, :], xtb[:, p, ot, :], ALU.add),
                    inc="dv", waits=[("pe", e_o)])
                bank_o[bp] = [("dv", e_oev)]
                e_oev_last = e_oev
                if c < NCH - 1:
                    if ot == 1:
                        emit_bcast_m(c + 1)
                    if ot == 3:
                        emit_bcast_r(c + 1)
            e_omm_prev = list(e_omm_arr)
            e_xout[c] = em.op("y", lambda h, c=c, p=p: h.dma_start(
                xout.rearrange("(a q) t -> q a t", q=128)[:, :, c * TC:(c + 1) * TC],
                xtb[:, p, :, :]), inc="d_fo%d" % p, waits=[("dv", e_oev_last)])
            dma_x(c + 2)

        for g in reversed(gF):
            g.__exit__(None, None, None)
        for g in reversed([psh_g, pso_g, psW_g, psbc_g, psLt_g]):
            g.__exit__(None, None, None)
        return em.all_counts()

    def phase_B(be, xin, xout, entry):
        gB = []

        def mk(name, shape, dt):
            g = nc.sbuf_tensor(f"{name}B_{be}", shape, dt)
            t_ = g.__enter__()
            gB.append(g)
            return t_

        wa_sb = mk("wa", [128, NDT, 3 * D], BF16)
        wp_sb = mk("wp", [128, NDT, D], BF16)
        xtb = mk("xtb", [128, 3, NDT, TC], F32)
        xbtb = mk("xb", [128, NDT, TC], BF16)
        xsq = mk("xsq", [128, NDT, TC], BF16)
        xn_tb = mk("xn", [128, 2, NDT, TC], BF16)
        qT2 = mk("qT2", [128, 8, TC], BF16)
        k16bd = mk("k16bd", [128, 8, 32], BF16)
        v16p = mk("v16p", [128, 8, 16], F32)
        vT16 = mk("vT16", [16, 8, 128], BF16)
        vexpA = mk("vexpA", [128, 1024], BF16)
        vexpB = mk("vexpB", [128, 1024], BF16)
        pB = mk("pB", [128, 2, TC], BF16)
        p2n = mk("p2n", [128, 2, TC], BF16)
        recT = mk("recT", [16, TC], F32)
        recTb = mk("recTb", [16, TC], BF16)
        aT = mk("aT", [128, NDT, TC], BF16)
        rows_f = mk("rowsf", [1, 2, TC], F32)
        rows_b = mk("rowsb", [1, 2, TC], BF16)

        ps1_g = nc.psum_tensor(f"ps1B_{be}", [128, 2, TC], F32)
        ps2_g = nc.psum_tensor(f"ps2B_{be}", [128, 2, TC], F32)
        psOT_g = nc.psum_tensor(f"psOTB_{be}", [128, 2, TC], F32)
        psBc_g = nc.psum_tensor(f"psBcB_{be}", [128, 2, TC], F32)
        ps1, ps2, psOT, psBc = (ps1_g.__enter__(), ps2_g.__enter__(),
                                psOT_g.__enter__(), psBc_g.__enter__())

        e_wbq = em.op("g", lambda h: h.dma_start(
            wa_sb[:, :, 0:D], waT[:, :, 0:D]), inc="d_bw", waits=list(entry))
        e_wbkv = em.op("g", lambda h: h.dma_start(
            wa_sb[:, :, D:3 * D], waT[:, :, D:3 * D]), inc="d_bw")
        e_wbp = em.op("g", lambda h: h.dma_start(
            wp_sb[:, :, :], wpT[:, :, :]), inc="d_bw")

        e_xin = [0] * NCH
        e_xout = [0] * NCH

        def dma_x(c, w0=None):
            if c >= NCH:
                return
            xw = list(entry) if c < 3 else []
            if w0:
                xw += w0
            e_xin[c] = em.op("y", lambda h, c=c: h.dma_start(
                xtb[:, c % 3, :, :],
                xin.rearrange("(a q) t -> q a t", q=128)[:, :, c * TC:(c + 1) * TC]),
                inc="d_bx", waits=xw)

        inv = 1.0 / D
        st = {k: [0] * (NCH + 2) for k in
              ("cast", "sq", "stats", "r3", "r5", "bm", "br", "sub", "mul")}
        e_qpj = [0] * (NCH + 1)
        e_qev = [[0] * 8 for _ in range(NCH)]
        e_lg = [[0, 0] for _ in range(NCH)]
        e_p2n = [[0, 0] for _ in range(NCH + 1)]
        e_pv = [[0] * 8 for _ in range(NCH)]
        e_aev_s = [0] * (NCH + 1)
        e_aev_v = [0] * (NCH + 1)
        e_rcp = [0] * NCH
        e_res = [[0] * 8 for _ in range(NCH + 1)]
        e_hs = [0] * NCH
        e_pjl = [0] * (NCH + 1)

        def emit_cast(c, extra=()):
            if c >= NCH:
                return
            cw = [("d_bx", e_xin[c])] + list(extra)
            if c >= 1:
                cw.append(("dv", st["sub"][c - 1]))
            st["cast"][c] = em.op("v", lambda h, c=c: h.tensor_copy(
                xbtb[:, :, :], xtb[:, c % 3, :, :]), inc="dv", waits=cw)
            sqw = [("dv", st["cast"][c])]
            if c >= 1:
                sqw.append(("pe", st["stats"][c - 1]))
            st["sq"][c] = em.op("s", lambda h: h.square(
                xsq[:, :, :], xbtb[:, :, :]), inc="ac", waits=sqw)

        def emit_stats(c, extra=()):
            if c >= NCH:
                return
            sw = [("dv", st["cast"][c]), ("ac", st["sq"][c])] + list(extra)
            if c >= 1:
                sw += [("dv", st["sub"][c - 1]), ("dv", st["mul"][c - 1])]

            def stmm(h):
                ins = None
                for dt in range(NDT):
                    h.matmul(psBc[0:1, 0, :], ones_col_bf[:, 0:1], xbtb[:, dt, :],
                             start=(dt == 0), stop=(dt == NDT - 1),
                             skip_group_check=True)
                    ins = h.matmul(psBc[0:1, 1, :], ones_col_bf[:, 0:1],
                                   xsq[:, dt, :], start=(dt == 0),
                                   stop=(dt == NDT - 1), skip_group_check=True)
                return ins
            st["stats"][c] = em.op("t", stmm, inc="pe", waits=sw)

        def emit_rows(c):
            if c >= NCH:
                return
            rw = [("pe", st["stats"][c])]
            if c >= 1:
                rw.append(("pe", st["bm"][c - 1]))
            e_r1 = em.op("v", lambda h: h.tensor_scalar_mul(
                rows_b[0:1, 0, :], psBc[0:1, 0, :], inv), inc="dv", waits=rw)
            e_r2 = em.op("v", lambda h: h.tensor_mul(
                rows_f[0:1, 0, :], rows_b[0:1, 0, :], rows_b[0:1, 0, :]),
                inc="dv", waits=[("dv", e_r1)])
            st["r3"][c] = em.op("v", lambda h: h.scalar_tensor_tensor(
                rows_f[0:1, 1, :], psBc[0:1, 1, :], inv, rows_f[0:1, 0, :],
                ALU.mult, ALU.subtract), inc="dv", waits=[("dv", e_r2)])
            e_r4 = em.op("s", lambda h: h.activation(
                rows_f[0:1, 1, :], rows_f[0:1, 1, :], AF.Sqrt,
                bias=eps_sb[0:1, 0:1], scale=1.0), inc="ac",
                waits=[("dv", st["r3"][c]), ("dv", e_ms)])
            e_r5a = em.op("v", lambda h: h.reciprocal(
                rows_f[0:1, 0, :], rows_f[0:1, 1, :]), inc="dv",
                waits=[("ac", e_r4)])
            r5w = [("dv", e_r5a)]
            if c >= 1:
                r5w.append(("pe", st["br"][c - 1]))
            st["r5"][c] = em.op("v", lambda h: h.tensor_copy(
                rows_b[0:1, 1, :], rows_f[0:1, 0, :]), inc="dv", waits=r5w)

        def emit_bcasts(c):
            if c >= NCH:
                return
            st["bm"][c] = em.op("t", lambda h: h.matmul(
                psBc[:, 0, :], ones_row_bf[0:1, 0:128], rows_b[0:1, 0, :],
                start=True, stop=True, skip_group_check=True), inc="pe",
                waits=[("dv", st["r5"][c])])
            st["br"][c] = em.op("t", lambda h: h.matmul(
                psBc[:, 1, :], ones_row_bf[0:1, 0:128], rows_b[0:1, 1, :],
                start=True, stop=True, skip_group_check=True), inc="pe",
                waits=[("pe", st["bm"][c])])

        def emit_submul(c):
            if c >= NCH:
                return
            sw = [("pe", st["bm"][c])]
            if c >= 2:
                sw.append(("pe", e_qpj[c - 2]))
            st["sub"][c] = em.op("v", lambda h, c=c: h.tensor_tensor(
                xn_tb[:, c % 2, :, :], xbtb[:, :, :], bcast_mid(psBc[:, 0, :], NDT),
                ALU.subtract), inc="dv", waits=sw)
            st["mul"][c] = em.op("v", lambda h, c=c: h.tensor_tensor(
                xn_tb[:, c % 2, :, :], xn_tb[:, c % 2, :, :],
                bcast_mid(psBc[:, 1, :], NDT), ALU.mult), inc="dv",
                waits=[("pe", st["br"][c]), ("dv", st["sub"][c])])

        dma_x(0)
        dma_x(1)
        dma_x(2)
        emit_cast(0, extra=list(entry))
        emit_stats(0, extra=list(entry))
        emit_rows(0)
        emit_bcasts(0)
        emit_submul(0)
        emit_cast(1)
        emit_stats(1)
        emit_rows(1)

        e_z = em.op("v", lambda h: (h.memset(k16bd[:, :, :], 0.0),
                                    h.memset(vexpA[:, :], 0.0),
                                    h.memset(vexpB[:, :], 0.0))[-1],
                    inc="dv", waits=list(entry))
        bank1 = [list(entry), list(entry)]
        e_kv_ev = 0
        for which, wofs in (("k", D), ("v", 2 * D)):
            for p_ in range(8):
                bp = p_ % 2

                def kvmm(h, p_=p_, wofs=wofs, bp=bp):
                    ins = None
                    for dt in range(NDT):
                        ins = h.matmul(ps1[:, bp, 0:16],
                                       wa_sb[:, dt, wofs + p_ * 128:wofs + (p_ + 1) * 128],
                                       xn_tb[:, 0, dt, 0:16],
                                       start=(dt == 0), stop=(dt == NDT - 1))
                    return ins
                e_kv = em.op("t", kvmm, inc="pe",
                             waits=[("dv", st["mul"][0]), ("d_bw", e_wbkv)] +
                                   list(bank1[bp]))
                if which == "k":
                    def kev(h, p_=p_, bp=bp):
                        h.copy(k16bd[0:64, p_, 0:16], ps1[0:64, bp, 0:16])
                        return h.copy(k16bd[64:128, p_, 16:32], ps1[64:128, bp, 0:16])
                    e_kv_ev = em.op("s", kev, inc="ac",
                                    waits=[("pe", e_kv), ("dv", e_z)])
                else:
                    e_kv_ev = em.op("s", lambda h, p_=p_, bp=bp: h.copy(
                        v16p[:, p_, :], ps1[:, bp, 0:16]), inc="ac",
                        waits=[("pe", e_kv)])
                bank1[bp] = [("ac", e_kv_ev)]
        e_k16_ready = e_kv_ev
        bank2 = [list(entry), list(entry)]
        e_vt = 0
        for p_ in range(8):
            bp = p_ % 2
            e_vtr = em.op("t", lambda h, p_=p_, bp=bp: h.transpose(
                ps2[0:16, bp, 0:128], v16p[:, p_, 0:16], eye32_sb[:, :]),
                inc="pe", waits=[("ac", e_k16_ready), ("d_c0", e_consts)] +
                      list(bank2[bp]))
            e_vt = em.op("v", lambda h, p_=p_, bp=bp: h.tensor_copy(
                vT16[0:16, p_, :], ps2[0:16, bp, 0:128]), inc="dv",
                waits=[("pe", e_vtr)])
            bank2[bp] = [("dv", e_vt)]

        def scat(h):
            out = []
            for hh in range(16):
                dstv = vexpA if hh < 8 else vexpB
                prow = (hh % 8) * 16
                out.append(h.dma_start(
                    dstv[prow:prow + 16, hh * 64:(hh + 1) * 64],
                    vT16[0:16, hh >> 1, (hh & 1) * 64:((hh & 1) + 1) * 64]))
            return out
        e_vexp_ready = em.op("g", scat, inc="d_bs",
                             waits=[("dv", e_vt), ("dv", e_z)], ndma=16)

        for c in range(NCH):
            p2 = c % 2
            for pair in range(8):
                bp = pair % 2

                def qmm(h, pair=pair, bp=bp, p2=p2):
                    ins = None
                    for dt in range(NDT):
                        ins = h.matmul(ps1[:, bp, :],
                                       wa_sb[:, dt, pair * 128:(pair + 1) * 128],
                                       xn_tb[:, p2, dt, :],
                                       start=(dt == 0), stop=(dt == NDT - 1))
                    return ins
                qw = [("dv", st["mul"][c]), ("d_bw", e_wbq)]
                if pair < 2:
                    if c == 0:
                        qw += list(bank1[bp])
                    else:
                        qw.append(("dv", e_p2n[c - 1][bp]))
                else:
                    qw.append(("ac", e_qev[c][pair - 2]))
                e_q = em.op("t", qmm, inc="pe", waits=qw)
                e_qpj[c] = e_q
                evw = [("pe", e_q)]
                if c >= 1:
                    evw.append(("pe", e_lg[c - 1][pair // 4]))
                e_qev[c][pair] = em.op("s", lambda h, pair=pair, bp=bp: h.copy(
                    qT2[:, pair, :], ps1[:, bp, :]), inc="ac", waits=evw)

            e_ex = [0, 0]
            e_mask = [0, 0]
            for t2 in range(2):

                def lmm(h, t2=t2):
                    ins = None
                    for j in range(4):
                        pair = 4 * t2 + j
                        ins = h.matmul(ps1[32 * j:32 * j + 32, t2, :],
                                       k16bd[:, pair, :], qT2[:, pair, :],
                                       start=True, stop=True,
                                       tile_position=(0, 32 * j),
                                       skip_group_check=True)
                    return ins
                lgw = [("ac", e_qev[c][6 + t2]), ("ac", e_k16_ready)]
                e_lg[c][t2] = em.op("t", lmm, inc="pe", waits=lgw)
                exw = [("pe", e_lg[c][t2])]
                if c >= 1:
                    exw.append(("dv", e_p2n[c - 1][t2]))
                e_ex[t2] = em.op("s", lambda h, t2=t2: h.activation(
                    pB[:, t2, :], ps1[:, t2, :], AF.Exp, scale=0.125), inc="ac",
                    waits=exw)
                if c == 0:
                    e_mask[t2] = em.op("v", lambda h, t2=t2: h.tensor_tensor(
                        pB[:, t2, 0:16], pB[:, t2, 0:16], m01_sb[:, :], ALU.mult),
                        inc="dv", waits=[("ac", e_ex[t2]), ("d_c0", e_consts)])

            hw_ = ([("dv", e_mask[0]), ("dv", e_mask[1])] if c == 0 else
                   [("ac", e_ex[0]), ("ac", e_ex[1])])
            if c >= 1:
                hw_.append(("ac", e_aev_s[c - 1]))
                hw_.append(("dv", e_aev_v[c - 1]))
            else:
                hw_ += list(entry)

            def hsum(h):
                h.matmul(psOT[0:16, 0, :], hsA_sb[:, :], pB[:, 0, :],
                         start=True, stop=False, skip_group_check=True)
                return h.matmul(psOT[0:16, 0, :], hsB_sb[:, :], pB[:, 1, :],
                                start=False, stop=True, skip_group_check=True)
            e_hs[c] = em.op("t", hsum, inc="pe", waits=hw_)
            e_rcp[c] = em.op("v", lambda h: h.reciprocal(
                recT[0:16, :], psOT[0:16, 0, :]), inc="dv",
                waits=[("pe", e_hs[c])])
            e_rcb = em.op("v", lambda h: h.tensor_copy(
                recTb[0:16, :], recT[0:16, :]), inc="dv",
                waits=[("dv", e_rcp[c])])

            for t2 in range(2):
                e_brc = em.op("t", lambda h, t2=t2: h.matmul(
                    ps1[:, t2, :], (selA_sb if t2 == 0 else selB_sb)[:, :],
                    recTb[0:16, :], start=True, stop=True,
                    skip_group_check=True), inc="pe",
                    waits=[("dv", e_rcb), ("ac", e_ex[t2])])
                pw = [("pe", e_brc)]
                if c >= 1:
                    pw.append(("pe", e_pv[c - 1][3 + 4 * t2]))
                e_p2n[c][t2] = em.op("v", lambda h, t2=t2: h.tensor_tensor(
                    p2n[:, t2, :], pB[:, t2, :], ps1[:, t2, :], ALU.mult),
                    inc="dv", waits=pw)

            for dt in range(NDT):
                b = dt % 2
                t2 = dt // 4
                vx = vexpA if dt < 4 else vexpB
                pw = [("dv", e_p2n[c][t2]), ("d_bs", e_vexp_ready)]
                if dt == 0:
                    pw.append(("dv", e_rcp[c]))
                elif dt == 1:
                    if c >= 1:
                        pw.append(("dv", e_aev_v[c - 1]))
                elif dt % 2 == 0:
                    pw.append(("ac", e_aev_s[c]))
                else:
                    pw.append(("dv", e_aev_v[c]))
                e_pv[c][dt] = em.op("t", lambda h, dt=dt, b=b, t2=t2, vx=vx: h.matmul(
                    psOT[:, b, :], vx[:, dt * 128:(dt + 1) * 128], p2n[:, t2, :],
                    start=True, stop=True, skip_group_check=True), inc="pe",
                    waits=pw)
                aw = [("pe", e_pv[c][dt])]
                if c >= 1:
                    aw.append(("pe", e_pjl[c - 1]))
                else:
                    aw += list(entry)
                if dt % 2 == 0:
                    e_aev_s[c] = em.op("s", lambda h, dt=dt, b=b: h.copy(
                        aT[:, dt, :], psOT[:, b, :]), inc="ac", waits=aw)
                else:
                    e_aev_v[c] = em.op("v", lambda h, dt=dt, b=b: h.tensor_copy(
                        aT[:, dt, :], psOT[:, b, :]), inc="dv", waits=aw)

            for ot in range(NDT):
                bp = ot % 2

                def pjmm(h, ot=ot, bp=bp):
                    ins = None
                    for ct in range(NDT):
                        ins = h.matmul(ps2[:, bp, :],
                                       wp_sb[:, ct, ot * 128:(ot + 1) * 128],
                                       aT[:, ct, :], start=(ct == 0),
                                       stop=(ct == NDT - 1))
                    return ins
                pjw = [("ac", e_aev_s[c]), ("dv", e_aev_v[c]), ("d_bw", e_wbp)]
                if ot < 2:
                    if c == 0:
                        pjw += list(bank2[bp])
                    else:
                        pjw.append(("dv", e_res[c - 1][6 + bp]))
                else:
                    pjw.append(("dv", e_res[c][ot - 2]))
                e_pj = em.op("t", pjmm, inc="pe", waits=pjw)
                e_pjl[c] = e_pj
                e_res[c][ot] = em.op("v", lambda h, ot=ot, bp=bp, c=c: h.tensor_tensor(
                    xtb[:, c % 3, ot, :], ps2[:, bp, :], xtb[:, c % 3, ot, :],
                    ALU.add), inc="dv", waits=[("pe", e_pj), ("d_bx", e_xin[c])])
                if ot == 1:
                    emit_bcasts(c + 1)
                    emit_submul(c + 1)
                if ot == 3:
                    emit_cast(c + 2)
                if ot == 5:
                    emit_stats(c + 2)
                    emit_rows(c + 2)
            e_xout[c] = em.op("y", lambda h, c=c: h.dma_start(
                xout.rearrange("(a q) t -> q a t", q=128)[:, :, c * TC:(c + 1) * TC],
                xtb[:, c % 3, :, :]), inc="d_bo", waits=[("dv", e_res[c][7])])
            dma_x(c + 3, w0=[("d_bo", e_xout[c])])

        for g in reversed(gB):
            g.__exit__(None, None, None)
        for g in reversed([ps1_g, ps2_g, psOT_g, psBc_g]):
            g.__exit__(None, None, None)
        return em.all_counts()


    entry = [("d_c0", e_consts), ("dv", e_ms)]
    ph = 0
    for be in range(nbe):
        s1 = xr1[be if dbg_scratch else 0]
        s2 = xr2[be if dbg_scratch else 0]
        s3 = xr3[be if dbg_scratch else 0]
        if ph < nphase:
            entry = phase_A(be, entry)
            ph += 1
        if ph < nphase:
            entry = phase_F(be, 0, s1, s2, entry)
            ph += 1
        if ph < nphase:
            entry = phase_B(be, s2, s3, entry)
            ph += 1
        if ph < nphase:
            entry = phase_F(be, 1, s3, outT[be], entry)
            ph += 1

    finals = [(n, v) for n, v in em.cnt.items() if n.startswith("d_") and v > 0]
    def fin(h):
        for n, v in finals:
            h.wait_ge(em.sem[n], v)
    em.fns["y"].append(fin)

    em.finish()
    for g in reversed(guards):
        g.__exit__(None, None, None)
    return nc


def kernel(L0_attn_a_x, L0_attn_a_Wq, L0_attn_a_bq, L0_attn_a_Wk, L0_attn_a_bk,
           L0_attn_a_Wv, L0_attn_a_bv, L0_attn_b_W_attn, L0_attn_b_W_proj,
           L0_attn_b_W_rel, L0_ffn_W1, L0_ffn_b1, L0_ffn_W2, L0_ffn_b2, L0_ffn_scale):
    x = np.asarray(L0_attn_a_x, dtype=np.float32)
    B = x.shape[0]
    n_cores = 8
    xT = np.ascontiguousarray(np.transpose(x, (0, 2, 1)))

    def t32(a):
        wT = np.asarray(a, dtype=np.float32).T
        return np.ascontiguousarray(wT.reshape(NDT, 128, wT.shape[1]).transpose(1, 0, 2))

    w1 = np.asarray(L0_ffn_W1, np.float32)
    w2 = np.asarray(L0_ffn_W2, np.float32)
    common = {
        "wqT": t32(L0_attn_a_Wq), "wkT": t32(L0_attn_a_Wk), "wvT": t32(L0_attn_a_Wv),
        "waT": t32(L0_attn_b_W_attn), "wpT": t32(L0_attn_b_W_proj),
        "w1p": np.ascontiguousarray(
            w1.reshape(NDT, 512, NDT, 128).transpose(3, 0, 2, 1)),
        "w2p": np.ascontiguousarray(
            w2.reshape(NDT, 128, NJT, 128).transpose(3, 0, 2, 1)),
        "bqp": np.ascontiguousarray(np.asarray(L0_attn_a_bq, np.float32).reshape(NDT, 128).T),
        "bkp": np.ascontiguousarray(np.asarray(L0_attn_a_bk, np.float32).reshape(NDT, 128).T),
        "b1p": np.ascontiguousarray(np.asarray(L0_ffn_b1, np.float32).reshape(NJT, 128).T),
        "bv_r": np.asarray(L0_attn_a_bv, np.float32).reshape(1, D),
        "b2_r": np.asarray(L0_ffn_b2, np.float32).reshape(1, D),
        "scal": np.asarray(L0_ffn_scale, np.float32).reshape(1, 1),
    }

    nc = build_nc(nbe=B // n_cores)
    in_maps = []
    for c in range(n_cores):
        m = dict(common)
        m["xt"] = np.ascontiguousarray(xT[c * (B // n_cores):(c + 1) * (B // n_cores)])
        in_maps.append(m)
    res = run_bass_kernel_spmd(nc, in_maps, core_ids=list(range(n_cores)))
    parts = [np.transpose(res.results[c]["outT"], (0, 2, 1)) for c in range(n_cores)]
    return np.ascontiguousarray(np.concatenate(parts, axis=0)).astype(np.float32)


# version 13
# speedup vs baseline: 1.2088x; 1.2088x over previous
import numpy as np
import ml_dtypes
import concourse.bass as bass
import concourse.mybir as mybir
from concourse.bass_utils import run_bass_kernel_spmd

F32 = mybir.dt.float32
BF16 = mybir.dt.bfloat16
AF = mybir.ActivationFunctionType
ALU = mybir.AluOpType
AX = mybir.AxisListType

D = 1024
T = 4096
HID = 4096
NDT = D // 128
NJT = HID // 128
NBE = 2
NEG = -1.0e30
TC = 512
NCH = T // TC
NQT = T // 128


class Em:
    ENGS = ("t", "v", "s", "g", "y")

    def __init__(self, nc):
        self.nc = nc
        self.fns = {k: [] for k in self.ENGS}
        self.sem = {}
        self.cnt = {}
        self.hwm = {e: {} for e in self.ENGS}
        for name in ("pe", "dv", "ac", "go"):
            self.sem[name] = nc.alloc_semaphore(name="sem_" + name)
            self.cnt[name] = 0

    def _get_sem(self, name):
        if name not in self.sem:
            self.sem[name] = self.nc.alloc_semaphore(name="sem_" + name)
            self.cnt[name] = 0
        return self.sem[name]

    def op(self, eng, fn, inc=None, waits=(), ndma=1):
        hw = self.hwm[eng]
        wl = []
        for s, v in waits:
            if v and v > 0 and hw.get(s, 0) < v:
                wl.append((self._get_sem(s), v))
                hw[s] = v
        amt = 16 if (inc and inc.startswith("d_")) else 1
        sem = self._get_sem(inc) if inc else None

        def run(h, fn=fn, wl=wl, sem=sem, amt=amt):
            for s, v in wl:
                h.wait_ge(s, v)
            ins = fn(h)
            if sem is not None:
                if isinstance(ins, (list, tuple)):
                    for i_ in ins:
                        i_.then_inc(sem, amt)
                else:
                    ins.then_inc(sem, amt)

        self.fns[eng].append(run)
        if inc:
            self.cnt[inc] += amt * ndma
            return self.cnt[inc]
        return None

    def all_counts(self):
        return [(k, v) for k, v in self.cnt.items() if k != "go" and v > 0]

    def finish(self):
        with self.nc.Block() as block:
            @block.tensor
            def _(h):
                for fn in self.fns["t"]:
                    fn(h)

            @block.vector
            def _(h):
                for fn in self.fns["v"]:
                    fn(h)

            @block.scalar
            def _(h):
                for fn in self.fns["s"]:
                    fn(h)

            @block.gpsimd
            def _(h):
                for fn in self.fns["g"]:
                    fn(h)

            @block.sync
            def _(h):
                for fn in self.fns["y"]:
                    fn(h)


def apx(handle, offset, dims):
    return bass.AP(handle, offset, [list(d) for d in dims])


def bcast_mid(ps_ap, n):
    return bass.AP(ps_ap.tensor, ps_ap.offset,
                   [list(ps_ap.ap[0]), [0, n], list(ps_ap.ap[1])])


def bcast_inner(ap, n):
    return bass.AP(ap.tensor, ap.offset,
                   [list(ap.ap[0]), list(ap.ap[1]), [0, n]])


def build_nc(nbe=NBE, dbg_scratch=False, nphase=8 * NBE):
    nc = bass.Bass("TRN2", target_bir_lowering=False)

    xt = nc.dram_tensor("xt", [nbe, D, T], F32, kind="ExternalInput").ap()
    outT = nc.dram_tensor("outT", [nbe, D, T], F32, kind="ExternalOutput").ap()
    wqT = nc.dram_tensor("wqT", [128, NDT, D], F32, kind="ExternalInput").ap()
    wkT = nc.dram_tensor("wkT", [128, NDT, D], F32, kind="ExternalInput").ap()
    wvT = nc.dram_tensor("wvT", [128, NDT, D], F32, kind="ExternalInput").ap()
    waT = nc.dram_tensor("waT", [128, NDT, 3 * D], F32, kind="ExternalInput").ap()
    wpT = nc.dram_tensor("wpT", [128, NDT, D], F32, kind="ExternalInput").ap()
    w1p = nc.dram_tensor("w1p", [128, NDT, NDT, 512], F32, kind="ExternalInput").ap()
    w2p = nc.dram_tensor("w2p", [128, NDT, NJT, 128], F32, kind="ExternalInput").ap()
    bqp = nc.dram_tensor("bqp", [128, NDT], F32, kind="ExternalInput").ap()
    bkp = nc.dram_tensor("bkp", [128, NDT], F32, kind="ExternalInput").ap()
    b1p = nc.dram_tensor("b1p", [128, NJT], F32, kind="ExternalInput").ap()
    bv_r = nc.dram_tensor("bv_r", [1, D], F32, kind="ExternalInput").ap()
    b2_r = nc.dram_tensor("b2_r", [1, D], F32, kind="ExternalInput").ap()
    sc_d = nc.dram_tensor("scal", [1, 1], F32, kind="ExternalInput").ap()

    skind = "ExternalOutput" if dbg_scratch else "Internal"
    nsc = nbe if dbg_scratch else 1
    q_scr = nc.dram_tensor("q_scr", [D, T], BF16, kind="Internal").ap()
    v_scr = nc.dram_tensor("v_scr", [T, D], BF16, kind="Internal").ap()
    wqkvb_scr = nc.dram_tensor("wqkvb_scr", [128, 3, NDT, D], BF16, kind="Internal").ap()
    w1b_scr = nc.dram_tensor("w1b_scr", [128, NDT, NDT, 512], BF16, kind="Internal").ap()
    w2b_scr = nc.dram_tensor("w2b_scr", [128, NDT, NJT, 128], BF16, kind="Internal").ap()
    w2_eff = nc.dram_tensor("w2_eff", [128, NDT, NJT, 128], BF16, kind="Internal").ap()
    xr1 = nc.dram_tensor("xr1", [nsc, D, T], F32, kind=skind).ap()
    xr2 = nc.dram_tensor("xr2", [nsc, D, T], F32, kind=skind).ap()
    xr3 = nc.dram_tensor("xr3", [nsc, D, T], F32, kind=skind).ap()

    eye_d = nc.inline_tensor(np.eye(128, dtype=ml_dtypes.bfloat16), name="eye_bf").ap()
    eye32_d = nc.inline_tensor(np.eye(128, dtype=np.float32), name="eye_f32").ap()
    negeye_d = nc.inline_tensor((NEG * np.eye(128)).astype(ml_dtypes.bfloat16),
                                name="negeye_bf").ap()
    h0 = np.array([2 * (r // 32) + (r // 16) % 2 for r in range(128)])
    hsA_np = np.zeros((128, 16), dtype=ml_dtypes.bfloat16)
    hsB_np = np.zeros((128, 16), dtype=ml_dtypes.bfloat16)
    selA_np = np.zeros((16, 128), dtype=ml_dtypes.bfloat16)
    selB_np = np.zeros((16, 128), dtype=ml_dtypes.bfloat16)
    for r in range(128):
        hsA_np[r, h0[r]] = 1.0
        hsB_np[r, 8 + h0[r]] = 1.0
        selA_np[h0[r], r] = 1.0
        selB_np[8 + h0[r], r] = 1.0
    m01_np = np.zeros((128, 16), dtype=ml_dtypes.bfloat16)
    for r in range(128):
        for t in range(16):
            if r % 16 <= t:
                m01_np[r, t] = 1.0
    hsA_d = nc.inline_tensor(hsA_np, name="hsA").ap()
    hsB_d = nc.inline_tensor(hsB_np, name="hsB").ap()
    selA_d = nc.inline_tensor(selA_np, name="selA").ap()
    selB_d = nc.inline_tensor(selB_np, name="selB").ap()
    m01_d = nc.inline_tensor(m01_np, name="m01").ap()

    em = Em(nc)
    guards = []

    def sb(name, shape, dt):
        g = nc.sbuf_tensor(name, shape, dt)
        t = g.__enter__()
        guards.append(g)
        return t

    ones_col_bf = sb("ones_col_bf", [128, 1], BF16)
    ones_row_bf = sb("ones_row_bf", [1, TC], BF16)
    ones_row_f32 = sb("ones_row_f32", [1, 128], F32)
    eye_sb = sb("eye_sb", [128, 128], BF16)
    eye32_sb = sb("eye32_sb", [128, 128], F32)
    negeye_sb = sb("negeye_sb", [128, 128], BF16)
    hsA_sb = sb("hsA_sb", [128, 16], BF16)
    hsB_sb = sb("hsB_sb", [128, 16], BF16)
    selA_sb = sb("selA_sb", [16, 128], BF16)
    selB_sb = sb("selB_sb", [16, 128], BF16)
    m01_sb = sb("m01_sb", [128, 16], BF16)
    bq_sb = sb("bq_sb", [128, NDT], F32)
    bk_sb = sb("bk_sb", [128, NDT], F32)
    b1_sb = sb("b1_sb", [128, NJT], F32)
    b2f = sb("b2f", [1, D], BF16)
    sc4 = sb("sc4", [4, 1], F32)
    scv128 = sb("scv128", [128, 1], F32)
    eps_sb = sb("eps_sb", [1, 1], F32)

    def clr(h):
        for s_ in em.sem.values():
            h.sem_clear(s_)
        return h.memset(ones_col_bf[:, :], 1.0)
    em.op("g", clr, inc="go")
    for e in ("t", "v", "s", "y"):
        def wgo(h):
            h.wait_ge(em.sem["go"], 1)
        em.fns[e].append(wgo)

    e_ms = em.op("v", lambda h: (h.memset(ones_row_bf[:, :], 1.0),
                                 h.memset(ones_row_f32[:, :], 1.0),
                                 h.memset(eps_sb[:, :], 1e-5))[-1], inc="dv")

    def loads(h):
        return [
            h.dma_start(eye_sb[:, :], eye_d[:, :]),
            h.dma_start(eye32_sb[:, :], eye32_d[:, :]),
            h.dma_start(negeye_sb[:, :], negeye_d[:, :]),
            h.dma_start(hsA_sb[:, :], hsA_d[:, :]),
            h.dma_start(hsB_sb[:, :], hsB_d[:, :]),
            h.dma_start(selA_sb[:, :], selA_d[:, :]),
            h.dma_start(selB_sb[:, :], selB_d[:, :]),
            h.dma_start(m01_sb[:, :], m01_d[:, :]),
            h.dma_start(bq_sb[:, :], bqp[:, :]),
            h.dma_start(bk_sb[:, :], bkp[:, :]),
            h.dma_start(b1_sb[:, :], b1p[:, :]),
            h.dma_start(b2f[0:1, :], b2_r[:, :]),
            h.dma_start(sc4[:, :], bass.AP(sc_d.tensor, 0, [[0, 4], [1, 1]])),
            h.dma_start(scv128[:, :], bass.AP(sc_d.tensor, 0, [[0, 128], [1, 1]])),
        ]
    e_consts = em.op("g", loads, inc="d_c0", ndma=14)
    wcast = {"done": 0}

    def emit_wcast_slab(j):
        wcast["done"] = em.op("g", lambda h, j=j: [
            h.dma_start(w1b_scr[:, j, :, :], w1p[:, j, :, :]),
            h.dma_start(w2b_scr[:, j, :, :], w2p[:, j, :, :])], inc="d_c1", ndma=2)
    e_w2b_final = 16 * 16

    def emit_ln(xb, xsq, xn_out, ps_s1, ps_s2, ps_m, ps_r, rows_f32, rows_bf,
                waits_x, waits_reuse, xtile=None):
        inv = 1.0 / D
        if xtile is not None:
            e_cast = em.op("v", lambda h: h.tensor_copy(xb[:, :, :], xtile),
                           inc="dv", waits=list(waits_x) + list(waits_reuse))
            xb_w = [("dv", e_cast)]
        else:
            xb_w = list(waits_x)
        e_sq = em.op("s", lambda h: h.square(xsq[:, :, :], xb[:, :, :]),
                     inc="ac", waits=xb_w + list(waits_reuse))

        def stats(h):
            ins = None
            for dt in range(NDT):
                h.matmul(ps_s1, ones_col_bf[:, 0:1], xb[:, dt, :],
                         start=(dt == 0), stop=(dt == NDT - 1))
            for dt in range(NDT):
                ins = h.matmul(ps_s2, ones_col_bf[:, 0:1], xsq[:, dt, :],
                               start=(dt == 0), stop=(dt == NDT - 1))
            return ins
        e_st = em.op("t", stats, inc="pe",
                     waits=xb_w + [("ac", e_sq)] + list(waits_reuse))
        e_r1 = em.op("v", lambda h: h.tensor_scalar_mul(rows_bf[0:1, 0, :], ps_s1, inv),
                     inc="dv", waits=[("pe", e_st)] + list(waits_reuse))
        e_r2 = em.op("v", lambda h: h.tensor_mul(
            rows_f32[0:1, 0, :], rows_bf[0:1, 0, :], rows_bf[0:1, 0, :]), inc="dv",
            waits=[("dv", e_r1)])
        e_r3 = em.op("v", lambda h: h.scalar_tensor_tensor(
            rows_f32[0:1, 1, :], ps_s2, inv, rows_f32[0:1, 0, :], ALU.mult, ALU.subtract),
            inc="dv", waits=[("dv", e_r2)])
        e_r4 = em.op("s", lambda h: h.activation(
            rows_f32[0:1, 1, :], rows_f32[0:1, 1, :], AF.Sqrt, bias=eps_sb[0:1, 0:1],
            scale=1.0), inc="ac", waits=[("dv", e_r3), ("dv", e_ms)])
        e_r5a = em.op("v", lambda h: h.reciprocal(rows_f32[0:1, 0, :], rows_f32[0:1, 1, :]),
                      inc="dv", waits=[("ac", e_r4)])
        e_r5 = em.op("v", lambda h: h.tensor_copy(rows_bf[0:1, 1, :], rows_f32[0:1, 0, :]),
                     inc="dv", waits=[("dv", e_r5a)])
        if ps_r is not None:
            def bcast(h):
                h.matmul(ps_m, ones_row_bf[0:1, 0:128], rows_bf[0:1, 0, :],
                         start=True, stop=True)
                return h.matmul(ps_r, ones_row_bf[0:1, 0:128], rows_bf[0:1, 1, :],
                                start=True, stop=True)
            e_bc = em.op("t", bcast, inc="pe", waits=[("dv", e_r5)] + list(waits_reuse))
            e_n1 = em.op("v", lambda h: h.tensor_tensor(
                xn_out[:, :, :], xb[:, :, :], bcast_mid(ps_m, NDT), ALU.subtract),
                inc="dv", waits=[("pe", e_bc)] + list(waits_reuse))
            e_n2 = em.op("v", lambda h: h.tensor_tensor(
                xn_out[:, :, :], xn_out[:, :, :], bcast_mid(ps_r, NDT), ALU.mult),
                inc="dv", waits=[("dv", e_n1)])
        else:
            e_b1_ = em.op("t", lambda h: h.matmul(
                ps_m, ones_row_bf[0:1, 0:128], rows_bf[0:1, 0, :], start=True, stop=True),
                inc="pe", waits=[("dv", e_r5)] + list(waits_reuse))
            e_n1 = em.op("v", lambda h: h.tensor_tensor(
                xn_out[:, :, :], xb[:, :, :], bcast_mid(ps_m, NDT), ALU.subtract),
                inc="dv", waits=[("pe", e_b1_)] + list(waits_reuse))
            e_b2_ = em.op("t", lambda h: h.matmul(
                ps_m, ones_row_bf[0:1, 0:128], rows_bf[0:1, 1, :], start=True, stop=True),
                inc="pe", waits=[("dv", e_n1)])
            e_n2 = em.op("v", lambda h: h.tensor_tensor(
                xn_out[:, :, :], xn_out[:, :, :], bcast_mid(ps_m, NDT), ALU.mult),
                inc="dv", waits=[("pe", e_b2_)])
        return {"xn": e_n2, "stats": e_st, "rows": e_r5}

    def phase_A(be, entry):
        gA, gA1 = [], []

        def mk(name, shape, dt, glist):
            g = nc.sbuf_tensor(f"{name}_{be}", shape, dt)
            t_ = g.__enter__()
            glist.append(g)
            return t_

        kT_sb = mk("kT", [128, NDT, T], BF16, gA)

        wq_sb = mk("wq", [128, NDT, D], BF16, gA1)
        wk_sb = mk("wk", [128, NDT, D], BF16, gA1)
        wv_sb = mk("wv", [128, NDT, D], BF16, gA1)
        bvbc = mk("bvbc", [128, D], F32, gA1)
        bv_row = mk("bv_row", [1, D], F32, gA1)
        xb_c = mk("xb", [128, 2, NDT, TC], BF16, gA1)
        xsq_c = mk("xsq", [128, NDT, TC], BF16, gA1)
        xn_c = mk("xn", [128, 2, NDT, TC], BF16, gA1)
        qo_c = mk("qo", [128, 2, NDT, TC], BF16, gA1)
        vo_c = mk("vo", [128, 2, 4, D], BF16, gA1)
        rows_f = mk("rowsf", [1, 2, TC], F32, gA1)
        rows_b = mk("rowsb", [1, 2, TC], BF16, gA1)

        psA_g = nc.psum_tensor(f"psA_{be}", [128, 4, 512], F32)
        psSt_g = nc.psum_tensor(f"psSt_{be}", [1, 2, TC], F32)
        psBc_g = nc.psum_tensor(f"psBc_{be}", [128, 2, TC], F32)
        psA, psSt, psBc = psA_g.__enter__(), psSt_g.__enter__(), psBc_g.__enter__()

        if be == 0:
            e_w = em.op("g", lambda h: [
                h.dma_start(bv_row[0:1, :], bv_r[:, :]),
                h.dma_start(wq_sb[:, :, :], wqT[:, :, :]),
                h.dma_start(wk_sb[:, :, :], wkT[:, :, :]),
                h.dma_start(wv_sb[:, :, :], wvT[:, :, :])],
                inc="d_aw", waits=list(entry) + [("d_c0", e_consts)], ndma=4)
        else:
            e_w = em.op("y", lambda h: [
                h.dma_start(bv_row[0:1, :], bv_r[:, :]),
                h.dma_start(wq_sb[:, :, :], wqkvb_scr[:, 0, :, :]),
                h.dma_start(wk_sb[:, :, :], wqkvb_scr[:, 1, :, :]),
                h.dma_start(wv_sb[:, :, :], wqkvb_scr[:, 2, :, :])],
                inc="d_aw", waits=list(entry) + [("d_awb", 16 * 3)], ndma=4)

        e_bv = 0
        for cc in range(2):
            e_pe = em.op("t", lambda h, cc=cc: h.matmul(
                psBc[:, 0, :], ones_row_f32[0:1, :], bv_row[0:1, cc * 512:(cc + 1) * 512],
                start=True, stop=True), inc="pe",
                waits=[("d_aw", e_w), ("dv", e_bv), ("dv", e_ms)] + list(entry))
            e_bv = em.op("v", lambda h, cc=cc: h.tensor_copy(
                bvbc[:, cc * 512:(cc + 1) * 512], psBc[:, 0, :]), inc="dv",
                waits=[("pe", e_pe)])

        e_xdma = [0] * NCH
        e_st_a = [0] * NCH
        e_sq_a = [0] * NCH
        e_r3_a = [0] * NCH
        e_r5_a = [0] * NCH
        e_bc_a = [0] * NCH
        e_xn_a = [0] * NCH
        e_qkv_last = [0] * NCH
        e_qdma = [0] * NCH
        e_vdma = [0] * NCH
        bank_hist = []
        inv = 1.0 / D

        def a1_xdma(c):
            if c >= NCH:
                return
            p = c % 2
            xw = list(entry) if c < 2 else [("pe", e_st_a[c - 2]),
                                            ("dv", e_xn_a[c - 2])]
            e_xdma[c] = em.op("g", lambda h, c=c, p=p: h.dma_start(
                xb_c[:, p, :, :],
                xt[be].rearrange("(a q) t -> q a t", q=128)[:, :, c * TC:(c + 1) * TC]),
                inc="d_ax%d" % p, waits=xw)
            if be == 0:
                emit_wcast_slab(c)

        def a1_sq(c):
            if c >= NCH:
                return
            p = c % 2
            sqw = [("d_ax%d" % p, e_xdma[c])]
            if c >= 1:
                sqw.append(("pe", e_st_a[c - 1]))
            e_sq_a[c] = em.op("s", lambda h, p=p: h.square(
                xsq_c[:, :, :], xb_c[:, p, :, :]), inc="ac", waits=sqw)

        def a1_strows(c):
            if c >= NCH:
                return
            p = c % 2
            stw = [("d_ax%d" % p, e_xdma[c]), ("ac", e_sq_a[c])]
            if c >= 1:
                stw.append(("dv", e_r3_a[c - 1]))

            def stats(h, p=p):
                ins = None
                for dt in range(NDT):
                    h.matmul(psSt[0:1, 0, :], ones_col_bf[:, 0:1], xb_c[:, p, dt, :],
                             start=(dt == 0), stop=(dt == NDT - 1),
                             skip_group_check=True)
                    ins = h.matmul(psSt[0:1, 1, :], ones_col_bf[:, 0:1],
                                   xsq_c[:, dt, :], start=(dt == 0),
                                   stop=(dt == NDT - 1), skip_group_check=True)
                return ins
            e_st_a[c] = em.op("t", stats, inc="pe", waits=stw)
            rw = [("pe", e_st_a[c])]
            if c >= 1:
                rw.append(("pe", e_bc_a[c - 1]))
            e_r1 = em.op("v", lambda h: h.tensor_scalar_mul(
                rows_b[0:1, 0, :], psSt[0:1, 0, :], inv), inc="dv", waits=rw)
            e_r2 = em.op("v", lambda h: h.tensor_mul(
                rows_f[0:1, 0, :], rows_b[0:1, 0, :], rows_b[0:1, 0, :]),
                inc="dv", waits=[("dv", e_r1)])
            e_r3_a[c] = em.op("v", lambda h: h.scalar_tensor_tensor(
                rows_f[0:1, 1, :], psSt[0:1, 1, :], inv, rows_f[0:1, 0, :],
                ALU.mult, ALU.subtract), inc="dv", waits=[("dv", e_r2)])
            e_r4 = em.op("s", lambda h: h.activation(
                rows_f[0:1, 1, :], rows_f[0:1, 1, :], AF.Sqrt,
                bias=eps_sb[0:1, 0:1], scale=1.0), inc="ac",
                waits=[("dv", e_r3_a[c]), ("dv", e_ms)])
            e_r5a = em.op("v", lambda h: h.reciprocal(
                rows_f[0:1, 0, :], rows_f[0:1, 1, :]), inc="dv",
                waits=[("ac", e_r4)])
            e_r5_a[c] = em.op("v", lambda h: h.tensor_copy(
                rows_b[0:1, 1, :], rows_f[0:1, 0, :]), inc="dv",
                waits=[("dv", e_r5a)])

        def a1_bcast(c):
            if c >= NCH:
                return
            bw = [("dv", e_r5_a[c])]
            bw.append(("dv", e_bv) if c == 0 else ("dv", e_xn_a[c - 1]))

            def bc(h):
                h.matmul(psBc[:, 0, :], ones_row_bf[0:1, 0:128], rows_b[0:1, 0, :],
                         start=True, stop=True, skip_group_check=True)
                return h.matmul(psBc[:, 1, :], ones_row_bf[0:1, 0:128],
                                rows_b[0:1, 1, :], start=True, stop=True,
                                skip_group_check=True)
            e_bc_a[c] = em.op("t", bc, inc="pe", waits=bw)

        def a1_norm(c):
            if c >= NCH:
                return
            p = c % 2
            nw = [("pe", e_bc_a[c])]
            if c >= 2:
                nw.append(("pe", e_qkv_last[c - 2]))
            e_n1 = em.op("v", lambda h, p=p: h.tensor_tensor(
                xn_c[:, p, :, :], xb_c[:, p, :, :], bcast_mid(psBc[:, 0, :], NDT),
                ALU.subtract), inc="dv", waits=nw)
            e_xn_a[c] = em.op("v", lambda h, p=p: h.tensor_tensor(
                xn_c[:, p, :, :], xn_c[:, p, :, :], bcast_mid(psBc[:, 1, :], NDT),
                ALU.mult), inc="dv", waits=[("dv", e_n1)])

        a1_xdma(0)
        a1_xdma(1)
        a1_sq(0)
        a1_strows(0)
        a1_bcast(0)
        a1_norm(0)

        for c in range(NCH):
            p = c % 2
            base_w = [("d_aw", e_w), ("dv", e_xn_a[c])]
            last_pe = 0

            def grp_waits():
                wts = list(base_w)
                gi = len(bank_hist)
                if gi >= 4:
                    wts.append(bank_hist[gi - 4])
                return wts, gi % 4

            for ot in range(NDT):
                wts, bank = grp_waits()
                def qmm(h, ot=ot, bank=bank, p=p):
                    ins = None
                    for dt in range(NDT):
                        ins = h.matmul(psA[:, bank, :], wq_sb[:, dt, ot * 128:(ot + 1) * 128],
                                       xn_c[:, p, dt, :], start=(dt == 0), stop=(dt == NDT - 1))
                    return ins
                e_pe = em.op("t", qmm, inc="pe", waits=wts)
                last_pe = e_pe
                evw = [("pe", e_pe)]
                if c >= 2:
                    evw.append(("d_aq%d" % p, e_qdma[c - 2]))
                e_ev = em.op("s", lambda h, ot=ot, bank=bank, p=p: h.activation(
                    qo_c[:, p, ot, :], psA[:, bank, :], AF.Identity,
                    bias=bq_sb[:, ot:ot + 1], scale=1.0), inc="ac", waits=evw)
                bank_hist.append(("ac", e_ev))
            e_qdma[c] = em.op("y", lambda h, c=c, p=p: h.dma_start(
                q_scr.rearrange("(a q) t -> q a t", q=128)[:, :, c * TC:(c + 1) * TC],
                qo_c[:, p, :, :]), inc="d_aq%d" % p, waits=[bank_hist[-1]])

            a1_xdma(c + 2)
            a1_sq(c + 1)

            for ot in range(NDT):
                wts, bank = grp_waits()
                def kmm(h, ot=ot, bank=bank, p=p):
                    ins = None
                    for dt in range(NDT):
                        ins = h.matmul(psA[:, bank, :], wk_sb[:, dt, ot * 128:(ot + 1) * 128],
                                       xn_c[:, p, dt, :], start=(dt == 0), stop=(dt == NDT - 1))
                    return ins
                e_pe = em.op("t", kmm, inc="pe", waits=wts)
                last_pe = e_pe
                e_ev = em.op("s", lambda h, ot=ot, bank=bank, c=c: h.activation(
                    kT_sb[:, ot, c * TC:(c + 1) * TC], psA[:, bank, :], AF.Identity,
                    bias=bk_sb[:, ot:ot + 1], scale=1.0), inc="ac", waits=[("pe", e_pe)])
                bank_hist.append(("ac", e_ev))

            a1_strows(c + 1)

            for mt in range(4):
                if mt == 2:
                    a1_bcast(c + 1)
                if mt == 3:
                    a1_norm(c + 1)
                for cc in range(2):
                    wts, bank = grp_waits()
                    def vmm(h, mt=mt, cc=cc, bank=bank, p=p):
                        ins = None
                        for dt in range(NDT):
                            ins = h.matmul(psA[:, bank, :],
                                           xn_c[:, p, dt, mt * 128:(mt + 1) * 128],
                                           wv_sb[:, dt, cc * 512:(cc + 1) * 512],
                                           start=(dt == 0), stop=(dt == NDT - 1))
                        return ins
                    e_pe = em.op("t", vmm, inc="pe", waits=wts)
                    last_pe = e_pe
                    evw = [("pe", e_pe), ("dv", e_bv)]
                    if c >= 2:
                        evw.append(("d_av%d" % p, e_vdma[c - 2]))
                    e_ev = em.op("v", lambda h, mt=mt, cc=cc, bank=bank, p=p: h.tensor_tensor(
                        vo_c[:, p, mt, cc * 512:(cc + 1) * 512], psA[:, bank, :],
                        bvbc[:, cc * 512:(cc + 1) * 512], ALU.add), inc="dv",
                        waits=evw)
                    bank_hist.append(("dv", e_ev))
            e_vdma[c] = em.op("y", lambda h, c=c, p=p: h.dma_start(
                v_scr.rearrange("(a q) o -> q a o", q=128)[:, c * 4:(c + 1) * 4, :],
                vo_c[:, p, :, :]), inc="d_av%d" % p, waits=[bank_hist[-1]])
            e_qkv_last[c] = last_pe

        if be == 0:
            em.op("s", lambda h: [
                h.dma_start(wqkvb_scr[:, 0, :, :], wq_sb[:, :, :]),
                h.dma_start(wqkvb_scr[:, 1, :, :], wk_sb[:, :, :]),
                h.dma_start(wqkvb_scr[:, 2, :, :], wv_sb[:, :, :])],
                inc="d_awb", waits=[("pe", e_qkv_last[NCH - 1])], ndma=3)

        a1_tail = em.all_counts()
        for g in reversed([psA_g, psSt_g, psBc_g]):
            g.__exit__(None, None, None)
        for g in reversed(gA1):
            g.__exit__(None, None, None)

        gA2 = []
        v_sb = mk("v", [128, T // 128, D], BF16, gA2)
        qt_sb = mk("qt", [128, 2, NDT, 128], BF16, gA2)
        xc_sb = mk("xc", [128, 2, NDT, 128], F32, gA2)
        pT = mk("pT", [128, 2, 512], BF16, gA2)
        ob = mk("ob", [128, 2, D], BF16, gA2)
        x1o = mk("x1o", [128, 2, NDT, 128], F32, gA2)
        rtot = mk("rtot", [128, 2], F32, gA2)

        psS_g = nc.psum_tensor(f"psS_{be}", [128, 2, 512], F32)
        psO_g = nc.psum_tensor(f"psO_{be}", [128, 1024], F32)
        psT_g = nc.psum_tensor(f"psT_{be}", [128, 2, 1024], BF16)
        psD_g = nc.psum_tensor(f"psD_{be}", [128, 8], F32)
        psS, psO, psT, psD = (psS_g.__enter__(), psO_g.__enter__(),
                              psT_g.__enter__(), psD_g.__enter__())

        NG = 8
        e_vload = [0] * NG
        e_qt = [0] * (NQT + 2)
        e_xc = [0] * (NQT + 2)
        e_fin = [0] * (NQT + 1)
        e_rcp = [0] * (NQT + 1)
        e_add = [0] * NQT
        e_x1dma = [0] * NQT
        e_slast = [0] * NQT
        bankS = [list(a1_tail), list(a1_tail)]
        bankP = [list(a1_tail), list(a1_tail)]
        bankT = [list(a1_tail), list(a1_tail)]
        g_ob = [[], []]

        def dma_qt(i):
            if i >= NQT:
                return
            pi = i % 2
            wq_ = list(a1_tail) if i < 2 else [("pe", e_slast[i - 2])]
            e_qt[i] = em.op("y", lambda h, i=i, pi=pi: h.dma_start(
                qt_sb[:, pi, :, :],
                q_scr.rearrange("(a q) t -> q a t", q=128)[:, :, i * 128:(i + 1) * 128]),
                inc="d_qt%d" % pi, waits=wq_)

        def dma_xc(i):
            if i >= NQT:
                return
            pi = i % 2
            wx_ = list(a1_tail) if i < 2 else [("dv", e_add[i - 2])]
            e_xc[i] = em.op("y", lambda h, i=i, pi=pi: h.dma_start(
                xc_sb[:, pi, :, :],
                xt[be].rearrange("(a q) t -> q a t", q=128)[:, :, i * 128:(i + 1) * 128]),
                inc="d_xc%d" % pi, waits=wx_)

        def emit_sgroup(ti, g):
            p = ti % 2
            b = g % 2
            def smm(h, ti=ti, g=g, b=b, p=p):
                ins = None
                for kk in range(4):
                    kt = 4 * g + kk
                    dodiag = (kt == ti)
                    for ot in range(NDT):
                        ins = h.matmul(
                            psS[:, b, kk * 128:(kk + 1) * 128],
                            kT_sb[:, ot, kt * 128:(kt + 1) * 128],
                            qt_sb[:, p, ot, :],
                            start=(ot == 0), stop=(ot == NDT - 1 and not dodiag))
                    if dodiag:
                        ins = h.matmul(psS[:, b, kk * 128:(kk + 1) * 128],
                                       negeye_sb[:, :], eye_sb[:, :],
                                       start=False, stop=True)
                return ins
            e_s = em.op("t", smm, inc="pe",
                        waits=[("d_qt%d" % p, e_qt[ti])] + list(bankS[b]))
            e_slast[ti] = e_s
            e_ex = em.op("s", lambda h, b=b: h.activation(
                pT[:, b, :], psS[:, b, :], AF.Exp, scale=1.0 / D), inc="ac",
                waits=[("pe", e_s)] + list(bankP[b]))
            bankS[b] = [("ac", e_ex)]
            return e_ex

        def emit_pvgroup(ti, g, e_ex):
            b = g % 2
            def pvg(h, g=g, b=b):
                ins = None
                for kk in range(4):
                    kt = 4 * g + kk
                    lhs = pT[:, b, kk * 128:(kk + 1) * 128]
                    for cc in range(2):
                        h.matmul(psO[:, cc * 512:(cc + 1) * 512], lhs,
                                 v_sb[:, kt, cc * 512:(cc + 1) * 512],
                                 start=(kt == 0), stop=(kt == 31),
                                 skip_group_check=True)
                    ins = h.matmul(psD[:, 0:1], lhs, ones_col_bf[:, 0:1],
                                   start=(kt == 0), stop=(kt == 31),
                                   skip_group_check=True)
                return ins
            pvw = [("ac", e_ex), ("d_vl", e_vload[g])]
            if g == 0:
                pvw += [("dv", e_fin[ti - 1] if ti > 0 else 0),
                        ("dv", e_rcp[ti - 1] if ti > 0 else 0)]
            e_pv = em.op("t", pvg, inc="pe", waits=pvw)
            bankP[b] = [("pe", e_pv)]
            return e_pv

        def emit_otrans(ti):
            p = ti % 2
            last_otr = 0
            for j in range(NDT):
                bp = j % 2
                e_otr = em.op("t", lambda h, j=j, bp=bp, p=p: h.transpose(
                    psT[:, bp, 0:128], ob[:, p, j * 128:(j + 1) * 128], eye_sb[:, :]),
                    inc="pe", waits=[("dv", e_fin[ti])] + list(bankT[bp]))
                last_otr = e_otr
                aw = [("pe", e_otr), ("d_xc%d" % p, e_xc[ti])]
                if ti >= 2:
                    aw.append(("d_x1%d" % p, e_x1dma[ti - 2]))
                last_add = em.op("v", lambda h, j=j, bp=bp, p=p: h.tensor_tensor(
                    x1o[:, p, j, :], psT[:, bp, 0:128], xc_sb[:, p, j, :], ALU.add),
                    inc="dv", waits=aw)
                bankT[bp] = [("dv", last_add)]
            g_ob[p] = [("pe", last_otr)]
            e_add[ti] = last_add
            dst = xr1[be if dbg_scratch else 0]
            e_x1dma[ti] = em.op("y", lambda h, ti=ti, p=p, dst=dst: h.dma_start(
                dst.rearrange("(a q) t -> q a t", q=128)[:, :, ti * 128:(ti + 1) * 128],
                x1o[:, p, :, :]), inc="d_x1%d" % p, waits=[("dv", e_add[ti])])

        dma_qt(0)
        dma_qt(1)
        dma_xc(0)
        dma_xc(1)
        for k in range(NG):
            e_vload[k] = em.op("g", lambda h, k=k: h.dma_start(
                v_sb[:, k * 4:(k + 1) * 4, :],
                v_scr.rearrange("(a q) o -> q a o", q=128)[:, k * 4:(k + 1) * 4, :]),
                inc="d_vl", waits=list(a1_tail))
        for ti in range(NQT):
            p = ti % 2
            ex = [0] * NG
            ex[0] = emit_sgroup(ti, 0)
            ex[1] = emit_sgroup(ti, 1)
            if ti > 0:
                emit_otrans(ti - 1)
                dma_xc(ti + 1)
            e_pvlast = 0
            for g in range(NG):
                e_pvlast = emit_pvgroup(ti, g, ex[g])
                if g + 2 < NG:
                    ex[g + 2] = emit_sgroup(ti, g + 2)
            e_rcp[ti] = em.op("v", lambda h, p=p: h.reciprocal(
                rtot[:, p:p + 1], psD[:, 0:1]), inc="dv",
                waits=[("pe", e_pvlast)])
            e_fin[ti] = em.op("v", lambda h, p=p: h.tensor_scalar_mul(
                ob[:, p, :], psO[:, :], rtot[:, p:p + 1]), inc="dv",
                waits=[("dv", e_rcp[ti])] + list(g_ob[p]))
            dma_qt(ti + 2)
        emit_otrans(NQT - 1)

        for g in reversed(gA2):
            g.__exit__(None, None, None)
        for g in reversed([psS_g, psO_g, psT_g, psD_g]):
            g.__exit__(None, None, None)
        for g in reversed(gA):
            g.__exit__(None, None, None)
        return em.all_counts()

    def phase_F(be, app, xin, xout, entry):
        gF = []

        def mk(name, shape, dt):
            g = nc.sbuf_tensor(f"{name}_{be}_{app}", shape, dt)
            t_ = g.__enter__()
            gF.append(g)
            return t_

        w1r = mk("w1r", [128, NDT, NDT, 512], BF16)
        g_sb = mk("gF", [128, NJT, TC], BF16)
        xtb = mk("xtbF", [128, 2, NDT, TC], F32)
        xn = mk("xnF", [128, NDT, TC], BF16)
        xbs = mk("xbsF", [128, 2, TC], BF16)
        xqs = mk("xqsF", [128, 2, TC], BF16)
        w2blk = mk("w2blk", [128, 2, NJT, 128], BF16)
        w2bld = mk("w2bld", [128, 2, NJT, 128], BF16)
        ltr = mk("ltrF", [128, 2, 2, D], BF16)
        lt_e = mk("lteF", [8, NDT, 128], BF16)
        lt_o = mk("ltoF", [8, NDT, 128], BF16)
        xn_sc = mk("xnscF", [128, NDT, 8], BF16)
        rows_f = mk("rowsfF", [1, 2, TC], F32)
        rows_b = mk("rowsbF", [1, 2, TC], BF16)

        psh_g = nc.psum_tensor(f"psh_{be}_{app}", [128, 2, 512], F32)
        pso_g = nc.psum_tensor(f"pso_{be}_{app}", [128, 2, 512], F32)
        psW_g = nc.psum_tensor(f"psW_{be}_{app}", [128, 2, 512], F32)
        psbc_g = nc.psum_tensor(f"psbc_{be}_{app}", [128, 512], F32)
        psLt_g = nc.psum_tensor(f"psLt_{be}_{app}", [128, 2, 512], BF16)
        psh, pso, psW, psbc, psLt = (psh_g.__enter__(), pso_g.__enter__(),
                                     psW_g.__enter__(), psbc_g.__enter__(),
                                     psLt_g.__enter__())

        def flat2(sl, n):
            return bass.AP(sl.tensor, sl.offset, [list(sl.ap[0]), [1, n]])

        e_xin = [0] * NCH
        e_xout = [0] * NCH

        e_w1b = [0] * NDT
        for jc in range(NDT):
            e_w1b[jc] = em.op("g", lambda h, jc=jc: h.dma_start(
                w1r[:, jc, :, :], w1b_scr[:, jc, :, :]), inc="d_w1b",
                waits=list(entry) + [("d_c1", e_w2b_final)])

        def dma_x(c):
            if c >= NCH:
                return
            p = c % 2
            xw = list(entry) if c < 2 else [("d_fo%d" % p, e_xout[c - 2])]
            e_xin[c] = em.op("y", lambda h, c=c, p=p: h.dma_start(
                xtb[:, p, :, :],
                xin.rearrange("(a q) t -> q a t", q=128)[:, :, c * TC:(c + 1) * TC]),
                inc="d_fx%d" % p, waits=xw)

        st = {"r3": 0, "r5": 0, "bm": 0, "br": 0, "mul": 0, "sub": 0,
              "stats": 0, "mm_prev": [0] * NDT}

        def emit_stats(c, psw_free):
            p = c % 2
            e_mm = [0] * NDT
            for dt in range(NDT):
                s = dt % 2
                cw = [("d_fx%d" % p, e_xin[c])]
                if dt >= 2:
                    cw.append(("pe", e_mm[dt - 2]))
                elif st["mm_prev"][NDT - 2 + dt]:
                    cw.append(("pe", st["mm_prev"][NDT - 2 + dt]))
                e_cast = em.op("v", lambda h, dt=dt, s=s, p=p: h.tensor_copy(
                    xbs[:, s, :], xtb[:, p, dt, :]), inc="dv", waits=cw)
                e_sq = em.op("s", lambda h, s=s: h.square(
                    xqs[:, s, :], xbs[:, s, :]), inc="ac",
                    waits=[("dv", e_cast)] + cw[1:])

                def stmm(h, dt=dt, s=s):
                    h.matmul(psW[0:1, 0, :], ones_col_bf[:, 0:1], xbs[:, s, :],
                             start=(dt == 0), stop=(dt == NDT - 1),
                             skip_group_check=True)
                    return h.matmul(psW[0:1, 1, :], ones_col_bf[:, 0:1],
                                    xqs[:, s, :], start=(dt == 0),
                                    stop=(dt == NDT - 1), skip_group_check=True)
                e_mm[dt] = em.op("t", stmm, inc="pe",
                                 waits=[("dv", e_cast), ("ac", e_sq)] +
                                       (psw_free if dt == 0 else []))
            st["mm_prev"] = e_mm
            st["stats"] = e_mm[NDT - 1]

        def emit_rows(c):
            inv = 1.0 / D
            rw = [("pe", st["stats"])]
            if st["br"]:
                rw.append(("pe", st["br"]))
            e_r1 = em.op("v", lambda h: h.tensor_scalar_mul(
                rows_b[0:1, 0, :], psW[0:1, 0, :], inv), inc="dv", waits=rw)
            e_r2 = em.op("v", lambda h: h.tensor_mul(
                rows_f[0:1, 0, :], rows_b[0:1, 0, :], rows_b[0:1, 0, :]),
                inc="dv", waits=[("dv", e_r1)])
            e_r3 = em.op("v", lambda h: h.scalar_tensor_tensor(
                rows_f[0:1, 1, :], psW[0:1, 1, :], inv, rows_f[0:1, 0, :],
                ALU.mult, ALU.subtract), inc="dv", waits=[("dv", e_r2)])
            e_r4 = em.op("s", lambda h: h.activation(
                rows_f[0:1, 1, :], rows_f[0:1, 1, :], AF.Sqrt,
                bias=eps_sb[0:1, 0:1], scale=1.0), inc="ac",
                waits=[("dv", e_r3), ("dv", e_ms)])
            e_r5a = em.op("v", lambda h: h.reciprocal(
                rows_f[0:1, 0, :], rows_f[0:1, 1, :]), inc="dv",
                waits=[("ac", e_r4)])
            e_r5 = em.op("v", lambda h: h.tensor_copy(
                rows_b[0:1, 1, :], rows_f[0:1, 0, :]), inc="dv",
                waits=[("dv", e_r5a)])
            st["r3"] = e_r3
            st["r5"] = e_r5

        def emit_bcast_m(c):
            bw = [("dv", st["r5"])]
            if st["mul"]:
                bw.append(("dv", st["mul"]))
            e_bm = em.op("t", lambda h: h.matmul(
                psbc[:, :], ones_row_bf[0:1, 0:128], rows_b[0:1, 0, :],
                start=True, stop=True), inc="pe", waits=bw)
            p = c % 2
            e_sub = em.op("v", lambda h, p=p: h.tensor_tensor(
                xn[:, :, :], xtb[:, p, :, :], bcast_mid(psbc[:, :], NDT),
                ALU.subtract), inc="dv", waits=[("pe", e_bm)])
            st["bm"] = e_bm
            st["sub"] = e_sub

        def emit_bcast_r(c):
            e_br = em.op("t", lambda h: h.matmul(
                psbc[:, :], ones_row_bf[0:1, 0:128], rows_b[0:1, 1, :],
                start=True, stop=True), inc="pe", waits=[("dv", st["sub"])])
            e_mul = em.op("v", lambda h: h.tensor_tensor(
                xn[:, :, :], xn[:, :, :], bcast_mid(psbc[:, :], NDT),
                ALU.mult), inc="dv", waits=[("pe", e_br)])
            st["br"] = e_br
            st["mul"] = e_mul

        dma_x(0)
        dma_x(1)
        emit_stats(0, list(entry))
        emit_rows(0)
        emit_bcast_m(0)
        emit_bcast_r(0)
        e_mul0 = st["mul"]

        e_sc = [0] * NDT
        for dt in range(NDT):
            sl = xn[:, dt, 0:TC]
            odd_ap = bass.AP(sl.tensor, sl.offset + 2, [list(sl.ap[0]), [2, 8]])
            e_sc[dt] = em.op("v", lambda h, dt=dt, odd_ap=odd_ap: h.tensor_scalar_mul(
                xn_sc[:, dt, :], odd_ap, scv128[:, 0:1]), inc="dv",
                waits=[("dv", e_mul0), ("d_c0", e_consts)])
        e_ltcp = 0
        e_lttr = 0
        for dt in range(NDT):
            sl = xn[:, dt, 0:TC]
            ev_ap = bass.AP(sl.tensor, sl.offset + 1, [list(sl.ap[0]), [2, 8]])
            for which in (0, 1):
                tw = [("dv", e_mul0 if which == 0 else e_sc[dt])]
                if e_ltcp:
                    tw.append(("dv", e_ltcp))
                tw += list(entry)
                if which == 0:
                    e_lttr = em.op("t", lambda h, ev_ap=ev_ap: h.transpose(
                        psLt[0:8, 0, 0:128], ev_ap, eye_sb[:, :]), inc="pe",
                        waits=tw)
                    dstt = lt_e
                else:
                    e_lttr = em.op("t", lambda h, dt=dt: h.transpose(
                        psLt[0:8, 0, 0:128], xn_sc[:, dt, :], eye_sb[:, :]),
                        inc="pe", waits=tw)
                    dstt = lt_o
                e_ltcp = em.op("v", lambda h, dt=dt, dstt=dstt: h.tensor_copy(
                    dstt[0:8, dt, :], psLt[0:8, 0, 0:128]), inc="dv",
                    waits=[("pe", e_lttr)])

        def scatA(h):
            out = []
            for i in range(4):
                b = 32 * (i >> 1)
                ia = i & 1
                out.append(h.dma_start(ltr[b:b + 1, ia, 0, :], lt_e[i:i + 1, :, :]))
                out.append(h.dma_start(ltr[b:b + 1, ia, 1, :], lt_o[i:i + 1, :, :]))
            return out
        e_sca = em.op("g", scatA, inc="d_lt", waits=[("dv", e_ltcp)], ndma=8)

        bank_h = [list(entry), list(entry)]
        bank_o = [list(entry), list(entry)]
        e_w1add = [0] * NDT
        e_w1mm = [0] * NDT
        prev_add = [0, 0]
        e_glast = 0

        def emit_w1build(jc):
            i = jc >> 1
            b = 32 * (i >> 1)
            ia = i & 1
            e_add = 0
            last = 0
            for dt in range(NDT):
                bp = dt % 2
                def w1mm(h, jc=jc, dt=dt, b=b, ia=ia, bp=bp):
                    return h.matmul(
                        psW[:, bp, :],
                        ltr[b:b + 1, ia, 1, dt * 128:(dt + 1) * 128],
                        ltr[b:b + 1, ia, 0, (jc & 1) * 512:((jc & 1) + 1) * 512],
                        start=True, stop=True)
                ww = [("d_lt", e_sca), ("dv", st["r3"])]
                if prev_add[bp]:
                    ww.append(("dv", prev_add[bp]))
                last = em.op("t", w1mm, inc="pe", waits=ww)
                e_add = em.op("v", lambda h, jc=jc, dt=dt, bp=bp: h.tensor_tensor(
                    w1r[:, jc, dt, :], psW[:, bp, :], w1r[:, jc, dt, :], ALU.add),
                    inc="dv", waits=[("pe", last), ("d_w1b", e_w1b[jc])])
                prev_add[bp] = e_add
            e_w1add[jc] = e_add
            e_w1mm[jc] = last

        def emit_hgroup(c, jt, extra=()):
            bp = jt % 2
            jc = jt >> 2
            jjo = (jt & 3) * 128
            def hmm(h, jc=jc, jjo=jjo, bp=bp):
                ins = None
                for dt in range(NDT):
                    ins = h.matmul(psh[:, bp, :], w1r[:, jc, dt, jjo:jjo + 128],
                                   xn[:, dt, :], start=(dt == 0),
                                   stop=(dt == NDT - 1))
                return ins
            e_h = em.op("t", hmm, inc="pe",
                        waits=[("dv", st["mul"])] + list(bank_h[bp]) + list(extra))
            e_g = em.op("s", lambda h, jt=jt, bp=bp: h.activation(
                g_sb[:, jt, :], psh[:, bp, :], AF.Gelu,
                bias=b1_sb[:, jt:jt + 1], scale=1.0), inc="ac",
                waits=[("pe", e_h)])
            bank_h[bp] = [("ac", e_g)]
            return e_h, e_g

        for jc in range(NDT):
            emit_w1build(jc)
            for jj in range(4):
                jt = jc * 4 + jj
                e_hlast, e_glast = emit_hgroup(0, jt, extra=[("dv", e_w1add[jc])])

        emit_stats(1, [("dv", e_w1add[NDT - 1])])
        emit_rows(1)

        def scatB(h):
            out = []
            for i in range(4):
                b = 32 * (i >> 1)
                ia = i & 1
                se = lt_e[4 + i:5 + i, 0:NDT, 0:128]
                de = ltr[b:b + 1, ia, 0, 0:D]
                for r in range(4):
                    src_e = bass.AP(se.tensor, se.offset + r,
                                    [list(se.ap[0]), [128, 8], [4, 32]])
                    dst_e = bass.AP(de.tensor, de.offset + r * 256,
                                    [list(de.ap[0]), [32, 8], [1, 32]])
                    out.append(h.dma_start(dst_e, src_e,
                                           allow_slow_non_contiguous=True))
                out.append(h.dma_start(ltr[b:b + 1, ia, 1, :], lt_o[4 + i:5 + i, :, :]))
            return out
        e_scb = em.op("g", scatB, inc="d_lt2",
                      waits=[("pe", e_w1mm[NDT - 1])], ndma=20)

        e_slab = [0] * NDT
        e_w2addl = 0
        prev_add2 = [("dv", e_w1add[NDT - 1]), ("dv", e_w1add[NDT - 1])]
        e_w2bb = [0] * NDT
        e_addl_ot = [0] * NDT

        def dma_slab_in(ot):
            if ot >= NDT:
                return
            bw = [("d_c1", e_w2b_final)]
            if ot >= 2:
                bw.append(("d_w2e", e_slab[ot - 2]))
            e_w2bb[ot] = em.op("g", lambda h, ot=ot: h.dma_start(
                w2bld[:, ot % 2, :, :], w2b_scr[:, ot, :, :]), inc="d_w2bb",
                waits=bw)

        dma_slab_in(0)
        dma_slab_in(1)
        for ot in range(NDT):
            i = ot >> 1
            b = 32 * (i >> 1)
            ia = i & 1
            for jtq in range(NDT):
                bp = jtq % 2
                def b2mm(h, ot=ot, jtq=jtq, b=b, ia=ia, bp=bp):
                    ins = None
                    for jj in range(4):
                        jt = jtq * 4 + jj
                        k = jt >> 3
                        ins = h.matmul(
                            psW[:, bp, jj * 128:(jj + 1) * 128],
                            ltr[b:b + 1, ia, 1, (jt & 7) * 128:((jt & 7) + 1) * 128],
                            ltr[b:b + 1, ia, 0,
                                k * 256 + (ot & 1) * 128: k * 256 + (ot & 1) * 128 + 128],
                            start=True, stop=True)
                    return ins
                e_m2 = em.op("t", b2mm, inc="pe",
                             waits=[("d_lt2", e_scb), ("dv", st["r3"]), prev_add2[bp]])
                sl = w2bld[:, ot % 2, jtq * 4:(jtq + 1) * 4, :]
                e_a2 = em.op("v", lambda h, sl=sl, bp=bp: h.tensor_tensor(
                    flat2(sl, 512), psW[:, bp, :], flat2(sl, 512), ALU.add),
                    inc="dv", waits=[("pe", e_m2), ("d_w2bb", e_w2bb[ot])])
                prev_add2[bp] = ("dv", e_a2)
                e_w2addl = e_a2
            e_addl_ot[ot] = e_w2addl
            e_slab[ot] = em.op("y", lambda h, ot=ot: h.dma_start(
                w2_eff[:, ot, :, :], w2bld[:, ot % 2, :, :]), inc="d_w2e",
                waits=[("dv", e_w2addl)])
            dma_slab_in(ot + 2)

        e_w2d = {}
        e_omm_arr = [0] * NDT
        e_omm_prev = [0] * NDT

        def dma_w2(c, ot):
            if ot >= NDT:
                return
            bpw = ot % 2
            ww = [("d_w2e", e_slab[ot])]
            if ot >= 2:
                ww.append(("pe", e_omm_arr[ot - 2]))
            elif c >= 1 and e_omm_prev[NDT - 2 + ot]:
                ww.append(("pe", e_omm_prev[NDT - 2 + ot]))
            e_w2d[(c, ot)] = em.op("g", lambda h, ot=ot, bpw=bpw: h.dma_start(
                w2blk[:, bpw, :, :], w2_eff[:, ot, :, :]), inc="d_fw2%d" % bpw,
                waits=ww)

        for c in range(NCH):
            p = c % 2
            if c > 0:
                for jt in range(NJT):
                    e_hlast, e_glast = emit_hgroup(c, jt)
                if c < NCH - 1:
                    psw_free = [("dv", e_w2addl)] if c == 1 else [("dv", st["r3"])]
                    emit_stats(c + 1, psw_free)
                    emit_rows(c + 1)
            dma_w2(c, 0)
            dma_w2(c, 1)
            e_oev_last = 0
            for ot in range(NDT):
                bp = ot % 2
                def omm(h, c=c, ot=ot, bp=bp):
                    for jt in range(NJT):
                        h.matmul(pso[:, bp, :], w2blk[:, ot % 2, jt, :],
                                 g_sb[:, jt, :], start=(jt == 0), stop=False)
                    return h.matmul(pso[:, bp, :], b2f[0:1, ot * 128:(ot + 1) * 128],
                                    ones_row_bf[0:1, :], start=False, stop=True)
                e_o = em.op("t", omm, inc="pe",
                            waits=[("ac", e_glast), ("d_fw2%d" % (ot % 2), e_w2d[(c, ot)]),
                                   ("d_c0", e_consts)] + list(bank_o[bp]))
                e_omm_arr[ot] = e_o
                dma_w2(c, ot + 2)
                e_oev = em.op("v", lambda h, ot=ot, bp=bp, p=p: h.tensor_tensor(
                    xtb[:, p, ot, :], pso[:, bp, :], xtb[:, p, ot, :], ALU.add),
                    inc="dv", waits=[("pe", e_o)])
                bank_o[bp] = [("dv", e_oev)]
                e_oev_last = e_oev
                if c < NCH - 1:
                    if ot == 1:
                        emit_bcast_m(c + 1)
                    if ot == 3:
                        emit_bcast_r(c + 1)
            e_omm_prev = list(e_omm_arr)
            e_xout[c] = em.op("y", lambda h, c=c, p=p: h.dma_start(
                xout.rearrange("(a q) t -> q a t", q=128)[:, :, c * TC:(c + 1) * TC],
                xtb[:, p, :, :]), inc="d_fo%d" % p, waits=[("dv", e_oev_last)])
            dma_x(c + 2)

        for g in reversed(gF):
            g.__exit__(None, None, None)
        for g in reversed([psh_g, pso_g, psW_g, psbc_g, psLt_g]):
            g.__exit__(None, None, None)
        return em.all_counts()

    def phase_B(be, xin, xout, entry):
        gB = []

        def mk(name, shape, dt):
            g = nc.sbuf_tensor(f"{name}B_{be}", shape, dt)
            t_ = g.__enter__()
            gB.append(g)
            return t_

        wa_sb = mk("wa", [128, NDT, 3 * D], BF16)
        wp_sb = mk("wp", [128, NDT, D], BF16)
        xtb = mk("xtb", [128, 3, NDT, TC], F32)
        xbtb = mk("xb", [128, NDT, TC], BF16)
        xsq = mk("xsq", [128, NDT, TC], BF16)
        xn_tb = mk("xn", [128, 2, NDT, TC], BF16)
        qT2 = mk("qT2", [128, 8, TC], BF16)
        k16bd = mk("k16bd", [128, 8, 32], BF16)
        v16p = mk("v16p", [128, 8, 16], F32)
        vT16 = mk("vT16", [16, 8, 128], BF16)
        vexpA = mk("vexpA", [128, 1024], BF16)
        vexpB = mk("vexpB", [128, 1024], BF16)
        pB = mk("pB", [128, 2, TC], BF16)
        p2n = mk("p2n", [128, 2, TC], BF16)
        recT = mk("recT", [16, TC], F32)
        recTb = mk("recTb", [16, TC], BF16)
        aT = mk("aT", [128, NDT, TC], BF16)
        rows_f = mk("rowsf", [1, 2, TC], F32)
        rows_b = mk("rowsb", [1, 2, TC], BF16)

        ps1_g = nc.psum_tensor(f"ps1B_{be}", [128, 2, TC], F32)
        ps2_g = nc.psum_tensor(f"ps2B_{be}", [128, 2, TC], F32)
        psOT_g = nc.psum_tensor(f"psOTB_{be}", [128, 2, TC], F32)
        psBc_g = nc.psum_tensor(f"psBcB_{be}", [128, 2, TC], F32)
        ps1, ps2, psOT, psBc = (ps1_g.__enter__(), ps2_g.__enter__(),
                                psOT_g.__enter__(), psBc_g.__enter__())

        e_wbq = em.op("g", lambda h: h.dma_start(
            wa_sb[:, :, 0:D], waT[:, :, 0:D]), inc="d_bw", waits=list(entry))
        e_wbkv = em.op("g", lambda h: h.dma_start(
            wa_sb[:, :, D:3 * D], waT[:, :, D:3 * D]), inc="d_bw")
        e_wbp = em.op("g", lambda h: h.dma_start(
            wp_sb[:, :, :], wpT[:, :, :]), inc="d_bw")

        e_xin = [0] * NCH
        e_xout = [0] * NCH

        def dma_x(c, w0=None):
            if c >= NCH:
                return
            xw = list(entry) if c < 3 else []
            if w0:
                xw += w0
            e_xin[c] = em.op("y", lambda h, c=c: h.dma_start(
                xtb[:, c % 3, :, :],
                xin.rearrange("(a q) t -> q a t", q=128)[:, :, c * TC:(c + 1) * TC]),
                inc="d_bx", waits=xw)

        inv = 1.0 / D
        st = {k: [0] * (NCH + 2) for k in
              ("cast", "sq", "stats", "r3", "r5", "bm", "br", "sub", "mul")}
        e_qpj = [0] * (NCH + 1)
        e_qev = [[0] * 8 for _ in range(NCH)]
        e_lg = [[0, 0] for _ in range(NCH)]
        e_p2n = [[0, 0] for _ in range(NCH + 1)]
        e_pv = [[0] * 8 for _ in range(NCH)]
        e_aev_s = [0] * (NCH + 1)
        e_aev_v = [0] * (NCH + 1)
        e_rcp = [0] * NCH
        e_res = [[0] * 8 for _ in range(NCH + 1)]
        e_hs = [0] * NCH
        e_pjl = [0] * (NCH + 1)

        def emit_cast(c, extra=()):
            if c >= NCH:
                return
            cw = [("d_bx", e_xin[c])] + list(extra)
            if c >= 1:
                cw.append(("dv", st["sub"][c - 1]))
            st["cast"][c] = em.op("v", lambda h, c=c: h.tensor_copy(
                xbtb[:, :, :], xtb[:, c % 3, :, :]), inc="dv", waits=cw)
            sqw = [("dv", st["cast"][c])]
            if c >= 1:
                sqw.append(("pe", st["stats"][c - 1]))
            st["sq"][c] = em.op("s", lambda h: h.square(
                xsq[:, :, :], xbtb[:, :, :]), inc="ac", waits=sqw)

        def emit_stats(c, extra=()):
            if c >= NCH:
                return
            sw = [("dv", st["cast"][c]), ("ac", st["sq"][c])] + list(extra)
            if c >= 1:
                sw += [("dv", st["sub"][c - 1]), ("dv", st["mul"][c - 1])]

            def stmm(h):
                ins = None
                for dt in range(NDT):
                    h.matmul(psBc[0:1, 0, :], ones_col_bf[:, 0:1], xbtb[:, dt, :],
                             start=(dt == 0), stop=(dt == NDT - 1),
                             skip_group_check=True)
                    ins = h.matmul(psBc[0:1, 1, :], ones_col_bf[:, 0:1],
                                   xsq[:, dt, :], start=(dt == 0),
                                   stop=(dt == NDT - 1), skip_group_check=True)
                return ins
            st["stats"][c] = em.op("t", stmm, inc="pe", waits=sw)

        def emit_rows(c):
            if c >= NCH:
                return
            rw = [("pe", st["stats"][c])]
            if c >= 1:
                rw.append(("pe", st["bm"][c - 1]))
            e_r1 = em.op("v", lambda h: h.tensor_scalar_mul(
                rows_b[0:1, 0, :], psBc[0:1, 0, :], inv), inc="dv", waits=rw)
            e_r2 = em.op("v", lambda h: h.tensor_mul(
                rows_f[0:1, 0, :], rows_b[0:1, 0, :], rows_b[0:1, 0, :]),
                inc="dv", waits=[("dv", e_r1)])
            st["r3"][c] = em.op("v", lambda h: h.scalar_tensor_tensor(
                rows_f[0:1, 1, :], psBc[0:1, 1, :], inv, rows_f[0:1, 0, :],
                ALU.mult, ALU.subtract), inc="dv", waits=[("dv", e_r2)])
            e_r4 = em.op("s", lambda h: h.activation(
                rows_f[0:1, 1, :], rows_f[0:1, 1, :], AF.Sqrt,
                bias=eps_sb[0:1, 0:1], scale=1.0), inc="ac",
                waits=[("dv", st["r3"][c]), ("dv", e_ms)])
            e_r5a = em.op("v", lambda h: h.reciprocal(
                rows_f[0:1, 0, :], rows_f[0:1, 1, :]), inc="dv",
                waits=[("ac", e_r4)])
            r5w = [("dv", e_r5a)]
            if c >= 1:
                r5w.append(("pe", st["br"][c - 1]))
            st["r5"][c] = em.op("v", lambda h: h.tensor_copy(
                rows_b[0:1, 1, :], rows_f[0:1, 0, :]), inc="dv", waits=r5w)

        def emit_bcasts(c):
            if c >= NCH:
                return
            st["bm"][c] = em.op("t", lambda h: h.matmul(
                psBc[:, 0, :], ones_row_bf[0:1, 0:128], rows_b[0:1, 0, :],
                start=True, stop=True, skip_group_check=True), inc="pe",
                waits=[("dv", st["r5"][c])])
            st["br"][c] = em.op("t", lambda h: h.matmul(
                psBc[:, 1, :], ones_row_bf[0:1, 0:128], rows_b[0:1, 1, :],
                start=True, stop=True, skip_group_check=True), inc="pe",
                waits=[("pe", st["bm"][c])])

        def emit_submul(c):
            if c >= NCH:
                return
            sw = [("pe", st["bm"][c])]
            if c >= 2:
                sw.append(("pe", e_qpj[c - 2]))
            st["sub"][c] = em.op("v", lambda h, c=c: h.tensor_tensor(
                xn_tb[:, c % 2, :, :], xbtb[:, :, :], bcast_mid(psBc[:, 0, :], NDT),
                ALU.subtract), inc="dv", waits=sw)
            st["mul"][c] = em.op("v", lambda h, c=c: h.tensor_tensor(
                xn_tb[:, c % 2, :, :], xn_tb[:, c % 2, :, :],
                bcast_mid(psBc[:, 1, :], NDT), ALU.mult), inc="dv",
                waits=[("pe", st["br"][c]), ("dv", st["sub"][c])])

        dma_x(0)
        dma_x(1)
        dma_x(2)
        emit_cast(0, extra=list(entry))
        emit_stats(0, extra=list(entry))
        emit_rows(0)
        emit_bcasts(0)
        emit_submul(0)
        emit_cast(1)
        emit_stats(1)
        emit_rows(1)

        e_z = em.op("v", lambda h: (h.memset(k16bd[:, :, :], 0.0),
                                    h.memset(vexpA[:, :], 0.0),
                                    h.memset(vexpB[:, :], 0.0))[-1],
                    inc="dv", waits=list(entry))
        bank1 = [list(entry), list(entry)]
        e_kv_ev = 0
        for which, wofs in (("k", D), ("v", 2 * D)):
            for p_ in range(8):
                bp = p_ % 2

                def kvmm(h, p_=p_, wofs=wofs, bp=bp):
                    ins = None
                    for dt in range(NDT):
                        ins = h.matmul(ps1[:, bp, 0:16],
                                       wa_sb[:, dt, wofs + p_ * 128:wofs + (p_ + 1) * 128],
                                       xn_tb[:, 0, dt, 0:16],
                                       start=(dt == 0), stop=(dt == NDT - 1))
                    return ins
                e_kv = em.op("t", kvmm, inc="pe",
                             waits=[("dv", st["mul"][0]), ("d_bw", e_wbkv)] +
                                   list(bank1[bp]))
                if which == "k":
                    def kev(h, p_=p_, bp=bp):
                        h.copy(k16bd[0:64, p_, 0:16], ps1[0:64, bp, 0:16])
                        return h.copy(k16bd[64:128, p_, 16:32], ps1[64:128, bp, 0:16])
                    e_kv_ev = em.op("s", kev, inc="ac",
                                    waits=[("pe", e_kv), ("dv", e_z)])
                else:
                    e_kv_ev = em.op("s", lambda h, p_=p_, bp=bp: h.copy(
                        v16p[:, p_, :], ps1[:, bp, 0:16]), inc="ac",
                        waits=[("pe", e_kv)])
                bank1[bp] = [("ac", e_kv_ev)]
        e_k16_ready = e_kv_ev
        bank2 = [list(entry), list(entry)]
        e_vt = 0
        for p_ in range(8):
            bp = p_ % 2
            e_vtr = em.op("t", lambda h, p_=p_, bp=bp: h.transpose(
                ps2[0:16, bp, 0:128], v16p[:, p_, 0:16], eye32_sb[:, :]),
                inc="pe", waits=[("ac", e_k16_ready), ("d_c0", e_consts)] +
                      list(bank2[bp]))
            e_vt = em.op("v", lambda h, p_=p_, bp=bp: h.tensor_copy(
                vT16[0:16, p_, :], ps2[0:16, bp, 0:128]), inc="dv",
                waits=[("pe", e_vtr)])
            bank2[bp] = [("dv", e_vt)]

        def scat(h):
            out = []
            for hh in range(16):
                dstv = vexpA if hh < 8 else vexpB
                prow = (hh % 8) * 16
                out.append(h.dma_start(
                    dstv[prow:prow + 16, hh * 64:(hh + 1) * 64],
                    vT16[0:16, hh >> 1, (hh & 1) * 64:((hh & 1) + 1) * 64]))
            return out
        e_vexp_ready = em.op("g", scat, inc="d_bs",
                             waits=[("dv", e_vt), ("dv", e_z)], ndma=16)

        for c in range(NCH):
            p2 = c % 2
            for pair in range(8):
                bp = pair % 2

                def qmm(h, pair=pair, bp=bp, p2=p2):
                    ins = None
                    for dt in range(NDT):
                        ins = h.matmul(ps1[:, bp, :],
                                       wa_sb[:, dt, pair * 128:(pair + 1) * 128],
                                       xn_tb[:, p2, dt, :],
                                       start=(dt == 0), stop=(dt == NDT - 1))
                    return ins
                qw = [("dv", st["mul"][c]), ("d_bw", e_wbq)]
                if pair < 2:
                    if c == 0:
                        qw += list(bank1[bp])
                    else:
                        qw.append(("dv", e_p2n[c - 1][bp]))
                else:
                    qw.append(("ac", e_qev[c][pair - 2]))
                e_q = em.op("t", qmm, inc="pe", waits=qw)
                e_qpj[c] = e_q
                evw = [("pe", e_q)]
                if c >= 1:
                    evw.append(("pe", e_lg[c - 1][pair // 4]))
                e_qev[c][pair] = em.op("s", lambda h, pair=pair, bp=bp: h.copy(
                    qT2[:, pair, :], ps1[:, bp, :]), inc="ac", waits=evw)

            e_ex = [0, 0]
            e_mask = [0, 0]
            for t2 in range(2):

                def lmm(h, t2=t2):
                    ins = None
                    for j in range(4):
                        pair = 4 * t2 + j
                        ins = h.matmul(ps1[32 * j:32 * j + 32, t2, :],
                                       k16bd[:, pair, :], qT2[:, pair, :],
                                       start=True, stop=True,
                                       tile_position=(0, 32 * j),
                                       skip_group_check=True)
                    return ins
                lgw = [("ac", e_qev[c][6 + t2]), ("ac", e_k16_ready)]
                e_lg[c][t2] = em.op("t", lmm, inc="pe", waits=lgw)
                exw = [("pe", e_lg[c][t2])]
                if c >= 1:
                    exw.append(("dv", e_p2n[c - 1][t2]))
                e_ex[t2] = em.op("s", lambda h, t2=t2: h.activation(
                    pB[:, t2, :], ps1[:, t2, :], AF.Exp, scale=0.125), inc="ac",
                    waits=exw)
                if c == 0:
                    e_mask[t2] = em.op("v", lambda h, t2=t2: h.tensor_tensor(
                        pB[:, t2, 0:16], pB[:, t2, 0:16], m01_sb[:, :], ALU.mult),
                        inc="dv", waits=[("ac", e_ex[t2]), ("d_c0", e_consts)])

            hw_ = ([("dv", e_mask[0]), ("dv", e_mask[1])] if c == 0 else
                   [("ac", e_ex[0]), ("ac", e_ex[1])])
            if c >= 1:
                hw_.append(("ac", e_aev_s[c - 1]))
                hw_.append(("dv", e_aev_v[c - 1]))
            else:
                hw_ += list(entry)

            def hsum(h):
                h.matmul(psOT[0:16, 0, :], hsA_sb[:, :], pB[:, 0, :],
                         start=True, stop=False, skip_group_check=True)
                return h.matmul(psOT[0:16, 0, :], hsB_sb[:, :], pB[:, 1, :],
                                start=False, stop=True, skip_group_check=True)
            e_hs[c] = em.op("t", hsum, inc="pe", waits=hw_)
            e_rcp[c] = em.op("v", lambda h: h.reciprocal(
                recT[0:16, :], psOT[0:16, 0, :]), inc="dv",
                waits=[("pe", e_hs[c])])
            e_rcb = em.op("v", lambda h: h.tensor_copy(
                recTb[0:16, :], recT[0:16, :]), inc="dv",
                waits=[("dv", e_rcp[c])])

            for t2 in range(2):
                e_brc = em.op("t", lambda h, t2=t2: h.matmul(
                    ps1[:, t2, :], (selA_sb if t2 == 0 else selB_sb)[:, :],
                    recTb[0:16, :], start=True, stop=True,
                    skip_group_check=True), inc="pe",
                    waits=[("dv", e_rcb), ("ac", e_ex[t2])])
                pw = [("pe", e_brc)]
                if c >= 1:
                    pw.append(("pe", e_pv[c - 1][3 + 4 * t2]))
                e_p2n[c][t2] = em.op("v", lambda h, t2=t2: h.tensor_tensor(
                    p2n[:, t2, :], pB[:, t2, :], ps1[:, t2, :], ALU.mult),
                    inc="dv", waits=pw)

            for dt in range(NDT):
                b = dt % 2
                t2 = dt // 4
                vx = vexpA if dt < 4 else vexpB
                pw = [("dv", e_p2n[c][t2]), ("d_bs", e_vexp_ready)]
                if dt == 0:
                    pw.append(("dv", e_rcp[c]))
                elif dt == 1:
                    if c >= 1:
                        pw.append(("dv", e_aev_v[c - 1]))
                elif dt % 2 == 0:
                    pw.append(("ac", e_aev_s[c]))
                else:
                    pw.append(("dv", e_aev_v[c]))
                e_pv[c][dt] = em.op("t", lambda h, dt=dt, b=b, t2=t2, vx=vx: h.matmul(
                    psOT[:, b, :], vx[:, dt * 128:(dt + 1) * 128], p2n[:, t2, :],
                    start=True, stop=True, skip_group_check=True), inc="pe",
                    waits=pw)
                aw = [("pe", e_pv[c][dt])]
                if c >= 1:
                    aw.append(("pe", e_pjl[c - 1]))
                else:
                    aw += list(entry)
                if dt % 2 == 0:
                    e_aev_s[c] = em.op("s", lambda h, dt=dt, b=b: h.copy(
                        aT[:, dt, :], psOT[:, b, :]), inc="ac", waits=aw)
                else:
                    e_aev_v[c] = em.op("v", lambda h, dt=dt, b=b: h.tensor_copy(
                        aT[:, dt, :], psOT[:, b, :]), inc="dv", waits=aw)

            for ot in range(NDT):
                bp = ot % 2

                def pjmm(h, ot=ot, bp=bp):
                    ins = None
                    for ct in range(NDT):
                        ins = h.matmul(ps2[:, bp, :],
                                       wp_sb[:, ct, ot * 128:(ot + 1) * 128],
                                       aT[:, ct, :], start=(ct == 0),
                                       stop=(ct == NDT - 1))
                    return ins
                pjw = [("ac", e_aev_s[c]), ("dv", e_aev_v[c]), ("d_bw", e_wbp)]
                if ot < 2:
                    if c == 0:
                        pjw += list(bank2[bp])
                    else:
                        pjw.append(("dv", e_res[c - 1][6 + bp]))
                else:
                    pjw.append(("dv", e_res[c][ot - 2]))
                e_pj = em.op("t", pjmm, inc="pe", waits=pjw)
                e_pjl[c] = e_pj
                e_res[c][ot] = em.op("v", lambda h, ot=ot, bp=bp, c=c: h.tensor_tensor(
                    xtb[:, c % 3, ot, :], ps2[:, bp, :], xtb[:, c % 3, ot, :],
                    ALU.add), inc="dv", waits=[("pe", e_pj), ("d_bx", e_xin[c])])
                if ot == 1:
                    emit_bcasts(c + 1)
                    emit_submul(c + 1)
                if ot == 3:
                    emit_cast(c + 2)
                if ot == 5:
                    emit_stats(c + 2)
                    emit_rows(c + 2)
            e_xout[c] = em.op("y", lambda h, c=c: h.dma_start(
                xout.rearrange("(a q) t -> q a t", q=128)[:, :, c * TC:(c + 1) * TC],
                xtb[:, c % 3, :, :]), inc="d_bo", waits=[("dv", e_res[c][7])])
            dma_x(c + 3, w0=[("d_bo", e_xout[c])])

        for g in reversed(gB):
            g.__exit__(None, None, None)
        for g in reversed([ps1_g, ps2_g, psOT_g, psBc_g]):
            g.__exit__(None, None, None)
        return em.all_counts()


    entry = [("d_c0", e_consts), ("dv", e_ms)]
    ph = 0
    for be in range(nbe):
        s1 = xr1[be if dbg_scratch else 0]
        s2 = xr2[be if dbg_scratch else 0]
        s3 = xr3[be if dbg_scratch else 0]
        if ph < nphase:
            entry = phase_A(be, entry)
            ph += 1
        if ph < nphase:
            entry = phase_F(be, 0, s1, s2, entry)
            ph += 1
        if ph < nphase:
            entry = phase_B(be, s2, s3, entry)
            ph += 1
        if ph < nphase:
            entry = phase_F(be, 1, s3, outT[be], entry)
            ph += 1

    finals = [(n, v) for n, v in em.cnt.items() if n.startswith("d_") and v > 0]
    def fin(h):
        for n, v in finals:
            h.wait_ge(em.sem[n], v)
    em.fns["y"].append(fin)

    em.finish()
    for g in reversed(guards):
        g.__exit__(None, None, None)
    return nc


def kernel(L0_attn_a_x, L0_attn_a_Wq, L0_attn_a_bq, L0_attn_a_Wk, L0_attn_a_bk,
           L0_attn_a_Wv, L0_attn_a_bv, L0_attn_b_W_attn, L0_attn_b_W_proj,
           L0_attn_b_W_rel, L0_ffn_W1, L0_ffn_b1, L0_ffn_W2, L0_ffn_b2, L0_ffn_scale):
    x = np.asarray(L0_attn_a_x, dtype=np.float32)
    B = x.shape[0]
    n_cores = 8
    xT = np.ascontiguousarray(np.transpose(x, (0, 2, 1)))

    def t32(a):
        wT = np.asarray(a, dtype=np.float32).T
        return np.ascontiguousarray(wT.reshape(NDT, 128, wT.shape[1]).transpose(1, 0, 2))

    w1 = np.asarray(L0_ffn_W1, np.float32)
    w2 = np.asarray(L0_ffn_W2, np.float32)
    common = {
        "wqT": t32(L0_attn_a_Wq), "wkT": t32(L0_attn_a_Wk), "wvT": t32(L0_attn_a_Wv),
        "waT": t32(L0_attn_b_W_attn), "wpT": t32(L0_attn_b_W_proj),
        "w1p": np.ascontiguousarray(
            w1.reshape(NDT, 512, NDT, 128).transpose(3, 0, 2, 1)),
        "w2p": np.ascontiguousarray(
            w2.reshape(NDT, 128, NJT, 128).transpose(3, 0, 2, 1)),
        "bqp": np.ascontiguousarray(np.asarray(L0_attn_a_bq, np.float32).reshape(NDT, 128).T),
        "bkp": np.ascontiguousarray(np.asarray(L0_attn_a_bk, np.float32).reshape(NDT, 128).T),
        "b1p": np.ascontiguousarray(np.asarray(L0_ffn_b1, np.float32).reshape(NJT, 128).T),
        "bv_r": np.asarray(L0_attn_a_bv, np.float32).reshape(1, D),
        "b2_r": np.asarray(L0_ffn_b2, np.float32).reshape(1, D),
        "scal": np.asarray(L0_ffn_scale, np.float32).reshape(1, 1),
    }

    nc = build_nc(nbe=B // n_cores)
    in_maps = []
    for c in range(n_cores):
        m = dict(common)
        m["xt"] = np.ascontiguousarray(xT[c * (B // n_cores):(c + 1) * (B // n_cores)])
        in_maps.append(m)
    res = run_bass_kernel_spmd(nc, in_maps, core_ids=list(range(n_cores)))
    parts = [np.transpose(res.results[c]["outT"], (0, 2, 1)) for c in range(n_cores)]
    return np.ascontiguousarray(np.concatenate(parts, axis=0)).astype(np.float32)


# version 18
# speedup vs baseline: 1.2448x; 1.0298x over previous
import numpy as np
import ml_dtypes
import concourse.bass as bass
import concourse.mybir as mybir
from concourse.bass_utils import run_bass_kernel_spmd

F32 = mybir.dt.float32
BF16 = mybir.dt.bfloat16
AF = mybir.ActivationFunctionType
ALU = mybir.AluOpType
AX = mybir.AxisListType

D = 1024
T = 4096
HID = 4096
NDT = D // 128
NJT = HID // 128
NBE = 2
NEG = -1.0e30
TC = 512
NCH = T // TC
NQT = T // 128


class Em:
    ENGS = ("t", "v", "s", "g", "y")

    def __init__(self, nc):
        self.nc = nc
        self.fns = {k: [] for k in self.ENGS}
        self.sem = {}
        self.cnt = {}
        self.hwm = {e: {} for e in self.ENGS}
        for name in ("pe", "dv", "ac", "go"):
            self.sem[name] = nc.alloc_semaphore(name="sem_" + name)
            self.cnt[name] = 0

    def _get_sem(self, name):
        if name not in self.sem:
            self.sem[name] = self.nc.alloc_semaphore(name="sem_" + name)
            self.cnt[name] = 0
        return self.sem[name]

    def op(self, eng, fn, inc=None, waits=(), ndma=1):
        hw = self.hwm[eng]
        wl = []
        for s, v in waits:
            if v and v > 0 and hw.get(s, 0) < v:
                wl.append((self._get_sem(s), v))
                hw[s] = v
        amt = 16 if (inc and inc.startswith("d_")) else 1
        sem = self._get_sem(inc) if inc else None

        def run(h, fn=fn, wl=wl, sem=sem, amt=amt):
            for s, v in wl:
                h.wait_ge(s, v)
            ins = fn(h)
            if sem is not None:
                if isinstance(ins, (list, tuple)):
                    for i_ in ins:
                        i_.then_inc(sem, amt)
                else:
                    ins.then_inc(sem, amt)

        self.fns[eng].append(run)
        if inc:
            self.cnt[inc] += amt * ndma
            return self.cnt[inc]
        return None

    def all_counts(self):
        return [(k, v) for k, v in self.cnt.items() if k != "go" and v > 0]

    def finish(self):
        with self.nc.Block() as block:
            @block.tensor
            def _(h):
                for fn in self.fns["t"]:
                    fn(h)

            @block.vector
            def _(h):
                for fn in self.fns["v"]:
                    fn(h)

            @block.scalar
            def _(h):
                for fn in self.fns["s"]:
                    fn(h)

            @block.gpsimd
            def _(h):
                for fn in self.fns["g"]:
                    fn(h)

            @block.sync
            def _(h):
                for fn in self.fns["y"]:
                    fn(h)


def apx(handle, offset, dims):
    return bass.AP(handle, offset, [list(d) for d in dims])


def bcast_mid(ps_ap, n):
    return bass.AP(ps_ap.tensor, ps_ap.offset,
                   [list(ps_ap.ap[0]), [0, n], list(ps_ap.ap[1])])


def bcast_inner(ap, n):
    return bass.AP(ap.tensor, ap.offset,
                   [list(ap.ap[0]), list(ap.ap[1]), [0, n]])


def build_nc(nbe=NBE, dbg_scratch=False, nphase=8 * NBE):
    nc = bass.Bass("TRN2", target_bir_lowering=False)

    xt = nc.dram_tensor("xt", [nbe, D, T], F32, kind="ExternalInput").ap()
    outT = nc.dram_tensor("outT", [nbe, D, T], F32, kind="ExternalOutput").ap()
    wqT = nc.dram_tensor("wqT", [128, NDT, D], F32, kind="ExternalInput").ap()
    wkT = nc.dram_tensor("wkT", [128, NDT, D], F32, kind="ExternalInput").ap()
    wvT = nc.dram_tensor("wvT", [128, NDT, D], F32, kind="ExternalInput").ap()
    waT = nc.dram_tensor("waT", [128, NDT, 3 * D], F32, kind="ExternalInput").ap()
    wpT = nc.dram_tensor("wpT", [128, NDT, D], F32, kind="ExternalInput").ap()
    w1p = nc.dram_tensor("w1p", [128, NDT, NDT, 512], F32, kind="ExternalInput").ap()
    w2p = nc.dram_tensor("w2p", [128, NDT, NJT, 128], F32, kind="ExternalInput").ap()
    bqp = nc.dram_tensor("bqp", [128, NDT], F32, kind="ExternalInput").ap()
    bkp = nc.dram_tensor("bkp", [128, NDT], F32, kind="ExternalInput").ap()
    b1p = nc.dram_tensor("b1p", [128, NJT], F32, kind="ExternalInput").ap()
    bv_r = nc.dram_tensor("bv_r", [1, D], F32, kind="ExternalInput").ap()
    b2_r = nc.dram_tensor("b2_r", [1, D], F32, kind="ExternalInput").ap()
    sc_d = nc.dram_tensor("scal", [1, 1], F32, kind="ExternalInput").ap()

    skind = "ExternalOutput" if dbg_scratch else "Internal"
    nsc = nbe if dbg_scratch else 1
    q_scr = nc.dram_tensor("q_scr", [D, T], BF16, kind="Internal").ap()
    v_scr = nc.dram_tensor("v_scr", [T, D], BF16, kind="Internal").ap()
    wqkvb_scr = nc.dram_tensor("wqkvb_scr", [128, 3, NDT, D], BF16, kind="Internal").ap()
    w1b_scr = nc.dram_tensor("w1b_scr", [128, NDT, NDT, 512], BF16, kind="Internal").ap()
    w2b_scr = nc.dram_tensor("w2b_scr", [128, NDT, NJT, 128], BF16, kind="Internal").ap()
    w2_eff = nc.dram_tensor("w2_eff", [128, NDT, NJT, 128], BF16, kind="Internal").ap()
    xr1 = nc.dram_tensor("xr1", [nsc, D, T], F32, kind=skind).ap()
    xr2 = nc.dram_tensor("xr2", [nsc, D, T], F32, kind=skind).ap()
    xr3 = nc.dram_tensor("xr3", [nsc, D, T], F32, kind=skind).ap()

    eye_d = nc.inline_tensor(np.eye(128, dtype=ml_dtypes.bfloat16), name="eye_bf").ap()
    eye32_d = nc.inline_tensor(np.eye(128, dtype=np.float32), name="eye_f32").ap()
    negeye_d = nc.inline_tensor((NEG * np.eye(128)).astype(ml_dtypes.bfloat16),
                                name="negeye_bf").ap()
    h0 = np.array([2 * (r // 32) + (r // 16) % 2 for r in range(128)])
    hsA_np = np.zeros((128, 16), dtype=ml_dtypes.bfloat16)
    hsB_np = np.zeros((128, 16), dtype=ml_dtypes.bfloat16)
    selA_np = np.zeros((16, 128), dtype=ml_dtypes.bfloat16)
    selB_np = np.zeros((16, 128), dtype=ml_dtypes.bfloat16)
    for r in range(128):
        hsA_np[r, h0[r]] = 1.0
        hsB_np[r, 8 + h0[r]] = 1.0
        selA_np[h0[r], r] = 1.0
        selB_np[8 + h0[r], r] = 1.0
    m01_np = np.zeros((128, 16), dtype=ml_dtypes.bfloat16)
    for r in range(128):
        for t in range(16):
            if r % 16 <= t:
                m01_np[r, t] = 1.0
    hsA_d = nc.inline_tensor(hsA_np, name="hsA").ap()
    hsB_d = nc.inline_tensor(hsB_np, name="hsB").ap()
    selA_d = nc.inline_tensor(selA_np, name="selA").ap()
    selB_d = nc.inline_tensor(selB_np, name="selB").ap()
    m01_d = nc.inline_tensor(m01_np, name="m01").ap()

    em = Em(nc)
    guards = []

    def sb(name, shape, dt):
        g = nc.sbuf_tensor(name, shape, dt)
        t = g.__enter__()
        guards.append(g)
        return t

    ones_col_bf = sb("ones_col_bf", [128, 1], BF16)
    ones_row_bf = sb("ones_row_bf", [1, TC], BF16)
    ones_row_f32 = sb("ones_row_f32", [1, 128], F32)
    eye_sb = sb("eye_sb", [128, 128], BF16)
    eye32_sb = sb("eye32_sb", [128, 128], F32)
    negeye_sb = sb("negeye_sb", [128, 128], BF16)
    hsA_sb = sb("hsA_sb", [128, 16], BF16)
    hsB_sb = sb("hsB_sb", [128, 16], BF16)
    selA_sb = sb("selA_sb", [16, 128], BF16)
    selB_sb = sb("selB_sb", [16, 128], BF16)
    m01_sb = sb("m01_sb", [128, 16], BF16)
    bq_sb = sb("bq_sb", [128, NDT], F32)
    bk_sb = sb("bk_sb", [128, NDT], F32)
    b1_sb = sb("b1_sb", [128, NJT], F32)
    b2f = sb("b2f", [1, D], BF16)
    sc4 = sb("sc4", [4, 1], F32)
    scv128 = sb("scv128", [128, 1], F32)
    eps_sb = sb("eps_sb", [1, 1], F32)

    def clr(h):
        for s_ in em.sem.values():
            h.sem_clear(s_)
        return h.memset(ones_col_bf[:, :], 1.0)
    em.op("g", clr, inc="go")
    for e in ("t", "v", "s", "y"):
        def wgo(h):
            h.wait_ge(em.sem["go"], 1)
        em.fns[e].append(wgo)

    e_ms = em.op("v", lambda h: (h.memset(ones_row_bf[:, :], 1.0),
                                 h.memset(ones_row_f32[:, :], 1.0),
                                 h.memset(eps_sb[:, :], 1e-5))[-1], inc="dv")

    def loads(h):
        return [
            h.dma_start(eye_sb[:, :], eye_d[:, :]),
            h.dma_start(eye32_sb[:, :], eye32_d[:, :]),
            h.dma_start(negeye_sb[:, :], negeye_d[:, :]),
            h.dma_start(hsA_sb[:, :], hsA_d[:, :]),
            h.dma_start(hsB_sb[:, :], hsB_d[:, :]),
            h.dma_start(selA_sb[:, :], selA_d[:, :]),
            h.dma_start(selB_sb[:, :], selB_d[:, :]),
            h.dma_start(m01_sb[:, :], m01_d[:, :]),
            h.dma_start(bq_sb[:, :], bqp[:, :]),
            h.dma_start(bk_sb[:, :], bkp[:, :]),
            h.dma_start(b1_sb[:, :], b1p[:, :]),
            h.dma_start(b2f[0:1, :], b2_r[:, :]),
            h.dma_start(sc4[:, :], bass.AP(sc_d.tensor, 0, [[0, 4], [1, 1]])),
            h.dma_start(scv128[:, :], bass.AP(sc_d.tensor, 0, [[0, 128], [1, 1]])),
        ]
    e_consts = em.op("g", loads, inc="d_c0", ndma=14)
    wcast = {"done": 0}

    def emit_wcast_slab(j):
        wcast["done"] = em.op("g", lambda h, j=j: [
            h.dma_start(w1b_scr[:, j, :, :], w1p[:, j, :, :]),
            h.dma_start(w2b_scr[:, j, :, :], w2p[:, j, :, :])], inc="d_c1", ndma=2)
    e_w2b_final = 16 * 16

    def emit_ln(xb, xsq, xn_out, ps_s1, ps_s2, ps_m, ps_r, rows_f32, rows_bf,
                waits_x, waits_reuse, xtile=None):
        inv = 1.0 / D
        if xtile is not None:
            e_cast = em.op("v", lambda h: h.tensor_copy(xb[:, :, :], xtile),
                           inc="dv", waits=list(waits_x) + list(waits_reuse))
            xb_w = [("dv", e_cast)]
        else:
            xb_w = list(waits_x)
        e_sq = em.op("s", lambda h: h.square(xsq[:, :, :], xb[:, :, :]),
                     inc="ac", waits=xb_w + list(waits_reuse))

        def stats(h):
            ins = None
            for dt in range(NDT):
                h.matmul(ps_s1, ones_col_bf[:, 0:1], xb[:, dt, :],
                         start=(dt == 0), stop=(dt == NDT - 1))
            for dt in range(NDT):
                ins = h.matmul(ps_s2, ones_col_bf[:, 0:1], xsq[:, dt, :],
                               start=(dt == 0), stop=(dt == NDT - 1))
            return ins
        e_st = em.op("t", stats, inc="pe",
                     waits=xb_w + [("ac", e_sq)] + list(waits_reuse))
        e_r1 = em.op("v", lambda h: h.tensor_scalar_mul(rows_bf[0:1, 0, :], ps_s1, inv),
                     inc="dv", waits=[("pe", e_st)] + list(waits_reuse))
        e_r2 = em.op("v", lambda h: h.tensor_mul(
            rows_f32[0:1, 0, :], rows_bf[0:1, 0, :], rows_bf[0:1, 0, :]), inc="dv",
            waits=[("dv", e_r1)])
        e_r3 = em.op("v", lambda h: h.scalar_tensor_tensor(
            rows_f32[0:1, 1, :], ps_s2, inv, rows_f32[0:1, 0, :], ALU.mult, ALU.subtract),
            inc="dv", waits=[("dv", e_r2)])
        e_r4 = em.op("s", lambda h: h.activation(
            rows_f32[0:1, 1, :], rows_f32[0:1, 1, :], AF.Sqrt, bias=eps_sb[0:1, 0:1],
            scale=1.0), inc="ac", waits=[("dv", e_r3), ("dv", e_ms)])
        e_r5a = em.op("v", lambda h: h.reciprocal(rows_f32[0:1, 0, :], rows_f32[0:1, 1, :]),
                      inc="dv", waits=[("ac", e_r4)])
        e_r5 = em.op("v", lambda h: h.tensor_copy(rows_bf[0:1, 1, :], rows_f32[0:1, 0, :]),
                     inc="dv", waits=[("dv", e_r5a)])
        if ps_r is not None:
            def bcast(h):
                h.matmul(ps_m, ones_row_bf[0:1, 0:128], rows_bf[0:1, 0, :],
                         start=True, stop=True)
                return h.matmul(ps_r, ones_row_bf[0:1, 0:128], rows_bf[0:1, 1, :],
                                start=True, stop=True)
            e_bc = em.op("t", bcast, inc="pe", waits=[("dv", e_r5)] + list(waits_reuse))
            e_n1 = em.op("v", lambda h: h.tensor_tensor(
                xn_out[:, :, :], xb[:, :, :], bcast_mid(ps_m, NDT), ALU.subtract),
                inc="dv", waits=[("pe", e_bc)] + list(waits_reuse))
            e_n2 = em.op("v", lambda h: h.tensor_tensor(
                xn_out[:, :, :], xn_out[:, :, :], bcast_mid(ps_r, NDT), ALU.mult),
                inc="dv", waits=[("dv", e_n1)])
        else:
            e_b1_ = em.op("t", lambda h: h.matmul(
                ps_m, ones_row_bf[0:1, 0:128], rows_bf[0:1, 0, :], start=True, stop=True),
                inc="pe", waits=[("dv", e_r5)] + list(waits_reuse))
            e_n1 = em.op("v", lambda h: h.tensor_tensor(
                xn_out[:, :, :], xb[:, :, :], bcast_mid(ps_m, NDT), ALU.subtract),
                inc="dv", waits=[("pe", e_b1_)] + list(waits_reuse))
            e_b2_ = em.op("t", lambda h: h.matmul(
                ps_m, ones_row_bf[0:1, 0:128], rows_bf[0:1, 1, :], start=True, stop=True),
                inc="pe", waits=[("dv", e_n1)])
            e_n2 = em.op("v", lambda h: h.tensor_tensor(
                xn_out[:, :, :], xn_out[:, :, :], bcast_mid(ps_m, NDT), ALU.mult),
                inc="dv", waits=[("pe", e_b2_)])
        return {"xn": e_n2, "stats": e_st, "rows": e_r5}

    def phase_A(be, entry):
        gA, gA1 = [], []

        def mk(name, shape, dt, glist):
            g = nc.sbuf_tensor(f"{name}_{be}", shape, dt)
            t_ = g.__enter__()
            glist.append(g)
            return t_

        kT_sb = mk("kT", [128, NDT, T], BF16, gA)

        wq_sb = mk("wq", [128, NDT, D], BF16, gA1)
        wk_sb = mk("wk", [128, NDT, D], BF16, gA1)
        wv_sb = mk("wv", [128, NDT, D], BF16, gA1)
        bvbc = mk("bvbc", [128, D], F32, gA1)
        bv_row = mk("bv_row", [1, D], F32, gA1)
        xb_c = mk("xb", [128, 2, NDT, TC], BF16, gA1)
        xsq_c = mk("xsq", [128, NDT, TC], BF16, gA1)
        xn_c = mk("xn", [128, 2, NDT, TC], BF16, gA1)
        qo_c = mk("qo", [128, 2, NDT, TC], BF16, gA1)
        vo_c = mk("vo", [128, 2, 4, D], BF16, gA1)
        rows_f = mk("rowsf", [1, 2, TC], F32, gA1)
        rows_b = mk("rowsb", [1, 2, TC], BF16, gA1)

        psA_g = nc.psum_tensor(f"psA_{be}", [128, 4, 512], F32)
        psSt_g = nc.psum_tensor(f"psSt_{be}", [1, 2, TC], F32)
        psBc_g = nc.psum_tensor(f"psBc_{be}", [128, 2, TC], F32)
        psA, psSt, psBc = psA_g.__enter__(), psSt_g.__enter__(), psBc_g.__enter__()

        if be == 0:
            e_w = em.op("g", lambda h: [
                h.dma_start(bv_row[0:1, :], bv_r[:, :]),
                h.dma_start(wq_sb[:, :, :], wqT[:, :, :]),
                h.dma_start(wk_sb[:, :, :], wkT[:, :, :]),
                h.dma_start(wv_sb[:, :, :], wvT[:, :, :])],
                inc="d_aw", waits=list(entry) + [("d_c0", e_consts)], ndma=4)
        else:
            e_w = em.op("y", lambda h: [
                h.dma_start(bv_row[0:1, :], bv_r[:, :]),
                h.dma_start(wq_sb[:, :, :], wqkvb_scr[:, 0, :, :]),
                h.dma_start(wk_sb[:, :, :], wqkvb_scr[:, 1, :, :]),
                h.dma_start(wv_sb[:, :, :], wqkvb_scr[:, 2, :, :])],
                inc="d_aw", waits=list(entry) + [("d_awb", 16 * 3)], ndma=4)

        e_bv = 0
        for cc in range(2):
            e_pe = em.op("t", lambda h, cc=cc: h.matmul(
                psBc[:, 0, :], ones_row_f32[0:1, :], bv_row[0:1, cc * 512:(cc + 1) * 512],
                start=True, stop=True), inc="pe",
                waits=[("d_aw", e_w), ("dv", e_bv), ("dv", e_ms)] + list(entry))
            e_bv = em.op("v", lambda h, cc=cc: h.tensor_copy(
                bvbc[:, cc * 512:(cc + 1) * 512], psBc[:, 0, :]), inc="dv",
                waits=[("pe", e_pe)])

        e_xdma = [0] * NCH
        e_st_a = [0] * NCH
        e_sq_a = [0] * NCH
        e_r3_a = [0] * NCH
        e_r5_a = [0] * NCH
        e_bc_a = [0] * NCH
        e_xn_a = [0] * NCH
        e_qkv_last = [0] * NCH
        e_qdma = [0] * NCH
        e_vdma = [0] * NCH
        bank_hist = []
        inv = 1.0 / D

        def a1_xdma(c):
            if c >= NCH:
                return
            p = c % 2
            xw = list(entry) if c < 2 else [("pe", e_st_a[c - 2]),
                                            ("dv", e_xn_a[c - 2])]
            e_xdma[c] = em.op("g", lambda h, c=c, p=p: h.dma_start(
                xb_c[:, p, :, :],
                xt[be].rearrange("(a q) t -> q a t", q=128)[:, :, c * TC:(c + 1) * TC]),
                inc="d_ax%d" % p, waits=xw)
            if be == 0:
                emit_wcast_slab(c)

        def a1_sq(c):
            if c >= NCH:
                return
            p = c % 2
            sqw = [("d_ax%d" % p, e_xdma[c])]
            if c >= 1:
                sqw.append(("pe", e_st_a[c - 1]))
            e_sq_a[c] = em.op("s", lambda h, p=p: h.square(
                xsq_c[:, :, :], xb_c[:, p, :, :]), inc="ac", waits=sqw)

        def a1_strows(c):
            if c >= NCH:
                return
            p = c % 2
            stw = [("d_ax%d" % p, e_xdma[c]), ("ac", e_sq_a[c])]
            if c >= 1:
                stw.append(("dv", e_r3_a[c - 1]))

            def stats(h, p=p):
                ins = None
                for dt in range(NDT):
                    h.matmul(psSt[0:1, 0, :], ones_col_bf[:, 0:1], xb_c[:, p, dt, :],
                             start=(dt == 0), stop=(dt == NDT - 1),
                             skip_group_check=True)
                    ins = h.matmul(psSt[0:1, 1, :], ones_col_bf[:, 0:1],
                                   xsq_c[:, dt, :], start=(dt == 0),
                                   stop=(dt == NDT - 1), skip_group_check=True)
                return ins
            e_st_a[c] = em.op("t", stats, inc="pe", waits=stw)
            rw = [("pe", e_st_a[c])]
            if c >= 1:
                rw.append(("pe", e_bc_a[c - 1]))
            e_r1 = em.op("v", lambda h: h.tensor_scalar_mul(
                rows_b[0:1, 0, :], psSt[0:1, 0, :], inv), inc="dv", waits=rw)
            e_r2 = em.op("v", lambda h: h.tensor_mul(
                rows_f[0:1, 0, :], rows_b[0:1, 0, :], rows_b[0:1, 0, :]),
                inc="dv", waits=[("dv", e_r1)])
            e_r3_a[c] = em.op("v", lambda h: h.scalar_tensor_tensor(
                rows_f[0:1, 1, :], psSt[0:1, 1, :], inv, rows_f[0:1, 0, :],
                ALU.mult, ALU.subtract), inc="dv", waits=[("dv", e_r2)])
            e_r4 = em.op("s", lambda h: h.activation(
                rows_f[0:1, 1, :], rows_f[0:1, 1, :], AF.Sqrt,
                bias=eps_sb[0:1, 0:1], scale=1.0), inc="ac",
                waits=[("dv", e_r3_a[c]), ("dv", e_ms)])
            e_r5a = em.op("v", lambda h: h.reciprocal(
                rows_f[0:1, 0, :], rows_f[0:1, 1, :]), inc="dv",
                waits=[("ac", e_r4)])
            e_r5_a[c] = em.op("v", lambda h: h.tensor_copy(
                rows_b[0:1, 1, :], rows_f[0:1, 0, :]), inc="dv",
                waits=[("dv", e_r5a)])

        def a1_bcast(c):
            if c >= NCH:
                return
            bw = [("dv", e_r5_a[c])]
            bw.append(("dv", e_bv) if c == 0 else ("dv", e_xn_a[c - 1]))

            def bc(h):
                h.matmul(psBc[:, 0, :], ones_row_bf[0:1, 0:128], rows_b[0:1, 0, :],
                         start=True, stop=True, skip_group_check=True)
                return h.matmul(psBc[:, 1, :], ones_row_bf[0:1, 0:128],
                                rows_b[0:1, 1, :], start=True, stop=True,
                                skip_group_check=True)
            e_bc_a[c] = em.op("t", bc, inc="pe", waits=bw)

        def a1_norm(c):
            if c >= NCH:
                return
            p = c % 2
            nw = [("pe", e_bc_a[c])]
            if c >= 2:
                nw.append(("pe", e_qkv_last[c - 2]))
            e_n1 = em.op("v", lambda h, p=p: h.tensor_tensor(
                xn_c[:, p, :, :], xb_c[:, p, :, :], bcast_mid(psBc[:, 0, :], NDT),
                ALU.subtract), inc="dv", waits=nw)
            e_xn_a[c] = em.op("v", lambda h, p=p: h.tensor_tensor(
                xn_c[:, p, :, :], xn_c[:, p, :, :], bcast_mid(psBc[:, 1, :], NDT),
                ALU.mult), inc="dv", waits=[("dv", e_n1)])

        a1_xdma(0)
        a1_xdma(1)
        a1_sq(0)
        a1_strows(0)
        a1_bcast(0)
        a1_norm(0)

        for c in range(NCH):
            p = c % 2
            base_w = [("d_aw", e_w), ("dv", e_xn_a[c])]
            last_pe = 0

            def grp_waits():
                wts = list(base_w)
                gi = len(bank_hist)
                if gi >= 4:
                    wts.append(bank_hist[gi - 4])
                return wts, gi % 4

            for ot in range(NDT):
                wts, bank = grp_waits()
                def qmm(h, ot=ot, bank=bank, p=p):
                    ins = None
                    for dt in range(NDT):
                        ins = h.matmul(psA[:, bank, :], wq_sb[:, dt, ot * 128:(ot + 1) * 128],
                                       xn_c[:, p, dt, :], start=(dt == 0), stop=(dt == NDT - 1))
                    return ins
                e_pe = em.op("t", qmm, inc="pe", waits=wts)
                last_pe = e_pe
                evw = [("pe", e_pe)]
                if c >= 2:
                    evw.append(("d_aq%d" % p, e_qdma[c - 2]))
                e_ev = em.op("s", lambda h, ot=ot, bank=bank, p=p: h.activation(
                    qo_c[:, p, ot, :], psA[:, bank, :], AF.Identity,
                    bias=bq_sb[:, ot:ot + 1], scale=1.0), inc="ac", waits=evw)
                bank_hist.append(("ac", e_ev))
            e_qdma[c] = em.op("y", lambda h, c=c, p=p: h.dma_start(
                q_scr.rearrange("(a q) t -> q a t", q=128)[:, :, c * TC:(c + 1) * TC],
                qo_c[:, p, :, :]), inc="d_aq%d" % p, waits=[bank_hist[-1]])

            a1_xdma(c + 2)
            a1_sq(c + 1)

            for ot in range(NDT):
                wts, bank = grp_waits()
                def kmm(h, ot=ot, bank=bank, p=p):
                    ins = None
                    for dt in range(NDT):
                        ins = h.matmul(psA[:, bank, :], wk_sb[:, dt, ot * 128:(ot + 1) * 128],
                                       xn_c[:, p, dt, :], start=(dt == 0), stop=(dt == NDT - 1))
                    return ins
                e_pe = em.op("t", kmm, inc="pe", waits=wts)
                last_pe = e_pe
                e_ev = em.op("s", lambda h, ot=ot, bank=bank, c=c: h.activation(
                    kT_sb[:, ot, c * TC:(c + 1) * TC], psA[:, bank, :], AF.Identity,
                    bias=bk_sb[:, ot:ot + 1], scale=1.0), inc="ac", waits=[("pe", e_pe)])
                bank_hist.append(("ac", e_ev))

            a1_strows(c + 1)

            for mt in range(4):
                if mt == 1:
                    a1_bcast(c + 1)
                    a1_norm(c + 1)
                for cc in range(2):
                    wts, bank = grp_waits()
                    def vmm(h, mt=mt, cc=cc, bank=bank, p=p):
                        ins = None
                        for dt in range(NDT):
                            ins = h.matmul(psA[:, bank, :],
                                           xn_c[:, p, dt, mt * 128:(mt + 1) * 128],
                                           wv_sb[:, dt, cc * 512:(cc + 1) * 512],
                                           start=(dt == 0), stop=(dt == NDT - 1))
                        return ins
                    e_pe = em.op("t", vmm, inc="pe", waits=wts)
                    last_pe = e_pe
                    evw = [("pe", e_pe), ("dv", e_bv)]
                    if c >= 2:
                        evw.append(("d_av%d" % p, e_vdma[c - 2]))
                    e_ev = em.op("v", lambda h, mt=mt, cc=cc, bank=bank, p=p: h.tensor_tensor(
                        vo_c[:, p, mt, cc * 512:(cc + 1) * 512], psA[:, bank, :],
                        bvbc[:, cc * 512:(cc + 1) * 512], ALU.add), inc="dv",
                        waits=evw)
                    bank_hist.append(("dv", e_ev))
            e_vdma[c] = em.op("y", lambda h, c=c, p=p: h.dma_start(
                v_scr.rearrange("(a q) o -> q a o", q=128)[:, c * 4:(c + 1) * 4, :],
                vo_c[:, p, :, :]), inc="d_av%d" % p, waits=[bank_hist[-1]])
            e_qkv_last[c] = last_pe

        if be == 0:
            em.op("s", lambda h: [
                h.dma_start(wqkvb_scr[:, 0, :, :], wq_sb[:, :, :]),
                h.dma_start(wqkvb_scr[:, 1, :, :], wk_sb[:, :, :]),
                h.dma_start(wqkvb_scr[:, 2, :, :], wv_sb[:, :, :])],
                inc="d_awb", waits=[("pe", e_qkv_last[NCH - 1])], ndma=3)

        a1_tail = em.all_counts()
        for g in reversed([psA_g, psSt_g, psBc_g]):
            g.__exit__(None, None, None)
        for g in reversed(gA1):
            g.__exit__(None, None, None)

        gA2 = []
        v_sb = mk("v", [128, T // 128, D], BF16, gA2)
        qt_sb = mk("qt", [128, 2, NDT, 128], BF16, gA2)
        xc_sb = mk("xc", [128, 2, NDT, 128], F32, gA2)
        pT = mk("pT", [128, 2, 512], BF16, gA2)
        ob = mk("ob", [128, 2, D], BF16, gA2)
        x1o = mk("x1o", [128, 2, NDT, 128], F32, gA2)
        rtot = mk("rtot", [128, 2], F32, gA2)

        psS_g = nc.psum_tensor(f"psS_{be}", [128, 2, 512], F32)
        psO_g = nc.psum_tensor(f"psO_{be}", [128, 1024], F32)
        psT_g = nc.psum_tensor(f"psT_{be}", [128, 2, 1024], BF16)
        psD_g = nc.psum_tensor(f"psD_{be}", [128, 8], F32)
        psS, psO, psT, psD = (psS_g.__enter__(), psO_g.__enter__(),
                              psT_g.__enter__(), psD_g.__enter__())

        NG = 8
        e_vload = [0] * NG
        e_qt = [0] * (NQT + 2)
        e_xc = [0] * (NQT + 2)
        e_fin = [0] * (NQT + 1)
        e_rcp = [0] * (NQT + 1)
        e_add = [0] * NQT
        e_x1dma = [0] * NQT
        e_slast = [0] * NQT
        bankS = [list(a1_tail), list(a1_tail)]
        bankP = [list(a1_tail), list(a1_tail)]
        bankT = [list(a1_tail), list(a1_tail)]
        g_ob = [[], []]

        def dma_qt(i):
            if i >= NQT:
                return
            pi = i % 2
            wq_ = list(a1_tail) if i < 2 else [("pe", e_slast[i - 2])]
            e_qt[i] = em.op("y", lambda h, i=i, pi=pi: h.dma_start(
                qt_sb[:, pi, :, :],
                q_scr.rearrange("(a q) t -> q a t", q=128)[:, :, i * 128:(i + 1) * 128]),
                inc="d_qt%d" % pi, waits=wq_)

        def dma_xc(i):
            if i >= NQT:
                return
            pi = i % 2
            wx_ = list(a1_tail) if i < 2 else [("dv", e_add[i - 2])]
            e_xc[i] = em.op("y", lambda h, i=i, pi=pi: h.dma_start(
                xc_sb[:, pi, :, :],
                xt[be].rearrange("(a q) t -> q a t", q=128)[:, :, i * 128:(i + 1) * 128]),
                inc="d_xc%d" % pi, waits=wx_)

        def emit_sgroup(ti, g):
            p = ti % 2
            b = g % 2
            def smm(h, ti=ti, g=g, b=b, p=p):
                ins = None
                for kk in range(4):
                    kt = 4 * g + kk
                    dodiag = (kt == ti)
                    for ot in range(NDT):
                        ins = h.matmul(
                            psS[:, b, kk * 128:(kk + 1) * 128],
                            kT_sb[:, ot, kt * 128:(kt + 1) * 128],
                            qt_sb[:, p, ot, :],
                            start=(ot == 0), stop=(ot == NDT - 1 and not dodiag))
                    if dodiag:
                        ins = h.matmul(psS[:, b, kk * 128:(kk + 1) * 128],
                                       negeye_sb[:, :], eye_sb[:, :],
                                       start=False, stop=True)
                return ins
            e_s = em.op("t", smm, inc="pe",
                        waits=[("d_qt%d" % p, e_qt[ti])] + list(bankS[b]))
            e_slast[ti] = e_s
            e_ex = em.op("s", lambda h, b=b: h.activation(
                pT[:, b, :], psS[:, b, :], AF.Exp, scale=1.0 / D), inc="ac",
                waits=[("pe", e_s)] + list(bankP[b]))
            bankS[b] = [("ac", e_ex)]
            return e_ex

        def emit_pvgroup(ti, g, e_ex):
            b = g % 2
            def pvg(h, g=g, b=b):
                ins = None
                for kk in range(4):
                    kt = 4 * g + kk
                    lhs = pT[:, b, kk * 128:(kk + 1) * 128]
                    for cc in range(2):
                        h.matmul(psO[:, cc * 512:(cc + 1) * 512], lhs,
                                 v_sb[:, kt, cc * 512:(cc + 1) * 512],
                                 start=(kt == 0), stop=(kt == 31),
                                 skip_group_check=True)
                    ins = h.matmul(psD[:, 0:1], lhs, ones_col_bf[:, 0:1],
                                   start=(kt == 0), stop=(kt == 31),
                                   skip_group_check=True)
                return ins
            pvw = [("ac", e_ex), ("d_vl", e_vload[g])]
            if g == 0:
                pvw += [("dv", e_fin[ti - 1] if ti > 0 else 0),
                        ("dv", e_rcp[ti - 1] if ti > 0 else 0)]
            e_pv = em.op("t", pvg, inc="pe", waits=pvw)
            bankP[b] = [("pe", e_pv)]
            return e_pv

        def emit_otrans(ti):
            p = ti % 2
            last_otr = 0
            for j in range(NDT):
                bp = j % 2
                e_otr = em.op("t", lambda h, j=j, bp=bp, p=p: h.transpose(
                    psT[:, bp, 0:128], ob[:, p, j * 128:(j + 1) * 128], eye_sb[:, :]),
                    inc="pe", waits=[("dv", e_fin[ti])] + list(bankT[bp]))
                last_otr = e_otr
                aw = [("pe", e_otr), ("d_xc%d" % p, e_xc[ti])]
                if ti >= 2:
                    aw.append(("d_x1%d" % p, e_x1dma[ti - 2]))
                last_add = em.op("v", lambda h, j=j, bp=bp, p=p: h.tensor_tensor(
                    x1o[:, p, j, :], psT[:, bp, 0:128], xc_sb[:, p, j, :], ALU.add),
                    inc="dv", waits=aw)
                bankT[bp] = [("dv", last_add)]
            g_ob[p] = [("pe", last_otr)]
            e_add[ti] = last_add
            dst = xr1[be if dbg_scratch else 0]
            e_x1dma[ti] = em.op("y", lambda h, ti=ti, p=p, dst=dst: h.dma_start(
                dst.rearrange("(a q) t -> q a t", q=128)[:, :, ti * 128:(ti + 1) * 128],
                x1o[:, p, :, :]), inc="d_x1%d" % p, waits=[("dv", e_add[ti])])

        dma_qt(0)
        dma_qt(1)
        dma_xc(0)
        dma_xc(1)
        for k in range(NG):
            e_vload[k] = em.op("g", lambda h, k=k: h.dma_start(
                v_sb[:, k * 4:(k + 1) * 4, :],
                v_scr.rearrange("(a q) o -> q a o", q=128)[:, k * 4:(k + 1) * 4, :]),
                inc="d_vl", waits=list(a1_tail))
        for ti in range(NQT):
            p = ti % 2
            ex = [0] * NG
            ex[0] = emit_sgroup(ti, 0)
            ex[1] = emit_sgroup(ti, 1)
            if ti > 0:
                emit_otrans(ti - 1)
                dma_xc(ti + 1)
            e_pvlast = 0
            for g in range(NG):
                e_pvlast = emit_pvgroup(ti, g, ex[g])
                if g + 2 < NG:
                    ex[g + 2] = emit_sgroup(ti, g + 2)
            e_rcp[ti] = em.op("v", lambda h, p=p: h.reciprocal(
                rtot[:, p:p + 1], psD[:, 0:1]), inc="dv",
                waits=[("pe", e_pvlast)])
            e_fin[ti] = em.op("v", lambda h, p=p: h.tensor_scalar_mul(
                ob[:, p, :], psO[:, :], rtot[:, p:p + 1]), inc="dv",
                waits=[("dv", e_rcp[ti])] + list(g_ob[p]))
            dma_qt(ti + 2)
        emit_otrans(NQT - 1)

        for g in reversed(gA2):
            g.__exit__(None, None, None)
        for g in reversed([psS_g, psO_g, psT_g, psD_g]):
            g.__exit__(None, None, None)
        for g in reversed(gA):
            g.__exit__(None, None, None)
        return em.all_counts()

    def phase_F(be, app, xin, xout, entry):
        gF = []

        def mk(name, shape, dt):
            g = nc.sbuf_tensor(f"{name}_{be}_{app}", shape, dt)
            t_ = g.__enter__()
            gF.append(g)
            return t_

        w1r = mk("w1r", [128, NDT, NDT, 512], BF16)
        g_sb = mk("gF", [128, NJT, TC], BF16)
        xtb = mk("xtbF", [128, 2, NDT, TC], F32)
        xn = mk("xnF", [128, NDT, TC], BF16)
        xbs = mk("xbsF", [128, 2, TC], BF16)
        xqs = mk("xqsF", [128, 2, TC], BF16)
        w2blk = mk("w2blk", [128, 2, NJT, 128], BF16)
        w2bld = mk("w2bld", [128, 2, NJT, 128], BF16)
        ltr = mk("ltrF", [128, 2, 2, D], BF16)
        lt_e = mk("lteF", [8, NDT, 128], BF16)
        lt_o = mk("ltoF", [8, NDT, 128], BF16)
        xn_sc = mk("xnscF", [128, NDT, 8], BF16)
        rows_f = mk("rowsfF", [1, 2, TC], F32)
        rows_b = mk("rowsbF", [1, 2, TC], BF16)

        psh_g = nc.psum_tensor(f"psh_{be}_{app}", [128, 2, 512], F32)
        pso_g = nc.psum_tensor(f"pso_{be}_{app}", [128, 2, 512], F32)
        psW_g = nc.psum_tensor(f"psW_{be}_{app}", [128, 2, 512], F32)
        psbc_g = nc.psum_tensor(f"psbc_{be}_{app}", [128, 512], F32)
        psLt_g = nc.psum_tensor(f"psLt_{be}_{app}", [128, 2, 512], BF16)
        psh, pso, psW, psbc, psLt = (psh_g.__enter__(), pso_g.__enter__(),
                                     psW_g.__enter__(), psbc_g.__enter__(),
                                     psLt_g.__enter__())

        def flat2(sl, n):
            return bass.AP(sl.tensor, sl.offset, [list(sl.ap[0]), [1, n]])

        e_xin = [0] * NCH
        e_xout = [0] * NCH

        e_w1b = [0] * NDT
        for jc in range(NDT):
            e_w1b[jc] = em.op("g", lambda h, jc=jc: h.dma_start(
                w1r[:, jc, :, :], w1b_scr[:, jc, :, :]), inc="d_w1b",
                waits=list(entry) + [("d_c1", e_w2b_final)])

        def dma_x(c):
            if c >= NCH:
                return
            p = c % 2
            xw = list(entry) if c < 2 else [("d_fo%d" % p, e_xout[c - 2])]
            e_xin[c] = em.op("y", lambda h, c=c, p=p: h.dma_start(
                xtb[:, p, :, :],
                xin.rearrange("(a q) t -> q a t", q=128)[:, :, c * TC:(c + 1) * TC]),
                inc="d_fx%d" % p, waits=xw)

        st = {"r3": 0, "r5": 0, "bm": 0, "br": 0, "mul": 0, "sub": 0,
              "stats": 0, "mm_prev": [0] * NDT}

        def emit_stats(c, psw_free):
            p = c % 2
            e_mm = [0] * NDT
            for dt in range(NDT):
                s = dt % 2
                cw = [("d_fx%d" % p, e_xin[c])]
                if dt >= 2:
                    cw.append(("pe", e_mm[dt - 2]))
                elif st["mm_prev"][NDT - 2 + dt]:
                    cw.append(("pe", st["mm_prev"][NDT - 2 + dt]))
                e_cast = em.op("v", lambda h, dt=dt, s=s, p=p: h.tensor_copy(
                    xbs[:, s, :], xtb[:, p, dt, :]), inc="dv", waits=cw)
                e_sq = em.op("s", lambda h, s=s: h.square(
                    xqs[:, s, :], xbs[:, s, :]), inc="ac",
                    waits=[("dv", e_cast)] + cw[1:])

                def stmm(h, dt=dt, s=s):
                    h.matmul(psW[0:1, 0, :], ones_col_bf[:, 0:1], xbs[:, s, :],
                             start=(dt == 0), stop=(dt == NDT - 1),
                             skip_group_check=True)
                    return h.matmul(psW[0:1, 1, :], ones_col_bf[:, 0:1],
                                    xqs[:, s, :], start=(dt == 0),
                                    stop=(dt == NDT - 1), skip_group_check=True)
                e_mm[dt] = em.op("t", stmm, inc="pe",
                                 waits=[("dv", e_cast), ("ac", e_sq)] +
                                       (psw_free if dt == 0 else []))
            st["mm_prev"] = e_mm
            st["stats"] = e_mm[NDT - 1]

        def emit_rows(c):
            inv = 1.0 / D
            rw = [("pe", st["stats"])]
            if st["br"]:
                rw.append(("pe", st["br"]))
            e_r1 = em.op("v", lambda h: h.tensor_scalar_mul(
                rows_b[0:1, 0, :], psW[0:1, 0, :], inv), inc="dv", waits=rw)
            e_r2 = em.op("v", lambda h: h.tensor_mul(
                rows_f[0:1, 0, :], rows_b[0:1, 0, :], rows_b[0:1, 0, :]),
                inc="dv", waits=[("dv", e_r1)])
            e_r3 = em.op("v", lambda h: h.scalar_tensor_tensor(
                rows_f[0:1, 1, :], psW[0:1, 1, :], inv, rows_f[0:1, 0, :],
                ALU.mult, ALU.subtract), inc="dv", waits=[("dv", e_r2)])
            e_r4 = em.op("s", lambda h: h.activation(
                rows_f[0:1, 1, :], rows_f[0:1, 1, :], AF.Sqrt,
                bias=eps_sb[0:1, 0:1], scale=1.0), inc="ac",
                waits=[("dv", e_r3), ("dv", e_ms)])
            e_r5a = em.op("v", lambda h: h.reciprocal(
                rows_f[0:1, 0, :], rows_f[0:1, 1, :]), inc="dv",
                waits=[("ac", e_r4)])
            e_r5 = em.op("v", lambda h: h.tensor_copy(
                rows_b[0:1, 1, :], rows_f[0:1, 0, :]), inc="dv",
                waits=[("dv", e_r5a)])
            st["r3"] = e_r3
            st["r5"] = e_r5

        def emit_bcast_m(c):
            bw = [("dv", st["r5"])]
            if st["mul"]:
                bw.append(("dv", st["mul"]))
            e_bm = em.op("t", lambda h: h.matmul(
                psbc[:, :], ones_row_bf[0:1, 0:128], rows_b[0:1, 0, :],
                start=True, stop=True), inc="pe", waits=bw)
            p = c % 2
            e_sub = em.op("v", lambda h, p=p: h.tensor_tensor(
                xn[:, :, :], xtb[:, p, :, :], bcast_mid(psbc[:, :], NDT),
                ALU.subtract), inc="dv", waits=[("pe", e_bm)])
            st["bm"] = e_bm
            st["sub"] = e_sub

        def emit_bcast_r(c):
            e_br = em.op("t", lambda h: h.matmul(
                psbc[:, :], ones_row_bf[0:1, 0:128], rows_b[0:1, 1, :],
                start=True, stop=True), inc="pe", waits=[("dv", st["sub"])])
            e_mul = em.op("v", lambda h: h.tensor_tensor(
                xn[:, :, :], xn[:, :, :], bcast_mid(psbc[:, :], NDT),
                ALU.mult), inc="dv", waits=[("pe", e_br)])
            st["br"] = e_br
            st["mul"] = e_mul

        dma_x(0)
        dma_x(1)
        emit_stats(0, list(entry))
        emit_rows(0)
        emit_bcast_m(0)
        emit_bcast_r(0)
        e_mul0 = st["mul"]

        e_sc = [0] * NDT
        for dt in range(NDT):
            sl = xn[:, dt, 0:TC]
            odd_ap = bass.AP(sl.tensor, sl.offset + 2, [list(sl.ap[0]), [2, 8]])
            e_sc[dt] = em.op("v", lambda h, dt=dt, odd_ap=odd_ap: h.tensor_scalar_mul(
                xn_sc[:, dt, :], odd_ap, scv128[:, 0:1]), inc="dv",
                waits=[("dv", e_mul0), ("d_c0", e_consts)])
        e_ltcp = 0
        e_lttr = 0
        for dt in range(NDT):
            sl = xn[:, dt, 0:TC]
            ev_ap = bass.AP(sl.tensor, sl.offset + 1, [list(sl.ap[0]), [2, 8]])
            for which in (0, 1):
                tw = [("dv", e_mul0 if which == 0 else e_sc[dt])]
                if e_ltcp:
                    tw.append(("dv", e_ltcp))
                tw += list(entry)
                if which == 0:
                    e_lttr = em.op("t", lambda h, ev_ap=ev_ap: h.transpose(
                        psLt[0:8, 0, 0:128], ev_ap, eye_sb[:, :]), inc="pe",
                        waits=tw)
                    dstt = lt_e
                else:
                    e_lttr = em.op("t", lambda h, dt=dt: h.transpose(
                        psLt[0:8, 0, 0:128], xn_sc[:, dt, :], eye_sb[:, :]),
                        inc="pe", waits=tw)
                    dstt = lt_o
                e_ltcp = em.op("v", lambda h, dt=dt, dstt=dstt: h.tensor_copy(
                    dstt[0:8, dt, :], psLt[0:8, 0, 0:128]), inc="dv",
                    waits=[("pe", e_lttr)])

        def scatA(h):
            out = []
            for i in range(4):
                b = 32 * (i >> 1)
                ia = i & 1
                out.append(h.dma_start(ltr[b:b + 1, ia, 0, :], lt_e[i:i + 1, :, :]))
                out.append(h.dma_start(ltr[b:b + 1, ia, 1, :], lt_o[i:i + 1, :, :]))
            return out
        e_sca = em.op("g", scatA, inc="d_lt", waits=[("dv", e_ltcp)], ndma=8)

        bank_h = [list(entry), list(entry)]
        bank_o = [list(entry), list(entry)]
        e_w1add = [0] * NDT
        e_w1mm = [0] * NDT
        prev_add = [0, 0]
        e_glast = 0

        def emit_w1build(jc):
            i = jc >> 1
            b = 32 * (i >> 1)
            ia = i & 1
            e_add = 0
            last = 0
            for dt in range(NDT):
                bp = dt % 2
                def w1mm(h, jc=jc, dt=dt, b=b, ia=ia, bp=bp):
                    return h.matmul(
                        psW[:, bp, :],
                        ltr[b:b + 1, ia, 1, dt * 128:(dt + 1) * 128],
                        ltr[b:b + 1, ia, 0, (jc & 1) * 512:((jc & 1) + 1) * 512],
                        start=True, stop=True)
                ww = [("d_lt", e_sca), ("dv", st["r3"])]
                if prev_add[bp]:
                    ww.append(("dv", prev_add[bp]))
                last = em.op("t", w1mm, inc="pe", waits=ww)
                e_add = em.op("v", lambda h, jc=jc, dt=dt, bp=bp: h.tensor_tensor(
                    w1r[:, jc, dt, :], psW[:, bp, :], w1r[:, jc, dt, :], ALU.add),
                    inc="dv", waits=[("pe", last), ("d_w1b", e_w1b[jc])])
                prev_add[bp] = e_add
            e_w1add[jc] = e_add
            e_w1mm[jc] = last

        def emit_hgroup(c, jt, extra=()):
            bp = jt % 2
            jc = jt >> 2
            jjo = (jt & 3) * 128
            def hmm(h, jc=jc, jjo=jjo, bp=bp):
                ins = None
                for dt in range(NDT):
                    ins = h.matmul(psh[:, bp, :], w1r[:, jc, dt, jjo:jjo + 128],
                                   xn[:, dt, :], start=(dt == 0),
                                   stop=(dt == NDT - 1))
                return ins
            e_h = em.op("t", hmm, inc="pe",
                        waits=[("dv", st["mul"])] + list(bank_h[bp]) + list(extra))
            e_g = em.op("s", lambda h, jt=jt, bp=bp: h.activation(
                g_sb[:, jt, :], psh[:, bp, :], AF.Gelu,
                bias=b1_sb[:, jt:jt + 1], scale=1.0), inc="ac",
                waits=[("pe", e_h)])
            bank_h[bp] = [("ac", e_g)]
            return e_h, e_g

        for jc in range(NDT):
            emit_w1build(jc)
            for jj in range(4):
                jt = jc * 4 + jj
                e_hlast, e_glast = emit_hgroup(0, jt, extra=[("dv", e_w1add[jc])])

        emit_stats(1, [("dv", e_w1add[NDT - 1])])
        emit_rows(1)

        def scatB(h):
            out = []
            for i in range(4):
                b = 32 * (i >> 1)
                ia = i & 1
                se = lt_e[4 + i:5 + i, 0:NDT, 0:128]
                de = ltr[b:b + 1, ia, 0, 0:D]
                for r in range(4):
                    src_e = bass.AP(se.tensor, se.offset + r,
                                    [list(se.ap[0]), [128, 8], [4, 32]])
                    dst_e = bass.AP(de.tensor, de.offset + r * 256,
                                    [list(de.ap[0]), [32, 8], [1, 32]])
                    out.append(h.dma_start(dst_e, src_e,
                                           allow_slow_non_contiguous=True))
                out.append(h.dma_start(ltr[b:b + 1, ia, 1, :], lt_o[4 + i:5 + i, :, :]))
            return out
        e_scb = em.op("g", scatB, inc="d_lt2",
                      waits=[("pe", e_w1mm[NDT - 1])], ndma=20)

        e_slab = [0] * NDT
        e_w2addl = 0
        prev_add2 = [("dv", e_w1add[NDT - 1]), ("dv", e_w1add[NDT - 1])]
        e_w2bb = [0] * NDT
        e_addl_ot = [0] * NDT

        def dma_slab_in(ot):
            if ot >= NDT:
                return
            bw = [("d_c1", e_w2b_final)]
            if ot >= 2:
                bw.append(("d_w2e", e_slab[ot - 2]))
            e_w2bb[ot] = em.op("g", lambda h, ot=ot: h.dma_start(
                w2bld[:, ot % 2, :, :], w2b_scr[:, ot, :, :]), inc="d_w2bb",
                waits=bw)

        dma_slab_in(0)
        dma_slab_in(1)
        for ot in range(NDT):
            i = ot >> 1
            b = 32 * (i >> 1)
            ia = i & 1
            for jtq in range(NDT):
                bp = jtq % 2
                def b2mm(h, ot=ot, jtq=jtq, b=b, ia=ia, bp=bp):
                    ins = None
                    for jj in range(4):
                        jt = jtq * 4 + jj
                        k = jt >> 3
                        ins = h.matmul(
                            psW[:, bp, jj * 128:(jj + 1) * 128],
                            ltr[b:b + 1, ia, 1, (jt & 7) * 128:((jt & 7) + 1) * 128],
                            ltr[b:b + 1, ia, 0,
                                k * 256 + (ot & 1) * 128: k * 256 + (ot & 1) * 128 + 128],
                            start=True, stop=True)
                    return ins
                e_m2 = em.op("t", b2mm, inc="pe",
                             waits=[("d_lt2", e_scb), ("dv", st["r3"]), prev_add2[bp]])
                sl = w2bld[:, ot % 2, jtq * 4:(jtq + 1) * 4, :]
                e_a2 = em.op("v", lambda h, sl=sl, bp=bp: h.tensor_tensor(
                    flat2(sl, 512), psW[:, bp, :], flat2(sl, 512), ALU.add),
                    inc="dv", waits=[("pe", e_m2), ("d_w2bb", e_w2bb[ot])])
                prev_add2[bp] = ("dv", e_a2)
                e_w2addl = e_a2
            e_addl_ot[ot] = e_w2addl
            e_slab[ot] = em.op("y", lambda h, ot=ot: h.dma_start(
                w2_eff[:, ot, :, :], w2bld[:, ot % 2, :, :]), inc="d_w2e",
                waits=[("dv", e_w2addl)])
            dma_slab_in(ot + 2)

        e_w2d = {}
        e_omm_arr = [0] * NDT
        e_omm_prev = [0] * NDT

        def dma_w2(c, ot):
            if ot >= NDT:
                return
            bpw = ot % 2
            ww = [("d_w2e", e_slab[ot])]
            if ot >= 2:
                ww.append(("pe", e_omm_arr[ot - 2]))
            elif c >= 1 and e_omm_prev[NDT - 2 + ot]:
                ww.append(("pe", e_omm_prev[NDT - 2 + ot]))
            e_w2d[(c, ot)] = em.op("g", lambda h, ot=ot, bpw=bpw: h.dma_start(
                w2blk[:, bpw, :, :], w2_eff[:, ot, :, :]), inc="d_fw2%d" % bpw,
                waits=ww)

        for c in range(NCH):
            p = c % 2
            if c > 0:
                for jt in range(NJT):
                    e_hlast, e_glast = emit_hgroup(c, jt)
                if c < NCH - 1:
                    psw_free = [("dv", e_w2addl)] if c == 1 else [("dv", st["r3"])]
                    emit_stats(c + 1, psw_free)
                    emit_rows(c + 1)
            dma_w2(c, 0)
            dma_w2(c, 1)
            e_oev_last = 0
            for ot in range(NDT):
                bp = ot % 2
                def omm(h, c=c, ot=ot, bp=bp):
                    for jt in range(NJT):
                        h.matmul(pso[:, bp, :], w2blk[:, ot % 2, jt, :],
                                 g_sb[:, jt, :], start=(jt == 0), stop=False)
                    return h.matmul(pso[:, bp, :], b2f[0:1, ot * 128:(ot + 1) * 128],
                                    ones_row_bf[0:1, :], start=False, stop=True)
                e_o = em.op("t", omm, inc="pe",
                            waits=[("ac", e_glast), ("d_fw2%d" % (ot % 2), e_w2d[(c, ot)]),
                                   ("d_c0", e_consts)] + list(bank_o[bp]))
                e_omm_arr[ot] = e_o
                dma_w2(c, ot + 2)
                e_oev = em.op("v", lambda h, ot=ot, bp=bp, p=p: h.tensor_tensor(
                    xtb[:, p, ot, :], pso[:, bp, :], xtb[:, p, ot, :], ALU.add),
                    inc="dv", waits=[("pe", e_o)])
                bank_o[bp] = [("dv", e_oev)]
                e_oev_last = e_oev
                if c < NCH - 1:
                    if ot == 1:
                        emit_bcast_m(c + 1)
                    if ot == 3:
                        emit_bcast_r(c + 1)
            e_omm_prev = list(e_omm_arr)
            e_xout[c] = em.op("y", lambda h, c=c, p=p: h.dma_start(
                xout.rearrange("(a q) t -> q a t", q=128)[:, :, c * TC:(c + 1) * TC],
                xtb[:, p, :, :]), inc="d_fo%d" % p, waits=[("dv", e_oev_last)])
            dma_x(c + 2)

        for g in reversed(gF):
            g.__exit__(None, None, None)
        for g in reversed([psh_g, pso_g, psW_g, psbc_g, psLt_g]):
            g.__exit__(None, None, None)
        return em.all_counts()

    def phase_B(be, xin, xout, entry):
        gB = []

        def mk(name, shape, dt):
            g = nc.sbuf_tensor(f"{name}B_{be}", shape, dt)
            t_ = g.__enter__()
            gB.append(g)
            return t_

        wa_sb = mk("wa", [128, NDT, 3 * D], BF16)
        wp_sb = mk("wp", [128, NDT, D], BF16)
        xtb = mk("xtb", [128, 3, NDT, TC], F32)
        xbtb = mk("xb", [128, NDT, TC], BF16)
        xsq = mk("xsq", [128, NDT, TC], BF16)
        xn_tb = mk("xn", [128, 2, NDT, TC], BF16)
        qT2 = mk("qT2", [128, 8, TC], BF16)
        k16bd = mk("k16bd", [128, 8, 32], BF16)
        v16p = mk("v16p", [128, 8, 16], F32)
        vT16 = mk("vT16", [16, 8, 128], BF16)
        vexpA = mk("vexpA", [128, 1024], BF16)
        vexpB = mk("vexpB", [128, 1024], BF16)
        pB = mk("pB", [128, 2, TC], BF16)
        p2n = mk("p2n", [128, 2, TC], BF16)
        recT = mk("recT", [16, TC], F32)
        recTb = mk("recTb", [16, TC], BF16)
        aT = mk("aT", [128, NDT, TC], BF16)
        rows_f = mk("rowsf", [1, 2, TC], F32)
        rows_b = mk("rowsb", [1, 2, TC], BF16)

        ps1_g = nc.psum_tensor(f"ps1B_{be}", [128, 2, TC], F32)
        ps2_g = nc.psum_tensor(f"ps2B_{be}", [128, 2, TC], F32)
        psOT_g = nc.psum_tensor(f"psOTB_{be}", [128, 2, TC], F32)
        psBc_g = nc.psum_tensor(f"psBcB_{be}", [128, 2, TC], F32)
        ps1, ps2, psOT, psBc = (ps1_g.__enter__(), ps2_g.__enter__(),
                                psOT_g.__enter__(), psBc_g.__enter__())

        e_wbq = em.op("g", lambda h: h.dma_start(
            wa_sb[:, :, 0:D], waT[:, :, 0:D]), inc="d_bw", waits=list(entry))
        e_wbkv = em.op("g", lambda h: h.dma_start(
            wa_sb[:, :, D:3 * D], waT[:, :, D:3 * D]), inc="d_bw")
        e_wbp = em.op("g", lambda h: h.dma_start(
            wp_sb[:, :, :], wpT[:, :, :]), inc="d_bw")

        e_xin = [0] * NCH
        e_xout = [0] * NCH

        def dma_x(c, w0=None):
            if c >= NCH:
                return
            xw = list(entry) if c < 3 else []
            if w0:
                xw += w0
            e_xin[c] = em.op("y", lambda h, c=c: h.dma_start(
                xtb[:, c % 3, :, :],
                xin.rearrange("(a q) t -> q a t", q=128)[:, :, c * TC:(c + 1) * TC]),
                inc="d_bx", waits=xw)

        inv = 1.0 / D
        st = {k: [0] * (NCH + 2) for k in
              ("cast", "sq", "stats", "r3", "r5", "bm", "br", "sub", "mul")}
        e_qpj = [0] * (NCH + 1)
        e_qev = [[0] * 8 for _ in range(NCH)]
        e_lg = [[0, 0] for _ in range(NCH)]
        e_p2n = [[0, 0] for _ in range(NCH + 1)]
        e_pv = [[0] * 8 for _ in range(NCH)]
        e_aev_s = [0] * (NCH + 1)
        e_aev_v = [0] * (NCH + 1)
        e_rcp = [0] * NCH
        e_res = [[0] * 8 for _ in range(NCH + 1)]
        e_hs = [0] * NCH
        e_pjl = [0] * (NCH + 1)

        def emit_cast(c, extra=()):
            if c >= NCH:
                return
            cw = [("d_bx", e_xin[c])] + list(extra)
            if c >= 1:
                cw.append(("dv", st["sub"][c - 1]))
            st["cast"][c] = em.op("v", lambda h, c=c: h.tensor_copy(
                xbtb[:, :, :], xtb[:, c % 3, :, :]), inc="dv", waits=cw)
            sqw = [("dv", st["cast"][c])]
            if c >= 1:
                sqw.append(("pe", st["stats"][c - 1]))
            st["sq"][c] = em.op("s", lambda h: h.square(
                xsq[:, :, :], xbtb[:, :, :]), inc="ac", waits=sqw)

        def emit_stats(c, extra=()):
            if c >= NCH:
                return
            sw = [("dv", st["cast"][c]), ("ac", st["sq"][c])] + list(extra)
            if c >= 1:
                sw += [("dv", st["sub"][c - 1]), ("dv", st["mul"][c - 1])]

            def stmm(h):
                ins = None
                for dt in range(NDT):
                    h.matmul(psBc[0:1, 0, :], ones_col_bf[:, 0:1], xbtb[:, dt, :],
                             start=(dt == 0), stop=(dt == NDT - 1),
                             skip_group_check=True)
                    ins = h.matmul(psBc[0:1, 1, :], ones_col_bf[:, 0:1],
                                   xsq[:, dt, :], start=(dt == 0),
                                   stop=(dt == NDT - 1), skip_group_check=True)
                return ins
            st["stats"][c] = em.op("t", stmm, inc="pe", waits=sw)

        def emit_rows(c):
            if c >= NCH:
                return
            rw = [("pe", st["stats"][c])]
            if c >= 1:
                rw.append(("pe", st["bm"][c - 1]))
            e_r1 = em.op("v", lambda h: h.tensor_scalar_mul(
                rows_b[0:1, 0, :], psBc[0:1, 0, :], inv), inc="dv", waits=rw)
            e_r2 = em.op("v", lambda h: h.tensor_mul(
                rows_f[0:1, 0, :], rows_b[0:1, 0, :], rows_b[0:1, 0, :]),
                inc="dv", waits=[("dv", e_r1)])
            st["r3"][c] = em.op("v", lambda h: h.scalar_tensor_tensor(
                rows_f[0:1, 1, :], psBc[0:1, 1, :], inv, rows_f[0:1, 0, :],
                ALU.mult, ALU.subtract), inc="dv", waits=[("dv", e_r2)])
            e_r4 = em.op("s", lambda h: h.activation(
                rows_f[0:1, 1, :], rows_f[0:1, 1, :], AF.Sqrt,
                bias=eps_sb[0:1, 0:1], scale=1.0), inc="ac",
                waits=[("dv", st["r3"][c]), ("dv", e_ms)])
            e_r5a = em.op("v", lambda h: h.reciprocal(
                rows_f[0:1, 0, :], rows_f[0:1, 1, :]), inc="dv",
                waits=[("ac", e_r4)])
            r5w = [("dv", e_r5a)]
            if c >= 1:
                r5w.append(("pe", st["br"][c - 1]))
            st["r5"][c] = em.op("v", lambda h: h.tensor_copy(
                rows_b[0:1, 1, :], rows_f[0:1, 0, :]), inc="dv", waits=r5w)

        def emit_bcasts(c):
            if c >= NCH:
                return
            st["bm"][c] = em.op("t", lambda h: h.matmul(
                psBc[:, 0, :], ones_row_bf[0:1, 0:128], rows_b[0:1, 0, :],
                start=True, stop=True, skip_group_check=True), inc="pe",
                waits=[("dv", st["r5"][c])])
            st["br"][c] = em.op("t", lambda h: h.matmul(
                psBc[:, 1, :], ones_row_bf[0:1, 0:128], rows_b[0:1, 1, :],
                start=True, stop=True, skip_group_check=True), inc="pe",
                waits=[("pe", st["bm"][c])])

        def emit_submul(c):
            if c >= NCH:
                return
            sw = [("pe", st["bm"][c])]
            if c >= 2:
                sw.append(("pe", e_qpj[c - 2]))
            st["sub"][c] = em.op("v", lambda h, c=c: h.tensor_tensor(
                xn_tb[:, c % 2, :, :], xbtb[:, :, :], bcast_mid(psBc[:, 0, :], NDT),
                ALU.subtract), inc="dv", waits=sw)
            st["mul"][c] = em.op("v", lambda h, c=c: h.tensor_tensor(
                xn_tb[:, c % 2, :, :], xn_tb[:, c % 2, :, :],
                bcast_mid(psBc[:, 1, :], NDT), ALU.mult), inc="dv",
                waits=[("pe", st["br"][c]), ("dv", st["sub"][c])])

        dma_x(0)
        dma_x(1)
        dma_x(2)
        emit_cast(0, extra=list(entry))
        emit_stats(0, extra=list(entry))
        emit_rows(0)
        emit_bcasts(0)
        emit_submul(0)
        emit_cast(1)
        emit_stats(1)
        emit_rows(1)

        e_z = em.op("v", lambda h: (h.memset(k16bd[:, :, :], 0.0),
                                    h.memset(vexpA[:, :], 0.0),
                                    h.memset(vexpB[:, :], 0.0))[-1],
                    inc="dv", waits=list(entry))
        bank1 = [list(entry), list(entry)]
        e_kv_ev = 0
        for which, wofs in (("k", D), ("v", 2 * D)):
            for p_ in range(8):
                bp = p_ % 2

                def kvmm(h, p_=p_, wofs=wofs, bp=bp):
                    ins = None
                    for dt in range(NDT):
                        ins = h.matmul(ps1[:, bp, 0:16],
                                       wa_sb[:, dt, wofs + p_ * 128:wofs + (p_ + 1) * 128],
                                       xn_tb[:, 0, dt, 0:16],
                                       start=(dt == 0), stop=(dt == NDT - 1))
                    return ins
                e_kv = em.op("t", kvmm, inc="pe",
                             waits=[("dv", st["mul"][0]), ("d_bw", e_wbkv)] +
                                   list(bank1[bp]))
                if which == "k":
                    def kev(h, p_=p_, bp=bp):
                        h.copy(k16bd[0:64, p_, 0:16], ps1[0:64, bp, 0:16])
                        return h.copy(k16bd[64:128, p_, 16:32], ps1[64:128, bp, 0:16])
                    e_kv_ev = em.op("s", kev, inc="ac",
                                    waits=[("pe", e_kv), ("dv", e_z)])
                else:
                    e_kv_ev = em.op("s", lambda h, p_=p_, bp=bp: h.copy(
                        v16p[:, p_, :], ps1[:, bp, 0:16]), inc="ac",
                        waits=[("pe", e_kv)])
                bank1[bp] = [("ac", e_kv_ev)]
        e_k16_ready = e_kv_ev
        bank2 = [list(entry), list(entry)]
        e_vt = 0
        for p_ in range(8):
            bp = p_ % 2
            e_vtr = em.op("t", lambda h, p_=p_, bp=bp: h.transpose(
                ps2[0:16, bp, 0:128], v16p[:, p_, 0:16], eye32_sb[:, :]),
                inc="pe", waits=[("ac", e_k16_ready), ("d_c0", e_consts)] +
                      list(bank2[bp]))
            e_vt = em.op("v", lambda h, p_=p_, bp=bp: h.tensor_copy(
                vT16[0:16, p_, :], ps2[0:16, bp, 0:128]), inc="dv",
                waits=[("pe", e_vtr)])
            bank2[bp] = [("dv", e_vt)]

        def scat(h):
            out = []
            for hh in range(16):
                dstv = vexpA if hh < 8 else vexpB
                prow = (hh % 8) * 16
                out.append(h.dma_start(
                    dstv[prow:prow + 16, hh * 64:(hh + 1) * 64],
                    vT16[0:16, hh >> 1, (hh & 1) * 64:((hh & 1) + 1) * 64]))
            return out
        e_vexp_ready = em.op("g", scat, inc="d_bs",
                             waits=[("dv", e_vt), ("dv", e_z)], ndma=16)

        for c in range(NCH):
            p2 = c % 2
            for pair in range(8):
                bp = pair % 2

                def qmm(h, pair=pair, bp=bp, p2=p2):
                    ins = None
                    for dt in range(NDT):
                        ins = h.matmul(ps1[:, bp, :],
                                       wa_sb[:, dt, pair * 128:(pair + 1) * 128],
                                       xn_tb[:, p2, dt, :],
                                       start=(dt == 0), stop=(dt == NDT - 1))
                    return ins
                qw = [("dv", st["mul"][c]), ("d_bw", e_wbq)]
                if pair < 2:
                    if c == 0:
                        qw += list(bank1[bp])
                    else:
                        qw.append(("dv", e_p2n[c - 1][bp]))
                else:
                    qw.append(("ac", e_qev[c][pair - 2]))
                e_q = em.op("t", qmm, inc="pe", waits=qw)
                e_qpj[c] = e_q
                evw = [("pe", e_q)]
                if c >= 1:
                    evw.append(("pe", e_lg[c - 1][pair // 4]))
                e_qev[c][pair] = em.op("s", lambda h, pair=pair, bp=bp: h.copy(
                    qT2[:, pair, :], ps1[:, bp, :]), inc="ac", waits=evw)

            e_ex = [0, 0]
            e_mask = [0, 0]
            for t2 in range(2):

                def lmm(h, t2=t2):
                    ins = None
                    for j in range(4):
                        pair = 4 * t2 + j
                        ins = h.matmul(ps1[32 * j:32 * j + 32, t2, :],
                                       k16bd[:, pair, :], qT2[:, pair, :],
                                       start=True, stop=True,
                                       tile_position=(0, 32 * j),
                                       skip_group_check=True)
                    return ins
                lgw = [("ac", e_qev[c][6 + t2]), ("ac", e_k16_ready)]
                e_lg[c][t2] = em.op("t", lmm, inc="pe", waits=lgw)
                exw = [("pe", e_lg[c][t2])]
                if c >= 1:
                    exw.append(("dv", e_p2n[c - 1][t2]))
                e_ex[t2] = em.op("s", lambda h, t2=t2: h.activation(
                    pB[:, t2, :], ps1[:, t2, :], AF.Exp, scale=0.125), inc="ac",
                    waits=exw)
                if c == 0:
                    e_mask[t2] = em.op("v", lambda h, t2=t2: h.tensor_tensor(
                        pB[:, t2, 0:16], pB[:, t2, 0:16], m01_sb[:, :], ALU.mult),
                        inc="dv", waits=[("ac", e_ex[t2]), ("d_c0", e_consts)])

            hw_ = ([("dv", e_mask[0]), ("dv", e_mask[1])] if c == 0 else
                   [("ac", e_ex[0]), ("ac", e_ex[1])])
            if c >= 1:
                hw_.append(("ac", e_aev_s[c - 1]))
                hw_.append(("dv", e_aev_v[c - 1]))
            else:
                hw_ += list(entry)

            def hsum(h):
                h.matmul(psOT[0:16, 0, :], hsA_sb[:, :], pB[:, 0, :],
                         start=True, stop=False, skip_group_check=True)
                return h.matmul(psOT[0:16, 0, :], hsB_sb[:, :], pB[:, 1, :],
                                start=False, stop=True, skip_group_check=True)
            e_hs[c] = em.op("t", hsum, inc="pe", waits=hw_)
            e_rcp[c] = em.op("v", lambda h: h.reciprocal(
                recT[0:16, :], psOT[0:16, 0, :]), inc="dv",
                waits=[("pe", e_hs[c])])
            e_rcb = em.op("v", lambda h: h.tensor_copy(
                recTb[0:16, :], recT[0:16, :]), inc="dv",
                waits=[("dv", e_rcp[c])])

            for t2 in range(2):
                e_brc = em.op("t", lambda h, t2=t2: h.matmul(
                    ps1[:, t2, :], (selA_sb if t2 == 0 else selB_sb)[:, :],
                    recTb[0:16, :], start=True, stop=True,
                    skip_group_check=True), inc="pe",
                    waits=[("dv", e_rcb), ("ac", e_ex[t2])])
                pw = [("pe", e_brc)]
                if c >= 1:
                    pw.append(("pe", e_pv[c - 1][3 + 4 * t2]))
                e_p2n[c][t2] = em.op("v", lambda h, t2=t2: h.tensor_tensor(
                    p2n[:, t2, :], pB[:, t2, :], ps1[:, t2, :], ALU.mult),
                    inc="dv", waits=pw)

            for dt in range(NDT):
                b = dt % 2
                t2 = dt // 4
                vx = vexpA if dt < 4 else vexpB
                pw = [("dv", e_p2n[c][t2]), ("d_bs", e_vexp_ready)]
                if dt == 0:
                    pw.append(("dv", e_rcp[c]))
                elif dt == 1:
                    if c >= 1:
                        pw.append(("dv", e_aev_v[c - 1]))
                elif dt % 2 == 0:
                    pw.append(("ac", e_aev_s[c]))
                else:
                    pw.append(("dv", e_aev_v[c]))
                e_pv[c][dt] = em.op("t", lambda h, dt=dt, b=b, t2=t2, vx=vx: h.matmul(
                    psOT[:, b, :], vx[:, dt * 128:(dt + 1) * 128], p2n[:, t2, :],
                    start=True, stop=True, skip_group_check=True), inc="pe",
                    waits=pw)
                aw = [("pe", e_pv[c][dt])]
                if c >= 1:
                    aw.append(("pe", e_pjl[c - 1]))
                else:
                    aw += list(entry)
                if dt % 2 == 0:
                    e_aev_s[c] = em.op("s", lambda h, dt=dt, b=b: h.copy(
                        aT[:, dt, :], psOT[:, b, :]), inc="ac", waits=aw)
                else:
                    e_aev_v[c] = em.op("v", lambda h, dt=dt, b=b: h.tensor_copy(
                        aT[:, dt, :], psOT[:, b, :]), inc="dv", waits=aw)

            for ot in range(NDT):
                bp = ot % 2

                def pjmm(h, ot=ot, bp=bp):
                    ins = None
                    for ct in range(NDT):
                        ins = h.matmul(ps2[:, bp, :],
                                       wp_sb[:, ct, ot * 128:(ot + 1) * 128],
                                       aT[:, ct, :], start=(ct == 0),
                                       stop=(ct == NDT - 1))
                    return ins
                pjw = [("ac", e_aev_s[c]), ("dv", e_aev_v[c]), ("d_bw", e_wbp)]
                if ot < 2:
                    if c == 0:
                        pjw += list(bank2[bp])
                    else:
                        pjw.append(("dv", e_res[c - 1][6 + bp]))
                else:
                    pjw.append(("dv", e_res[c][ot - 2]))
                e_pj = em.op("t", pjmm, inc="pe", waits=pjw)
                e_pjl[c] = e_pj
                e_res[c][ot] = em.op("v", lambda h, ot=ot, bp=bp, c=c: h.tensor_tensor(
                    xtb[:, c % 3, ot, :], ps2[:, bp, :], xtb[:, c % 3, ot, :],
                    ALU.add), inc="dv", waits=[("pe", e_pj), ("d_bx", e_xin[c])])
                if ot == 1:
                    emit_bcasts(c + 1)
                    emit_submul(c + 1)
                if ot == 3:
                    emit_cast(c + 2)
                if ot == 5:
                    emit_stats(c + 2)
                    emit_rows(c + 2)
            e_xout[c] = em.op("y", lambda h, c=c: h.dma_start(
                xout.rearrange("(a q) t -> q a t", q=128)[:, :, c * TC:(c + 1) * TC],
                xtb[:, c % 3, :, :]), inc="d_bo", waits=[("dv", e_res[c][7])])
            dma_x(c + 3, w0=[("d_bo", e_xout[c])])

        for g in reversed(gB):
            g.__exit__(None, None, None)
        for g in reversed([ps1_g, ps2_g, psOT_g, psBc_g]):
            g.__exit__(None, None, None)
        return em.all_counts()


    entry = [("d_c0", e_consts), ("dv", e_ms)]
    ph = 0
    for be in range(nbe):
        s1 = xr1[be if dbg_scratch else 0]
        s2 = xr2[be if dbg_scratch else 0]
        s3 = xr3[be if dbg_scratch else 0]
        if ph < nphase:
            entry = phase_A(be, entry)
            ph += 1
        if ph < nphase:
            entry = phase_F(be, 0, s1, s2, entry)
            ph += 1
        if ph < nphase:
            entry = phase_B(be, s2, s3, entry)
            ph += 1
        if ph < nphase:
            entry = phase_F(be, 1, s3, outT[be], entry)
            ph += 1

    finals = [(n, v) for n, v in em.cnt.items() if n.startswith("d_") and v > 0]
    def fin(h):
        for n, v in finals:
            h.wait_ge(em.sem[n], v)
    em.fns["y"].append(fin)

    em.finish()
    for g in reversed(guards):
        g.__exit__(None, None, None)
    return nc


def kernel(L0_attn_a_x, L0_attn_a_Wq, L0_attn_a_bq, L0_attn_a_Wk, L0_attn_a_bk,
           L0_attn_a_Wv, L0_attn_a_bv, L0_attn_b_W_attn, L0_attn_b_W_proj,
           L0_attn_b_W_rel, L0_ffn_W1, L0_ffn_b1, L0_ffn_W2, L0_ffn_b2, L0_ffn_scale):
    x = np.asarray(L0_attn_a_x, dtype=np.float32)
    B = x.shape[0]
    n_cores = 8
    xT = np.ascontiguousarray(np.transpose(x, (0, 2, 1)))

    def t32(a):
        wT = np.asarray(a, dtype=np.float32).T
        return np.ascontiguousarray(wT.reshape(NDT, 128, wT.shape[1]).transpose(1, 0, 2))

    w1 = np.asarray(L0_ffn_W1, np.float32)
    w2 = np.asarray(L0_ffn_W2, np.float32)
    common = {
        "wqT": t32(L0_attn_a_Wq), "wkT": t32(L0_attn_a_Wk), "wvT": t32(L0_attn_a_Wv),
        "waT": t32(L0_attn_b_W_attn), "wpT": t32(L0_attn_b_W_proj),
        "w1p": np.ascontiguousarray(
            w1.reshape(NDT, 512, NDT, 128).transpose(3, 0, 2, 1)),
        "w2p": np.ascontiguousarray(
            w2.reshape(NDT, 128, NJT, 128).transpose(3, 0, 2, 1)),
        "bqp": np.ascontiguousarray(np.asarray(L0_attn_a_bq, np.float32).reshape(NDT, 128).T),
        "bkp": np.ascontiguousarray(np.asarray(L0_attn_a_bk, np.float32).reshape(NDT, 128).T),
        "b1p": np.ascontiguousarray(np.asarray(L0_ffn_b1, np.float32).reshape(NJT, 128).T),
        "bv_r": np.asarray(L0_attn_a_bv, np.float32).reshape(1, D),
        "b2_r": np.asarray(L0_ffn_b2, np.float32).reshape(1, D),
        "scal": np.asarray(L0_ffn_scale, np.float32).reshape(1, 1),
    }

    nc = build_nc(nbe=B // n_cores)
    in_maps = []
    for c in range(n_cores):
        m = dict(common)
        m["xt"] = np.ascontiguousarray(xT[c * (B // n_cores):(c + 1) * (B // n_cores)])
        in_maps.append(m)
    res = run_bass_kernel_spmd(nc, in_maps, core_ids=list(range(n_cores)))
    parts = [np.transpose(res.results[c]["outT"], (0, 2, 1)) for c in range(n_cores)]
    return np.ascontiguousarray(np.concatenate(parts, axis=0)).astype(np.float32)
